# version 03
# baseline (speedup 1.0000x reference)
import os
import contextlib
import numpy as np
import ml_dtypes

os.environ.setdefault("TRN_ALLOW_ICI_ALL_TO_ALL", "1")

import concourse.bass as bass
import concourse.mybir as mybir
from concourse.bass_utils import run_bass_kernel_spmd

F32 = mybir.dt.float32
BF16 = mybir.dt.bfloat16
I32 = mybir.dt.int32
AF = mybir.ActivationFunctionType
ALU = mybir.AluOpType

W = 8
S = 8192
D = 4096
HD = 128
HPC = 4
EOWN = HPC * HD
TOK = S // W
INTER = 11008
NSL = INTER // W
NSLP = NSL // 8
NG = 32
NGD = 86
NKT = 11
SC = 16
QC = 16
NCH = HPC * QC
NTT = TOK // 128

EXP_BIAS = -12.0
QK_SCALE = 1.0 / float(np.sqrt(HD))
EPS = 1e-5

C_EXPB, C_QKSC, C_EPS, C_NHALF = range(4)
NCONST = 4
RG = [list(range(W))]

DQT = []
for _mat in range(3):
    _nkt = 32 if _mat < 2 else NKT
    for _kt in range(_nkt):
        _rows = 96 if (_mat == 2 and _kt == NKT - 1) else 128
        DQT.append((_mat, _kt, _rows,
                    NSL if _mat < 2 else D,
                    NG if _mat < 2 else NGD,
                    D if _mat < 2 else NSL))
NDQ = len(DQT)
OHC = [(D + 511) // 512, (D + 511) // 512, (NSL + 511) // 512]
OHOFF = [0, 8, 16]
DQOFF = [0, 32, 64]
DQU = []
UOFF = []
for _ktn, (_mat, _kt, _rows, _nwid, _ngr, _klen) in enumerate(DQT):
    UOFF.append(len(DQU))
    for _chk in range(-(-_nwid // 512)):
        _wid = min(512, _nwid - _chk * 512)
        DQU.append((_ktn, _mat, _kt, _rows, _nwid, _ngr, _klen, _chk, _wid))
NDQU = len(DQU)
UOFF.append(NDQU)
UMAT = [0, UOFF[32], UOFF[64]]


def _ceil(a, b):
    return -(-a // b)


def build_nc(upto=9, dbg=()):
    nc = bass.Bass("TRN2", target_bir_lowering=False, num_devices=W)

    di = lambda n, s, d: nc.dram_tensor(n, s, d, kind="ExternalInput").ap()
    xh = di("xh", [128, 32, S], F32)
    x_own = di("x_own", [128, 32, TOK], F32)
    wqT = di("wqT", [128, 32, EOWN], F32)
    wkT = di("wkT", [128, 32, EOWN], F32)
    wvT = di("wvT", [128, 32, EOWN], F32)
    woT = di("woT", [128, 32, D], F32)
    cos2 = di("cos2", [128, S], F32)
    sin2 = di("sin2", [128, S], F32)
    ident = di("ident", [128, 128], F32)
    ones_row = di("ones_row", [1, 128], F32)
    ones_col_f = di("ones_col_f", [128, 1], F32)
    ones_col_b = di("ones_col_b", [128, 1], BF16)
    iota32 = di("iota32", [32, 1], F32)
    iota86 = di("iota86", [86, 1], F32)
    shifts = di("shifts", [128, 1], I32)
    cvec = di("cvec", [128, NCONST], F32)
    g_qw = di("g_qw", [D, NSL], I32)
    u_qw = di("u_qw", [D, NSL], I32)
    d_qw = di("d_qw", [NSL, D], I32)
    g_sc = di("g_sc", [NG, NSL], F32)
    u_sc = di("u_sc", [NG, NSL], F32)
    d_sc = di("d_sc", [NGD, D], F32)
    g_qz = di("g_qz", [NG, NSLP], I32)
    u_qz = di("u_qz", [NG, NSLP], I32)
    d_qz = di("d_qz", [NGD, D // 8], I32)
    g_gi = di("g_gi", [1, D], I32)
    u_gi = di("u_gi", [1, D], I32)
    d_gi = di("d_gi", [1, NSL], I32)
    y_out = nc.dram_tensor("y", [TOK, D], F32, kind="ExternalOutput").ap()

    dt = lambda n, s, d: nc.dram_tensor(
        n, s, d, kind=("ExternalOutput" if n in dbg else "Internal")).ap()
    q_d = dt("q_d", [HPC, 128, S], BF16)
    k_d = dt("k_d", [HPC, 128, S], BF16)
    v_d = dt("v_d", [HPC, 128, S // 128, HD], BF16)
    a2a_in = dt("a2a_in", [W, HPC, 128, TOK], BF16)
    a2a_out = dt("a2a_out", [W, HPC, 128, TOK], BF16)
    s1_d = dt("s1_d", [32, 128, TOK], F32)
    agx_in = dt("agx_in", [32, 128, TOK], BF16)
    agx_out = [nc.dram_tensor(f"agx_out{i}", [W, 8, 128, TOK], BF16,
                              kind="Internal", addr_space="Shared").ap()
               for i in range(4)]
    wg_d = dt("wg_d", [32, 128, NSL], BF16)
    wu_d = dt("wu_d", [32, 128, NSL], BF16)
    wd_d = dt("wd_d", [NKT, 128, D], BF16)
    hg_d = dt("hg_d", [NKT, 128, S], BF16)
    h_d = dt("h_d", [NKT, 128, S], BF16)
    rs_in = [dt(f"rs_in{j}", [W, 1024, TOK], F32) for j in range(4)]
    rs_out = [dt(f"rs_out{j}", [1024, TOK], F32) for j in range(4)]
    if "rs_dbg" in dbg:
        rsi_dbg = nc.dram_tensor("rsi_dbg", [W, 1024, TOK], F32, kind="ExternalOutput").ap()
        rso_dbg = nc.dram_tensor("rso_dbg", [1024, TOK], F32, kind="ExternalOutput").ap()
    WMAT = [wg_d, wu_d, wd_d]

    es = contextlib.ExitStack()
    mk = lambda n: es.enter_context(nc.semaphore(n))
    mk2 = lambda n: [mk(n + "0"), mk(n + "1")]
    (dPre, tQ, tK, vRope, tV, sEv, tS, sExp, vZ, tPv,
     vZf, tZ, vRec, tZb, sZb, vO, cA2A, dCtxIn, tY, vS1,
     tSt, vSt, sRstd, tBc, vBc, vXf, cAGX, dDq, dSc, tOh, vOh,
     vOh2, tG, vDq, dWres, tH, sSilu, tH2, vH2,
     tY3, vE3, cRS, vF, tTr, sTe, vLn, sLn, vLn2) = [mk(f"s{i}") for i in range(48)]
    dX = mk2("dX")
    dXv = mk2("dXv")
    dCS = mk2("dCS")
    dSpQ = mk2("dSpQ")
    dSpK = mk2("dSpK")
    dSpV = mk2("dSpV")
    dKV = mk2("dKV")
    dCtx = mk2("dCtx")
    dWo = mk2("dWo")
    dXo = mk2("dXo")
    dS1 = mk2("dS1")
    dS1r = mk2("dS1r")
    dAgx = mk2("dAgx")
    dQw = mk2("dQw")
    dWbf = mk2("dWbf")
    dXf = mk2("dXf")
    dHg = mk2("dHg")
    dHgr = mk2("dHgr")
    dH2 = mk2("dH2")
    dH3 = mk2("dH3")
    dRs4 = [mk(f"dRs{j}") for j in range(4)]
    dF1 = mk2("dF1")
    dYs = mk2("dYs")
    dOut = mk2("dOut")
    vCh = mk("vCh")
    sCh = mk("sCh")

    AL = nc.alloc_sbuf_tensor
    ident_sb = AL("ident_sb", [128, 128], F32).ap()
    ones_row_sb = AL("ones_row_sb", [1, 128], F32).ap()
    ones_col_f_sb = AL("ones_col_f_sb", [128, 1], F32).ap()
    ones_col_b_sb = AL("ones_col_b_sb", [128, 1], BF16).ap()
    iota32_sb = AL("iota32_sb", [32, 1], F32).ap()
    iota86_sb = AL("iota86_sb", [86, 1], F32).ap()
    shifts_sb = AL("shifts_sb", [128, 1], I32).ap()
    cv = AL("cv_sb", [128, NCONST], F32).ap()
    mub_sb = AL("mub_sb", [128, TOK], F32).ap()
    rstdb_sb = AL("rstdb_sb", [128, TOK], F32).ap()
    red_sb = AL("red_sb", [128, 6], F32).ap()

    ABASE = nc.sbuf_base
    ATOP = nc.sbuf_top

    class Ph:
        def __init__(self):
            self.off = ABASE

        def t(self, name, shape, dtype):
            per = int(np.prod(shape[1:])) * np.dtype(mybir.dt.np(dtype)).itemsize
            off = (self.off + 31) & ~31
            h = nc.alloc_sbuf_tensor_at(name, shape, dtype, offset=off)
            self.off = off + per
            assert self.off <= ATOP, f"{name}: arena overflow {self.off} > {ATOP}"
            return h.ap()

    ph = Ph()
    wqT_sb = ph.t("wqT_sb", [128, 32, EOWN], BF16)
    wkT_sb = ph.t("wkT_sb", [128, 32, EOWN], BF16)
    xt_sb = [ph.t(f"xt{i}", [128, 32, 512], BF16) for i in range(2)]
    cos_sb = [ph.t(f"cos{i}", [128, 512], F32) for i in range(2)]
    sin_sb = [ph.t(f"sin{i}", [128, 512], F32) for i in range(2)]
    uv_sb = ph.t("uv_sb", [128, 1024], F32)
    qst_sb = [ph.t(f"qst{i}", [128, HPC, 512], BF16) for i in range(2)]
    kst_sb = [ph.t(f"kst{i}", [128, HPC, 512], BF16) for i in range(2)]
    wvT_sb = ph.t("wvT_sb", [128, 32, EOWN], BF16)
    vst_sb = [ph.t(f"vst{i}", [128, 4 * EOWN], BF16) for i in range(2)]

    pa = Ph()
    k_sb = [pa.t(f"k_sb{i}", [128, S], BF16) for i in range(2)]
    v_sb = [pa.t(f"v_sb{i}", [128, S // 128, HD], BF16) for i in range(2)]
    q_sb = [pa.t(f"q_sb{i}", [128, S], BF16) for i in range(2)]
    pring = pa.t("pring", [128, 4, 1024], BF16)
    accb = pa.t("accb", [128, 1024], BF16)
    acc32 = pa.t("acc32", [128, 1024], F32)
    accf = pa.t("accf", [128, 512], F32)
    recip_sb = pa.t("recip_sb", [1, 512], F32)
    zb_sb = pa.t("zb_sb", [128, 512], F32)
    ctxst = [pa.t(f"ctxst{i}", [128, 512], BF16) for i in range(2)]

    po = Ph()
    ctx_sb = po.t("ctx_sb", [128, 32, TOK], BF16)
    woT_sb = [po.t(f"woT{i}", [128, 32, 128], BF16) for i in range(2)]
    xo_sb = [po.t(f"xo{i}", [128, TOK], F32) for i in range(2)]
    s1_st = [po.t(f"s1st{i}", [128, TOK], F32) for i in range(2)]
    s1b_sb = po.t("s1b_sb", [128, TOK], BF16)
    sqb_sb = po.t("sqb_sb", [128, TOK], BF16)
    mu_sb = po.t("mu_sb", [1, TOK], F32)
    msq_sb = po.t("msq_sb", [1, TOK], F32)
    var_sb = po.t("var_sb", [1, TOK], F32)
    var2_sb = po.t("var2_sb", [1, TOK], F32)
    rstd_sb = po.t("rstd_sb", [1, TOK], F32)
    s1r_sb = [po.t(f"s1r{i}", [128, TOK], F32) for i in range(2)]
    xfb_st = [po.t(f"xfb{i}", [128, TOK], BF16) for i in range(2)]

    pd = Ph()
    gi_i = pd.t("gi_i", [1, D], I32)
    gi_f = pd.t("gi_f", [1, D], F32)
    oh_sb = pd.t("oh_sb", [86, D], BF16)
    z_sb = pd.t("z_sb", [86, D], I32)
    zp1_sb = pd.t("zp1_sb", [86, D], BF16)
    sc_sb = pd.t("sc_sb", [86, D], BF16)
    qz_sb = pd.t("qz_sb", [86, 512], I32)
    qw_sb = [pd.t(f"qw{i}", [128, D], I32) for i in range(2)]
    w4f_sb = pd.t("w4f_sb", [128, D], F32)
    tdq_sb = pd.t("tdq_sb", [128, 512], F32)
    wst_sb = [pd.t(f"wst{i}", [128, D], BF16) for i in range(2)]

    pf = Ph()
    wres_raw = pf.t("wres_sb", [128, NKT * D], BF16)
    wres_gu = wres_raw[:, 0:32 * NSL].rearrange("p (k n) -> p k n", n=NSL)
    wres_d = wres_raw.rearrange("p (k n) -> p k n", n=D)
    xf_off = pf.off
    xf_sb = [pf.t(f"xf{i}", [128, 32, 512], BF16) for i in range(2)]
    hst_sb = [pf.t(f"hst{i}", [128, NKT, 512], BF16) for i in range(2)]
    hgr_sb = [pf.t(f"hgr{i}", [128, NKT, 512], BF16) for i in range(2)]
    yst_sb = [nc.alloc_sbuf_tensor_at(
        f"yst{i}", [128, 8, 512], F32,
        offset=((xf_off + 31) & ~31) + i * 8 * 512 * 4).ap() for i in range(2)]

    pz = Ph()
    s1f_sb = [pz.t(f"s1f{i}", [128, 32, 128], F32) for i in range(2)]
    ysl_sb = [pz.t(f"ysl{i}", [128, 32, 128], F32) for i in range(2)]
    s2_sb = pz.t("s2_sb", [128, 32, 128], F32)
    tmaj_sb = [pz.t(f"tmaj{i}", [128, D], F32) for i in range(2)]
    t3_sb = pz.t("t3_sb", [128, D], F32)
    out_st = [pz.t(f"outst{i}", [128, D], F32) for i in range(2)]

    ps_a = nc.alloc_psum_tensor("ps_a", [128, 2048], F32).ap()
    ps_b = nc.alloc_psum_tensor("ps_b", [128, 2048], F32).ap()
    ps_s = [ps_a[:, 0:1024], ps_a[:, 1024:2048], ps_b[:, 0:1024]]
    ps_o = ps_b[:, 1024:1536]
    ps_zzb = ps_b[:, 1536:2048]
    ps_q = ps_a
    ps_k = ps_b
    ps_y = [ps_a[:, 0:1024], ps_a[:, 1024:2048]]
    ps_mu = ps_b[0:1, 0:1024]
    ps_sq = ps_b[0:1, 1024:2048]
    ps_bc = ps_a
    ps_dq = [[ps_a[:, 0:512], ps_a[:, 512:1024]],
             [ps_a[:, 1024:1536], ps_a[:, 1536:2048]]]
    ps_ohb = ps_b[0:86, 0:512]
    ps_h = [ps_a[:, 0:512], ps_a[:, 512:1024]]
    ps_y3 = [ps_a[:, 1024:1536], ps_a[:, 1536:2048]]
    ps_tr = [ps_b[:, 0:128], ps_b[:, 512:640]]

    NPRE = 10
    vch = [0]
    sch = [0]

    with es:
        with nc.Block() as block:

            @block.sync
            def _(sync):
                sync.dma_start(out=ident_sb, in_=ident).then_inc(dPre, 16)
                sync.dma_start(out=ones_row_sb, in_=ones_row).then_inc(dPre, 16)
                sync.dma_start(out=ones_col_f_sb, in_=ones_col_f).then_inc(dPre, 16)
                sync.dma_start(out=ones_col_b_sb, in_=ones_col_b).then_inc(dPre, 16)
                sync.dma_start(out=iota32_sb, in_=iota32).then_inc(dPre, 16)
                sync.dma_start(out=iota86_sb, in_=iota86).then_inc(dPre, 16)
                sync.dma_start(out=shifts_sb, in_=shifts).then_inc(dPre, 16)
                sync.dma_start(out=cv, in_=cvec).then_inc(dPre, 16)

                for c in range(SC if upto >= 1 else 0):
                    sl = slice(c * 512, (c + 1) * 512)
                    if c >= 2:
                        sync.wait_ge(vRope, 2 * (c - 1))
                    sync.dma_start(out=cos_sb[c % 2], in_=cos2[:, sl]).then_inc(dCS[c % 2], 16)
                    sync.dma_start(out=sin_sb[c % 2], in_=sin2[:, sl]).then_inc(dCS[c % 2], 16)
                    sync.wait_ge(vRope, 2 * c + 1)
                    sync.dma_start(out=q_d[:, :, sl].rearrange("h p s -> p h s"),
                                   in_=qst_sb[c % 2]).then_inc(dSpQ[c % 2], 16)
                    sync.wait_ge(vRope, 2 * c + 2)
                    sync.dma_start(out=k_d[:, :, sl].rearrange("h p s -> p h s"),
                                   in_=kst_sb[c % 2]).then_inc(dSpK[c % 2], 16)
                for c in range(SC if upto >= 2 else 0):
                    sl = slice(c * 512, (c + 1) * 512)
                    sync.wait_ge(sEv, c + 1)
                    vv = vst_sb[c % 2].rearrange("p (t h d) -> p t h d", t=4, h=HPC)
                    for hh in range(HPC):
                        sync.dma_start(
                            out=v_d[hh, :, 4 * c:4 * c + 4, :].rearrange("p t d -> p t d"),
                            in_=vv[:, :, hh, :]).then_inc(dSpV[c % 2], 16)

                for h in range(HPC if upto >= 3 else 0):
                    if h == 0:
                        for sm_ in (dSpQ[0], dSpQ[1], dSpK[0], dSpK[1]):
                            sync.wait_ge(sm_, 16 * 8)
                        sync.wait_ge(dSpV[0], 64 * 8)
                        sync.wait_ge(dSpV[1], 64 * 8)
                    elif h >= 2:
                        sync.wait_ge(tPv, 32 * QC * (h - 1))
                    sync.dma_start(out=k_sb[h % 2], in_=k_d[h]).then_inc(dKV[h % 2], 16)
                    sync.dma_start(out=v_sb[h % 2], in_=v_d[h]).then_inc(dKV[h % 2], 16)
                    sync.dma_start(out=q_sb[h % 2], in_=q_d[h]).then_inc(dKV[h % 2], 16)

                if upto < 4:
                    return
                sync.wait_ge(cA2A, 1)
                sync.dma_start(out=ctx_sb,
                               in_=a2a_out.rearrange("r h p t -> p (r h) t")
                               ).then_inc(dCtxIn, 16)
                for ec in range(32):
                    if ec >= 2:
                        sync.wait_ge(vS1, ec - 1)
                    sync.dma_start(out=xo_sb[ec % 2], in_=x_own[:, ec, :]).then_inc(dXo[ec % 2], 16)
                    sync.wait_ge(vS1, ec + 1)
                    sync.dma_start(out=s1_d[ec], in_=s1_st[ec % 2]).then_inc(dS1[ec % 2], 16)
                for ec in range(32):
                    if ec >= 2:
                        sync.wait_ge(vXf, ec - 1)
                    sync.dma_start(out=s1r_sb[ec % 2], in_=s1_d[ec]).then_inc(dS1r[ec % 2], 16)
                    sync.wait_ge(vXf, ec + 1)
                    sync.dma_start(out=agx_in[ec], in_=xfb_st[ec % 2]).then_inc(dAgx[ec % 2], 16)

                if upto < 6:
                    return
                sync.wait_ge(vXf, 32)
                sync.dma_start(out=qz_sb[0:NG, 0:NSLP], in_=g_qz).then_inc(dDq, 16)
                sync.dma_start(out=gi_i, in_=g_gi).then_inc(dDq, 16)
                for ktn, (mat, kt, rows, nwid, ngr, klen) in enumerate(DQT):
                    if ktn >= 2:
                        sync.wait_ge(vDq, UOFF[ktn - 1])
                    sync.dma_start(
                        out=qw_sb[ktn % 2][0:rows, 0:nwid],
                        in_=(g_qw, u_qw, d_qw)[mat][kt * 128: kt * 128 + rows, :]
                    ).then_inc(dQw[ktn % 2], 16)
                    if ktn >= 1:
                        pm, pk, prow, pnw, _, _ = DQT[ktn - 1]
                        sync.wait_ge(vDq, UOFF[ktn])
                        sync.dma_start(
                            out=WMAT[pm][pk][0:prow, :],
                            in_=wst_sb[(ktn - 1) % 2][0:prow, 0:pnw]).then_inc(dWbf[(ktn - 1) % 2], 16)
                    if mat < 2 and kt == 31:
                        sync.wait_ge(vOh, mat + 1)
                        sync.wait_ge(tOh, OHOFF[mat + 1])
                        nxt_qz, nxt_gi = (u_qz, u_gi) if mat == 0 else (d_qz, d_gi)
                        sync.dma_start(out=qz_sb[0:(NG if mat == 0 else NGD),
                                                 0:(NSLP if mat == 0 else D // 8)],
                                       in_=nxt_qz).then_inc(dDq, 16)
                        sync.dma_start(out=gi_i[:, 0:(D if mat == 0 else NSL)],
                                       in_=nxt_gi).then_inc(dDq, 16)
                pm, pk, prow, pnw, _, _ = DQT[NDQ - 1]
                sync.wait_ge(vDq, NDQU)
                sync.dma_start(out=WMAT[pm][pk][0:prow, :],
                               in_=wst_sb[(NDQ - 1) % 2][0:prow, 0:pnw]).then_inc(dWbf[(NDQ - 1) % 2], 16)

                if upto < 7:
                    return
                sync.wait_ge(dWbf[0], 16 * 38)
                sync.wait_ge(dWbf[1], 16 * 37)
                sync.dma_start(out=wres_gu,
                               in_=wg_d.rearrange("k p n -> p k n")).then_inc(dWres, 16)
                for tc in range(16):
                    r, hf = tc // 2, tc % 2
                    if tc == 0:
                        sync.wait_ge(cAGX, 4)
                    if tc >= 2:
                        sync.wait_ge(tH, NKT * (tc - 1))
                    for i in range(4):
                        sync.dma_start(
                            out=xf_sb[tc % 2][:, 8 * i:8 * (i + 1), :],
                            in_=agx_out[i][r, :, :, hf * 512:(hf + 1) * 512].rearrange(
                                "m p t -> p m t")).then_inc(dXf[tc % 2], 16)
                    sync.wait_ge(sSilu, NKT * (tc + 1))
                    sync.dma_start(
                        out=hg_d[:, :, tc * 512:(tc + 1) * 512].rearrange("n p t -> p n t"),
                        in_=hst_sb[tc % 2]).then_inc(dHg[tc % 2], 16)
                sync.wait_ge(tH, NKT * 16)
                sync.dma_start(out=wres_gu,
                               in_=wu_d.rearrange("k p n -> p k n")).then_inc(dWres, 16)
                for tc in range(16):
                    r, hf = tc // 2, tc % 2
                    if tc >= 2:
                        sync.wait_ge(tH2, NKT * (tc - 1))
                        sync.wait_ge(dH2[tc % 2], 16 * (tc // 2))
                    for i in range(4):
                        sync.dma_start(
                            out=xf_sb[tc % 2][:, 8 * i:8 * (i + 1), :],
                            in_=agx_out[i][r, :, :, hf * 512:(hf + 1) * 512].rearrange(
                                "m p t -> p m t")).then_inc(dXf[tc % 2], 16)
                    sync.dma_start(
                        out=hgr_sb[tc % 2],
                        in_=hg_d[:, :, tc * 512:(tc + 1) * 512].rearrange(
                            "n p t -> p n t")).then_inc(dHgr[tc % 2], 16)
                    sync.wait_ge(vH2, NKT * (tc + 1))
                    sync.dma_start(
                        out=h_d[:, :, tc * 512:(tc + 1) * 512].rearrange("n p t -> p n t"),
                        in_=hgr_sb[tc % 2]).then_inc(dH2[tc % 2], 16)
                if upto < 8:
                    return
                sync.wait_ge(tH2, NKT * 16)
                sync.wait_ge(dH2[0], 16 * 8)
                sync.wait_ge(dH2[1], 16 * 8)
                sync.dma_start(out=wres_d,
                               in_=wd_d.rearrange("k p n -> p k n")).then_inc(dWres, 16)
                i3 = 0
                for j in range(4):
                    for tc in range(16):
                        r, hf = tc // 2, tc % 2
                        if i3 >= 2:
                            sync.wait_ge(tY3, 8 * (i3 - 1))
                        sync.dma_start(
                            out=hgr_sb[i3 % 2],
                            in_=h_d[:, :, tc * 512:(tc + 1) * 512].rearrange(
                                "n p t -> p n t")).then_inc(dH3[i3 % 2], 16)
                        sync.wait_ge(vE3, 8 * (i3 + 1))
                        sync.dma_start(
                            out=rs_in[j][r, :, hf * 512:(hf + 1) * 512].rearrange(
                                "(ee p) t -> p ee t", p=128),
                            in_=yst_sb[i3 % 2]).then_inc(dRs4[j], 16)
                        i3 += 1

                if "rs_dbg" in dbg:
                    sync.wait_ge(dRs4[0], 16 * 16)
                    sync.dma_start(out=rsi_dbg, in_=rs_in[0]).then_inc(dPre, 16)
                    sync.wait_ge(cRS, 1)
                    sync.dma_start(out=rso_dbg, in_=rs_out[0]).then_inc(dPre, 16)
                if upto < 9:
                    return
                for tt in range(NTT):
                    tsl = slice(tt * 128, (tt + 1) * 128)
                    if tt == 0:
                        sync.wait_ge(cRS, 4)
                    if tt >= 2:
                        sync.wait_ge(vF, tt - 1)
                    sync.dma_start(
                        out=s1f_sb[tt % 2],
                        in_=s1_d[:, :, tsl].rearrange("e p t -> p e t")).then_inc(dF1[tt % 2], 16)
                    for j in range(4):
                        sync.dma_start(
                            out=ysl_sb[tt % 2][:, j * 8:(j + 1) * 8, :],
                            in_=rs_out[j][:, tsl].rearrange(
                                "(ee p) t -> p ee t", p=128)).then_inc(dYs[tt % 2], 16)

            @block.gpsimd
            def _(gpsimd):
                gpsimd.dma_start(out=wqT_sb, in_=wqT).then_inc(dPre, 16)
                gpsimd.dma_start(out=wkT_sb, in_=wkT).then_inc(dPre, 16)
                for c in range(SC if upto >= 1 else 0):
                    if c >= 2:
                        gpsimd.wait_ge(tK, c - 1)
                    gpsimd.dma_start(out=xt_sb[c % 2],
                                     in_=xh[:, :, c * 512:(c + 1) * 512]).then_inc(dX[c % 2], 16)
                if upto < 2:
                    return
                gpsimd.wait_ge(tK, SC)
                gpsimd.wait_ge(vRope, 2 * SC)
                for sm_ in (dSpQ[0], dSpQ[1], dSpK[0], dSpK[1]):
                    gpsimd.wait_ge(sm_, 16 * 8)
                gpsimd.dma_start(out=wvT_sb, in_=wvT).then_inc(dPre, 16)
                for c in range(SC):
                    if c >= 2:
                        gpsimd.wait_ge(tV, c - 1)
                    gpsimd.dma_start(out=xt_sb[c % 2],
                                     in_=xh[:, :, c * 512:(c + 1) * 512]).then_inc(dXv[c % 2], 16)
                if upto < 3:
                    return
                for ci in range(NCH):
                    h, qc = divmod(ci, QC)
                    gpsimd.wait_ge(vO, ci + 1)
                    gpsimd.dma_start(
                        out=a2a_in[qc // 2, h, :, (qc % 2) * 512:((qc % 2) + 1) * 512],
                        in_=ctxst[ci % 2]).then_inc(dCtx[ci % 2], 16)
                if upto < 4:
                    return
                gpsimd.wait_ge(dCtx[0], 16 * NCH // 2)
                gpsimd.wait_ge(dCtx[1], 16 * NCH // 2)
                gpsimd.collective_compute(
                    "AllToAll", ALU.bypass, replica_groups=RG,
                    ins=[a2a_in.opt()], outs=[a2a_out.opt()]).then_inc(cA2A, 1)
                for ec in range(32):
                    if ec >= 2:
                        gpsimd.wait_ge(tY, ec - 1)
                    gpsimd.dma_start(out=woT_sb[ec % 2],
                                     in_=woT[:, :, ec * 128:(ec + 1) * 128]).then_inc(dWo[ec % 2], 16)
                if upto < 5:
                    return
                for i in range(4):
                    gpsimd.wait_ge(dAgx[0], 16 * 4 * (i + 1))
                    gpsimd.wait_ge(dAgx[1], 16 * 4 * (i + 1))
                    gpsimd.collective_compute(
                        "AllGather", ALU.bypass, replica_groups=RG,
                        ins=[agx_in[8 * i:8 * (i + 1)].opt()],
                        outs=[agx_out[i].opt()]).then_inc(cAGX, 1)
                if upto < 6:
                    return
                gpsimd.dma_start(out=sc_sb[0:NG, 0:NSL], in_=g_sc).then_inc(dSc, 16)
                gpsimd.wait_ge(tG, UMAT[1])
                gpsimd.dma_start(out=sc_sb[0:NG, 0:NSL], in_=u_sc).then_inc(dSc, 16)
                gpsimd.wait_ge(tG, UMAT[2])
                gpsimd.dma_start(out=sc_sb[0:NGD, 0:D], in_=d_sc).then_inc(dSc, 16)
                if upto < 8:
                    return
                for j in range(4):
                    gpsimd.wait_ge(dRs4[j], 16 * 16)
                    gpsimd.collective_compute(
                        "ReduceScatter", ALU.add, replica_groups=RG,
                        ins=[rs_in[j].opt()], outs=[rs_out[j].opt()]).then_inc(cRS, 1)

            @block.tensor
            def _(tensor):
                tensor.wait_ge(dPre, 16 * NPRE)
                for c in range(SC if upto >= 1 else 0):
                    tensor.wait_ge(dX[c % 2], 16 * (c // 2 + 1))
                    if c >= 1:
                        tensor.wait_ge(vRope, 2 * c - 1)
                    for et in range(HPC):
                        for m in range(32):
                            last = tensor.matmul(ps_q[:, et * 512:(et + 1) * 512],
                                          wqT_sb[:, m, et * 128:(et + 1) * 128],
                                          xt_sb[c % 2][:, m, :],
                                          start=(m == 0), stop=(m == 31))
                    last.then_inc(tQ, 1)
                    if c >= 1:
                        tensor.wait_ge(vRope, 2 * c)
                    for et in range(HPC):
                        for m in range(32):
                            last = tensor.matmul(ps_k[:, et * 512:(et + 1) * 512],
                                          wkT_sb[:, m, et * 128:(et + 1) * 128],
                                          xt_sb[c % 2][:, m, :],
                                          start=(m == 0), stop=(m == 31))
                    last.then_inc(tK, 1)
                if upto < 2:
                    return
                tensor.wait_ge(vRope, 2 * SC)
                for c in range(SC):
                    tensor.wait_ge(dXv[c % 2], 16 * (c // 2 + 1))
                    tensor.wait_ge(dPre, 16 * (NPRE + 1))
                    if c >= 1:
                        tensor.wait_ge(sEv, c)
                    for st in range(4):
                        for m in range(32):
                            last = tensor.matmul(ps_q[:, st * 512:(st + 1) * 512],
                                          xt_sb[c % 2][:, m, st * 128:(st + 1) * 128],
                                          wvT_sb[:, m, :],
                                          start=(m == 0), stop=(m == 31))
                    last.then_inc(tV, 1)

                if upto < 3:
                    return
                for h in range(HPC):
                    tensor.wait_ge(dKV[h % 2], 16 * 3 * (h // 2 + 1))
                    for qc in range(QC):
                        ci = h * QC + qc
                        qsl = slice(qc * 512, (qc + 1) * 512)
                        for g in range(32):
                            gg = 32 * ci + g
                            tensor.wait_ge(sExp, max(0, gg - 2))
                            if ci == 0 and g == 0:
                                tensor.wait_ge(sEv, SC)
                            for jj in range(2):
                                kt = 2 * g + jj
                                last = tensor.matmul(ps_s[g % 3][:, jj * 512:(jj + 1) * 512],
                                              k_sb[h % 2][:, kt * 128:(kt + 1) * 128],
                                              q_sb[h % 2][:, qsl],
                                              start=True, stop=True)
                            last.then_inc(tS, 1)
                            if g >= 1:
                                tensor.wait_ge(sExp, gg)
                                if g == 1:
                                    tensor.wait_ge(vO, ci)
                                    tensor.wait_ge(sZb, ci)
                                for jj in range(2):
                                    kt = 2 * (g - 1) + jj
                                    last = tensor.matmul(ps_o, v_sb[h % 2][:, kt, :],
                                                  pring[:, (g - 1) % 4,
                                                        jj * 512:(jj + 1) * 512],
                                                  start=(kt == 0), stop=False)
                                last.then_inc(tPv, 1)
                        gg = 32 * ci + 31
                        tensor.wait_ge(sExp, gg + 1)
                        for jj in range(2):
                            kt = 62 + jj
                            last = tensor.matmul(ps_o, v_sb[h % 2][:, kt, :],
                                          pring[:, 3, jj * 512:(jj + 1) * 512],
                                          start=False, stop=(kt == 63))
                        last.then_inc(tPv, 1)
                        tensor.wait_ge(vZf, ci + 1)
                        tensor.matmul(ps_zzb[0:1, :], ones_col_f_sb, accf,
                                      start=True, stop=True).then_inc(tZ, 1)
                        tensor.wait_ge(vRec, ci + 1)
                        tensor.matmul(ps_zzb, ones_row_sb, recip_sb,
                                      start=True, stop=True).then_inc(tZb, 1)

                if upto < 4:
                    return
                tensor.wait_ge(dCtxIn, 16)
                for ec in range(32):
                    tensor.wait_ge(dWo[ec % 2], 16 * (ec // 2 + 1))
                    if ec < 2:
                        tensor.wait_ge(sZb, NCH)
                        tensor.wait_ge(vO, NCH)
                    else:
                        tensor.wait_ge(vS1, ec - 1)
                    for ts in range(2):
                        for m in range(32):
                            last = tensor.matmul(ps_y[ec % 2][:, ts * 512:(ts + 1) * 512],
                                          woT_sb[ec % 2][:, m, :],
                                          ctx_sb[:, m, ts * 512:(ts + 1) * 512],
                                          start=(m == 0), stop=(m == 31))
                    last.then_inc(tY, 1)
                    tensor.wait_ge(vS1, ec + 1)
                    for ts in range(2):
                        tensor.matmul(ps_mu[:, ts * 512:(ts + 1) * 512], ones_col_b_sb,
                                      s1b_sb[:, ts * 512:(ts + 1) * 512],
                                      start=(ec == 0), stop=(ec == 31))
                        last = tensor.matmul(ps_sq[:, ts * 512:(ts + 1) * 512], ones_col_b_sb,
                                      sqb_sb[:, ts * 512:(ts + 1) * 512],
                                      start=(ec == 0), stop=(ec == 31))
                    if ec == 31:
                        last.then_inc(tSt, 1)
                tensor.wait_ge(sRstd, 1)
                for ts in range(2):
                    tensor.matmul(ps_bc[:, ts * 512:(ts + 1) * 512], ones_row_sb,
                                  mu_sb[:, ts * 512:(ts + 1) * 512], start=True, stop=True)
                    last = tensor.matmul(ps_bc[:, 1024 + ts * 512:1024 + (ts + 1) * 512],
                                  ones_row_sb, rstd_sb[:, ts * 512:(ts + 1) * 512],
                                  start=True, stop=True)
                last.then_inc(tBc, 1)

                if upto < 6:
                    return
                tensor.wait_ge(vBc, 1)
                for (ktn, mat, kt, rows, nwid, ngr, klen, chk, wid) in DQU:
                    if kt == 0 and chk == 0:
                        tensor.wait_ge(vOh, mat + 1)
                        for ochk in range(OHC[mat]):
                            owid = min(512, klen - ochk * 512)
                            if OHOFF[mat] + ochk >= 1:
                                tensor.wait_ge(vOh2, OHOFF[mat] + ochk)
                            tensor.matmul(ps_ohb[0:ngr, 0:owid], ones_row_sb[:, 0:ngr],
                                          gi_f[:, ochk * 512:ochk * 512 + owid],
                                          start=True, stop=True).then_inc(tOh, 1)
                        tensor.wait_ge(vOh2, OHOFF[mat] + OHC[mat])
                        tensor.wait_ge(dSc, 16 * (mat + 1))
                    u = UOFF[ktn] + chk
                    if u >= 2:
                        tensor.wait_ge(vDq, u - 1)
                    pss = ps_dq[u % 2]
                    tensor.matmul(pss[0][0:rows, 0:wid],
                                  oh_sb[0:ngr, kt * 128:kt * 128 + rows],
                                  sc_sb[0:ngr, chk * 512:chk * 512 + wid],
                                  start=True, stop=True)
                    tensor.matmul(pss[1][0:rows, 0:wid],
                                  oh_sb[0:ngr, kt * 128:kt * 128 + rows],
                                  zp1_sb[0:ngr, chk * 512:chk * 512 + wid],
                                  start=True, stop=True).then_inc(tG, 1)

                if upto < 7:
                    return
                tensor.wait_ge(vDq, NDQU)
                tensor.wait_ge(dWres, 16)
                for tc in range(16):
                    tensor.wait_ge(dXf[tc % 2], 64 * (tc // 2 + 1))
                    for nt in range(NKT):
                        rows = 128 if nt < 10 else 96
                        it = NKT * tc + nt
                        if it >= 2:
                            tensor.wait_ge(sSilu, it - 1)
                        for kt in range(32):
                            last = tensor.matmul(ps_h[it % 2][0:rows, :],
                                          wres_gu[:, kt, nt * 128:nt * 128 + rows],
                                          xf_sb[tc % 2][:, kt, :],
                                          start=(kt == 0), stop=(kt == 31))
                        last.then_inc(tH, 1)
                tensor.wait_ge(dWres, 32)
                tensor.wait_ge(sSilu, NKT * 16)
                for tc in range(16):
                    tensor.wait_ge(dXf[tc % 2], 64 * (8 + tc // 2 + 1))
                    for nt in range(NKT):
                        rows = 128 if nt < 10 else 96
                        it = NKT * tc + nt
                        if it >= 2:
                            tensor.wait_ge(vH2, it - 1)
                        for kt in range(32):
                            last = tensor.matmul(ps_h[it % 2][0:rows, :],
                                          wres_gu[:, kt, nt * 128:nt * 128 + rows],
                                          xf_sb[tc % 2][:, kt, :],
                                          start=(kt == 0), stop=(kt == 31))
                        last.then_inc(tH2, 1)
                if upto < 8:
                    return
                tensor.wait_ge(dWres, 48)
                tensor.wait_ge(vH2, NKT * 16)
                i3 = 0
                for j in range(4):
                    for tc in range(16):
                        tensor.wait_ge(dH3[i3 % 2], 16 * (i3 // 2 + 1))
                        for ecc in range(8):
                            ie = 8 * i3 + ecc
                            if ie >= 2:
                                tensor.wait_ge(vE3, ie - 1)
                            esl = slice((j * 8 + ecc) * 128, (j * 8 + ecc + 1) * 128)
                            for nt in range(NKT):
                                rows = 128 if nt < 10 else 96
                                last = tensor.matmul(ps_y3[ecc % 2],
                                              wres_d[0:rows, nt, esl],
                                              hgr_sb[i3 % 2][0:rows, nt, :],
                                              start=(nt == 0), stop=(nt == NKT - 1))
                            last.then_inc(tY3, 1)
                        i3 += 1

                if upto < 9:
                    return
                for tt in range(NTT):
                    tensor.wait_ge(vF, tt + 1)
                    for ec in range(32):
                        ie = 32 * tt + ec
                        if ie >= 2:
                            tensor.wait_ge(sTe, ie - 1)
                        tensor.transpose(ps_tr[ec % 2], s2_sb[:, ec, :],
                                         ident_sb).then_inc(tTr, 1)

            @block.vector
            def _(vector):
                vector.wait_ge(dPre, 16 * NPRE)
                for c in range(SC if upto >= 1 else 0):
                    vector.wait_ge(dCS[c % 2], 32 * (c // 2 + 1))
                    if c >= 2:
                        vector.wait_ge(dSpQ[c % 2], 16 * (c // 2))
                        vector.wait_ge(dSpK[c % 2], 16 * (c // 2))
                    vector.wait_ge(tQ, c + 1)
                    for et in range(HPC):
                        psl = ps_q[:, et * 512:(et + 1) * 512]
                        vector.tensor_mul(uv_sb[:, 0:512], psl, cos_sb[c % 2])
                        vector.tensor_mul(psl, psl, sin_sb[c % 2])
                        vector.tensor_sub(qst_sb[c % 2][0:64, et, :],
                                          uv_sb[0:64, 0:512], psl[64:128, :])
                        last = vector.tensor_add(qst_sb[c % 2][64:128, et, :],
                                          uv_sb[64:128, 0:512], psl[0:64, :])
                    last.then_inc(vRope, 1)
                    vector.wait_ge(tK, c + 1)
                    for et in range(HPC):
                        psl = ps_k[:, et * 512:(et + 1) * 512]
                        vector.tensor_mul(uv_sb[:, 0:512], psl, cos_sb[c % 2])
                        vector.tensor_mul(psl, psl, sin_sb[c % 2])
                        vector.tensor_sub(kst_sb[c % 2][0:64, et, :],
                                          uv_sb[0:64, 0:512], psl[64:128, :])
                        last = vector.tensor_add(kst_sb[c % 2][64:128, et, :],
                                          uv_sb[64:128, 0:512], psl[0:64, :])
                    last.then_inc(vRope, 1)

                if upto < 3:
                    return
                for ci in range(NCH):
                    for g in range(32):
                        gg = 32 * ci + g
                        vector.wait_ge(sExp, gg + 1)
                        pg = pring[:, g % 4, :]
                        if g % 4 == 0:
                            last = vector.tensor_copy(accb, pg)
                        else:
                            last = vector.tensor_add(accb, accb, pg)
                        if g % 4 == 3:
                            if g == 3:
                                last = vector.tensor_copy(acc32, accb)
                            else:
                                last = vector.tensor_add(acc32, acc32, accb)
                        last.then_inc(vZ, 1)
                    vector.tensor_add(accf, acc32[:, 0:512], acc32[:, 512:1024]).then_inc(vZf, 1)
                    vector.wait_ge(tZ, ci + 1)
                    vector.reciprocal(recip_sb, ps_zzb[0:1, :]).then_inc(vRec, 1)
                    vector.wait_ge(sZb, ci + 1)
                    vector.wait_ge(tPv, 32 * (ci + 1))
                    if ci >= 2:
                        vector.wait_ge(dCtx[ci % 2], 16 * (ci // 2))
                    vector.tensor_mul(ctxst[ci % 2], ps_o, zb_sb).then_inc(vO, 1)

                if upto < 4:
                    return
                for ec in range(32):
                    vector.wait_ge(tY, ec + 1)
                    vector.wait_ge(dXo[ec % 2], 16 * (ec // 2 + 1))
                    if ec >= 2:
                        vector.wait_ge(dS1[ec % 2], 16 * (ec // 2))
                    vector.tensor_add(s1_st[ec % 2], ps_y[ec % 2], xo_sb[ec % 2])
                    vector.tensor_copy(s1b_sb, s1_st[ec % 2])
                    vector.tensor_mul(sqb_sb, s1b_sb, s1b_sb).then_inc(vS1, 1)
                vector.wait_ge(tSt, 1)
                vector.tensor_scalar(mu_sb, ps_mu, 1.0 / D, None, ALU.mult)
                vector.tensor_scalar(msq_sb, ps_sq, 1.0 / D,
                                     None, ALU.mult).then_inc(vCh, 1)
                vector.wait_ge(vCh, vch[0] + 1)
                vch[0] += 1
                vector.tensor_mul(var_sb, mu_sb, mu_sb).then_inc(vCh, 1)
                vector.wait_ge(vCh, vch[0] + 1)
                vch[0] += 1
                vector.tensor_sub(var2_sb, msq_sb, var_sb).then_inc(vSt, 1)
                vector.wait_ge(tBc, 1)
                vector.tensor_copy(mub_sb, ps_bc[:, 0:1024])
                vector.tensor_copy(rstdb_sb, ps_bc[:, 1024:2048]).then_inc(vBc, 1)
                for ec in range(32):
                    vector.wait_ge(dS1r[ec % 2], 16 * (ec // 2 + 1))
                    if ec >= 2:
                        vector.wait_ge(dAgx[ec % 2], 16 * (ec // 2))
                    vector.tensor_sub(s1r_sb[ec % 2], s1r_sb[ec % 2], mub_sb)
                    vector.tensor_mul(xfb_st[ec % 2], s1r_sb[ec % 2], rstdb_sb).then_inc(vXf, 1)

                if upto < 6:
                    return
                for (ktn, mat, kt, rows, nwid, ngr, klen, chk, wid) in DQU:
                    u = UOFF[ktn] + chk
                    pss = ps_dq[u % 2]
                    if kt == 0 and chk == 0:
                        npk = NSLP if mat < 2 else D // 8
                        vector.wait_ge(dDq, 16 * 2 * (mat + 1))
                        vector.wait_ge(dSc, 16 * (mat + 1))
                        if mat >= 1:
                            vector.wait_ge(tOh, OHOFF[mat])
                            vector.wait_ge(tG, UMAT[mat])
                        vector.tensor_copy(gi_f[:, 0:klen], gi_i[:, 0:klen])
                        for jjj in range(8):
                            vector.tensor_scalar(z_sb[0:ngr, jjj:nwid:8],
                                                 qz_sb[0:ngr, 0:npk],
                                                 4 * jjj, 15,
                                                 ALU.logical_shift_right, ALU.bitwise_and)
                        vector.tensor_copy(w4f_sb[0:ngr, 0:nwid], z_sb[0:ngr, 0:nwid])
                        vector.tensor_scalar(zp1_sb[0:ngr, 0:nwid], w4f_sb[0:ngr, 0:nwid],
                                             1.0, None, ALU.add).then_inc(vOh, 1)
                        for ochk in range(OHC[mat]):
                            owid = min(512, klen - ochk * 512)
                            vector.wait_ge(tOh, OHOFF[mat] + ochk + 1)
                            vector.tensor_scalar(
                                oh_sb[0:ngr, ochk * 512:ochk * 512 + owid],
                                ps_ohb[0:ngr, 0:owid],
                                iota32_sb[0:ngr, 0:1] if mat < 2 else iota86_sb[0:ngr, 0:1],
                                None, ALU.is_equal).then_inc(vOh2, 1)
                    if chk == 0:
                        vector.wait_ge(dQw[ktn % 2], 16 * (ktn // 2 + 1))
                        if ktn >= 2:
                            vector.wait_ge(dWbf[ktn % 2], 16 * (ktn // 2))
                        vector.tensor_scalar(qw_sb[ktn % 2][0:rows, 0:nwid],
                                             qw_sb[ktn % 2][0:rows, 0:nwid],
                                             shifts_sb[0:rows, 0:1], 15,
                                             ALU.logical_shift_right, ALU.bitwise_and)
                        vector.tensor_copy(w4f_sb[0:rows, 0:nwid],
                                           qw_sb[ktn % 2][0:rows, 0:nwid])
                    vector.wait_ge(tG, u + 1)
                    csl = slice(chk * 512, chk * 512 + wid)
                    vector.tensor_sub(tdq_sb[0:rows, 0:wid],
                                      w4f_sb[0:rows, csl], pss[1][0:rows, 0:wid])
                    vector.tensor_mul(wst_sb[ktn % 2][0:rows, csl],
                                      tdq_sb[0:rows, 0:wid],
                                      pss[0][0:rows, 0:wid]).then_inc(vDq, 1)

                if upto < 7:
                    return
                for tc in range(16):
                    vector.wait_ge(dHgr[tc % 2], 16 * (tc // 2 + 1))
                    for nt in range(NKT):
                        rows = 128 if nt < 10 else 96
                        it = NKT * tc + nt
                        vector.wait_ge(tH2, it + 1)
                        vector.tensor_mul(hgr_sb[tc % 2][0:rows, nt, :],
                                          ps_h[it % 2][0:rows, :],
                                          hgr_sb[tc % 2][0:rows, nt, :]).then_inc(vH2, 1)
                if upto < 8:
                    return
                i3 = 0
                for j in range(4):
                    for tc in range(16):
                        if i3 >= 2:
                            pj, pt = divmod(i3 - 2, 16)
                            vector.wait_ge(dRs4[pj], 16 * (pt + 1))
                        for ecc in range(8):
                            ie = 8 * i3 + ecc
                            vector.wait_ge(tY3, ie + 1)
                            vector.tensor_copy(yst_sb[i3 % 2][:, ecc, :],
                                               ps_y3[ecc % 2]).then_inc(vE3, 1)
                        i3 += 1

                if upto < 9:
                    return
                for tt in range(NTT):
                    vector.wait_ge(dF1[tt % 2], 16 * (tt // 2 + 1))
                    vector.wait_ge(dYs[tt % 2], 64 * (tt // 2 + 1))
                    tsl = slice(tt * 128, (tt + 1) * 128)
                    for ec in range(32):
                        vector.tensor_sub(s2_sb[:, ec, :], s1f_sb[tt % 2][:, ec, :],
                                          mub_sb[:, tsl]).then_inc(vCh, 1)
                        vector.wait_ge(vCh, vch[0] + 1)
                        vch[0] += 1
                        vector.tensor_mul(s2_sb[:, ec, :], s2_sb[:, ec, :],
                                          rstdb_sb[:, tsl]).then_inc(vCh, 1)
                        vector.wait_ge(vCh, vch[0] + 1)
                        vch[0] += 1
                        last = vector.tensor_add(s2_sb[:, ec, :], s2_sb[:, ec, :],
                                          ysl_sb[tt % 2][:, ec, :])
                    last.then_inc(vF, 1)
                    vector.wait_ge(sTe, 32 * (tt + 1))
                    vector.tensor_mul(t3_sb, tmaj_sb[tt % 2], tmaj_sb[tt % 2])
                    vector.reduce_sum(red_sb[:, 0:1], tmaj_sb[tt % 2],
                                      mybir.AxisListType.X)
                    vector.reduce_sum(red_sb[:, 1:2], t3_sb,
                                      mybir.AxisListType.X).then_inc(vCh, 1)
                    vector.wait_ge(vCh, vch[0] + 1)
                    vch[0] += 1
                    vector.tensor_scalar(red_sb[:, 2:3], red_sb[:, 0:1],
                                         -1.0 / D, None, ALU.mult)
                    vector.tensor_scalar(red_sb[:, 4:5], red_sb[:, 1:2],
                                         1.0 / D, None, ALU.mult).then_inc(vCh, 1)
                    vector.wait_ge(vCh, vch[0] + 1)
                    vch[0] += 1
                    vector.tensor_mul(red_sb[:, 3:4], red_sb[:, 2:3],
                                      red_sb[:, 2:3]).then_inc(vCh, 1)
                    vector.wait_ge(vCh, vch[0] + 1)
                    vch[0] += 1
                    vector.tensor_sub(red_sb[:, 5:6], red_sb[:, 4:5],
                                      red_sb[:, 3:4]).then_inc(vLn, 1)
                    vector.wait_ge(sLn, tt + 1)
                    if tt >= 2:
                        vector.wait_ge(dOut[tt % 2], 16 * (tt // 2))
                    vector.tensor_scalar(out_st[tt % 2], tmaj_sb[tt % 2],
                                         red_sb[:, 2:3], red_sb[:, 3:4],
                                         ALU.add, ALU.mult).then_inc(vLn2, 1)

            @block.scalar
            def _(scalar):
                scalar.wait_ge(dPre, 16 * NPRE)
                for c in range(SC if upto >= 2 else 0):
                    scalar.wait_ge(tV, c + 1)
                    if c >= 2:
                        scalar.wait_ge(dSpV[c % 2], 64 * (c // 2))
                    scalar.copy(vst_sb[c % 2], ps_q).then_inc(sEv, 1)

                if upto < 3:
                    return
                for ci in range(NCH):
                    for g in range(32):
                        gg = 32 * ci + g
                        scalar.wait_ge(tS, gg + 1)
                        if gg >= 4:
                            scalar.wait_ge(tPv, gg - 3)
                            scalar.wait_ge(vZ, gg - 3)
                        scalar.activation(pring[:, g % 4, :], ps_s[g % 3], AF.Exp,
                                          bias=cv[:, C_EXPB:C_EXPB + 1],
                                          scale=cv[:, C_QKSC:C_QKSC + 1]).then_inc(sExp, 1)
                    scalar.wait_ge(tZb, ci + 1)
                    scalar.copy(zb_sb, ps_zzb).then_inc(sZb, 1)

                if upto < 4:
                    return
                scalar.wait_ge(vSt, 1)
                scalar.activation(var_sb, var2_sb, AF.Ln,
                                  bias=cv[0:1, C_EPS:C_EPS + 1]).then_inc(sCh, 1)
                scalar.wait_ge(sCh, sch[0] + 1)
                sch[0] += 1
                scalar.activation(rstd_sb, var_sb, AF.Exp,
                                  scale=cv[0:1, C_NHALF:C_NHALF + 1]).then_inc(sRstd, 1)

                if upto < 7:
                    return
                for tc in range(16):
                    for nt in range(NKT):
                        rows = 128 if nt < 10 else 96
                        it = NKT * tc + nt
                        scalar.wait_ge(tH, it + 1)
                        if tc >= 2 and nt == 0:
                            scalar.wait_ge(dHg[tc % 2], 16 * (tc // 2))
                        scalar.activation(hst_sb[tc % 2][0:rows, nt, :],
                                          ps_h[it % 2][0:rows, :],
                                          AF.Silu).then_inc(sSilu, 1)

                if upto < 9:
                    return
                for tt in range(NTT):
                    for ec in range(32):
                        scalar.wait_ge(tTr, 32 * tt + ec + 1)
                        scalar.copy(tmaj_sb[tt % 2][:, ec * 128:(ec + 1) * 128],
                                    ps_tr[ec % 2]).then_inc(sTe, 1)
                    scalar.wait_ge(vLn, tt + 1)
                    scalar.activation(red_sb[:, 4:5], red_sb[:, 5:6], AF.Ln,
                                      bias=cv[:, C_EPS:C_EPS + 1]).then_inc(sCh, 1)
                    scalar.wait_ge(sCh, sch[0] + 1)
                    sch[0] += 1
                    scalar.activation(red_sb[:, 3:4], red_sb[:, 4:5], AF.Exp,
                                      scale=cv[:, C_NHALF:C_NHALF + 1]).then_inc(sLn, 1)
                    scalar.wait_ge(vLn2, tt + 1)
                    scalar.dma_start(out=y_out[tt * 128:(tt + 1) * 128, :],
                                     in_=out_st[tt % 2]).then_inc(dOut[tt % 2], 16)
                scalar.wait_ge(dOut[0], 16 * NTT // 2)
                scalar.wait_ge(dOut[1], 16 * NTT // 2)

    return nc


def _prep_inputs(inputs):
    bf = ml_dtypes.bfloat16
    x = np.asarray(inputs["L0_attn_x"])
    Wq = np.asarray(inputs["L0_attn_Wq"])
    Wk = np.asarray(inputs["L0_attn_Wk"])
    Wv = np.asarray(inputs["L0_attn_Wv"])
    Wo = np.asarray(inputs["L0_attn_Wo"])

    x0 = np.ascontiguousarray(x.reshape(S, D))
    xh = np.ascontiguousarray(x0.T.reshape(32, 128, S).transpose(1, 0, 2))

    def tiled_T(Wm):
        return np.ascontiguousarray(Wm.T.reshape(32, 128, D).transpose(1, 0, 2))

    wqT, wkT, wvT, woT = map(tiled_T, (Wq, Wk, Wv, Wo))

    inv = 1.0 / (10000.0 ** (np.arange(0, HD, 2, dtype=np.float64) / HD))
    fr = inv[:, None] * np.arange(S, dtype=np.float64)[None, :]
    cos1 = np.cos(fr).astype(np.float32)
    sin1 = np.sin(fr).astype(np.float32)
    cos2 = np.ascontiguousarray(np.concatenate([cos1, cos1], axis=0))
    sin2 = np.ascontiguousarray(np.concatenate([sin1, sin1], axis=0))

    cvec = np.zeros((128, NCONST), dtype=np.float32)
    cvec[:, C_EXPB] = EXP_BIAS
    cvec[:, C_QKSC] = QK_SCALE
    cvec[:, C_EPS] = EPS
    cvec[:, C_NHALF] = -0.5

    shared = {
        "xh": xh, "cos2": cos2, "sin2": sin2,
        "ident": np.eye(128, dtype=np.float32),
        "ones_row": np.ones((1, 128), dtype=np.float32),
        "ones_col_f": np.ones((128, 1), dtype=np.float32),
        "ones_col_b": np.ones((128, 1), dtype=bf),
        "iota32": np.arange(32, dtype=np.float32).reshape(32, 1),
        "iota86": np.arange(86, dtype=np.float32).reshape(86, 1),
        "shifts": (4 * (np.arange(128) % 8)).astype(np.int32).reshape(128, 1),
        "cvec": cvec, "woT": woT,
        "g_gi": np.ascontiguousarray(np.asarray(inputs["L0_ffn_gate_g_idx"]).reshape(1, D)),
        "u_gi": np.ascontiguousarray(np.asarray(inputs["L0_ffn_up_g_idx"]).reshape(1, D)),
        "d_sc": np.ascontiguousarray(np.asarray(inputs["L0_ffn_down_scales"])),
        "d_qz": np.ascontiguousarray(np.asarray(inputs["L0_ffn_down_qzeros"])),
    }
    g_qw_full = np.asarray(inputs["L0_ffn_gate_qweight"])
    u_qw_full = np.asarray(inputs["L0_ffn_up_qweight"])
    d_qw_full = np.asarray(inputs["L0_ffn_down_qweight"])
    g_sc_full = np.asarray(inputs["L0_ffn_gate_scales"])
    u_sc_full = np.asarray(inputs["L0_ffn_up_scales"])
    g_qz_full = np.asarray(inputs["L0_ffn_gate_qzeros"])
    u_qz_full = np.asarray(inputs["L0_ffn_up_qzeros"])
    d_gi_full = np.asarray(inputs["L0_ffn_down_g_idx"])

    in_maps = []
    for c in range(W):
        esl = slice(EOWN * c, EOWN * (c + 1))
        nsl = slice(NSL * c, NSL * (c + 1))
        nslp = slice(NSLP * c, NSLP * (c + 1))
        m = dict(shared)
        m["x_own"] = np.ascontiguousarray(xh[:, :, TOK * c:TOK * (c + 1)])
        m["wqT"] = np.ascontiguousarray(wqT[:, :, esl])
        m["wkT"] = np.ascontiguousarray(wkT[:, :, esl])
        m["wvT"] = np.ascontiguousarray(wvT[:, :, esl])
        m["g_qw"] = np.ascontiguousarray(np.repeat(g_qw_full[:, nsl], 8, axis=0))
        m["u_qw"] = np.ascontiguousarray(np.repeat(u_qw_full[:, nsl], 8, axis=0))
        m["d_qw"] = np.ascontiguousarray(
            np.repeat(d_qw_full[NSLP * c:NSLP * (c + 1), :], 8, axis=0))
        m["g_sc"] = np.ascontiguousarray(g_sc_full[:, nsl])
        m["u_sc"] = np.ascontiguousarray(u_sc_full[:, nsl])
        m["g_qz"] = np.ascontiguousarray(g_qz_full[:, nslp])
        m["u_qz"] = np.ascontiguousarray(u_qz_full[:, nslp])
        m["d_gi"] = np.ascontiguousarray(d_gi_full[nsl].reshape(1, NSL))
        in_maps.append(m)
    return in_maps


def kernel(**inputs):
    in_maps = _prep_inputs(inputs)
    nc = build_nc()
    res = run_bass_kernel_spmd(nc, in_maps, core_ids=list(range(W)))
    out = np.concatenate([res.results[c]["y"] for c in range(W)], axis=0)
    return out.reshape(1, S, D).astype(np.float32)


# version 05
# speedup vs baseline: 1.0167x; 1.0167x over previous
import os
import contextlib
import numpy as np
import ml_dtypes

os.environ.setdefault("TRN_ALLOW_ICI_ALL_TO_ALL", "1")

import concourse.bass as bass
import concourse.mybir as mybir
from concourse.bass_utils import run_bass_kernel_spmd

F32 = mybir.dt.float32
BF16 = mybir.dt.bfloat16
I32 = mybir.dt.int32
AF = mybir.ActivationFunctionType
ALU = mybir.AluOpType

W = 8
S = 8192
D = 4096
HD = 128
HPC = 4
EOWN = HPC * HD
TOK = S // W
INTER = 11008
NSL = INTER // W
NSLP = NSL // 8
NG = 32
NGD = 86
NKT = 11
SC = 16
QC = 16
NCH = HPC * QC
NTT = TOK // 128

EXP_BIAS = -12.0
QK_SCALE = 1.0 / float(np.sqrt(HD))
EPS = 1e-5

C_EXPB, C_QKSC, C_EPS, C_NHALF = range(4)
NCONST = 4
RG = [list(range(W))]

DQT = []
for _mat in range(3):
    _nkt = 32 if _mat < 2 else NKT
    for _kt in range(_nkt):
        _rows = 96 if (_mat == 2 and _kt == NKT - 1) else 128
        DQT.append((_mat, _kt, _rows,
                    NSL if _mat < 2 else D,
                    NG if _mat < 2 else NGD,
                    D if _mat < 2 else NSL))
NDQ = len(DQT)
OHC = [(D + 511) // 512, (D + 511) // 512, (NSL + 511) // 512]
OHOFF = [0, 8, 16]
DQOFF = [0, 32, 64]
DQU = []
UOFF = []
for _ktn, (_mat, _kt, _rows, _nwid, _ngr, _klen) in enumerate(DQT):
    UOFF.append(len(DQU))
    for _chk in range(-(-_nwid // 512)):
        _wid = min(512, _nwid - _chk * 512)
        DQU.append((_ktn, _mat, _kt, _rows, _nwid, _ngr, _klen, _chk, _wid))
NDQU = len(DQU)
UOFF.append(NDQU)
UMAT = [0, UOFF[32], UOFF[64]]


def _ceil(a, b):
    return -(-a // b)


def build_nc(upto=9, dbg=()):
    nc = bass.Bass("TRN2", target_bir_lowering=False, num_devices=W)

    di = lambda n, s, d: nc.dram_tensor(n, s, d, kind="ExternalInput").ap()
    xh = di("xh", [128, 32, S], F32)
    x_own = di("x_own", [128, 32, TOK], F32)
    wqT = di("wqT", [128, 32, EOWN], F32)
    wkT = di("wkT", [128, 32, EOWN], F32)
    wvT = di("wvT", [128, 32, EOWN], F32)
    woT = di("woT", [128, 32, D], F32)
    cos2 = di("cos2", [128, S], F32)
    sin2 = di("sin2", [128, S], F32)
    ident = di("ident", [128, 128], F32)
    ones_row = di("ones_row", [1, 128], F32)
    ones_col_f = di("ones_col_f", [128, 1], F32)
    ones_col_b = di("ones_col_b", [128, 1], BF16)
    iota32 = di("iota32", [32, 1], F32)
    iota86 = di("iota86", [86, 1], F32)
    shifts = di("shifts", [128, 1], I32)
    cvec = di("cvec", [128, NCONST], F32)
    g_qw = di("g_qw", [D, NSL], I32)
    u_qw = di("u_qw", [D, NSL], I32)
    d_qw = di("d_qw", [NSL, D], I32)
    g_sc = di("g_sc", [NG, NSL], F32)
    u_sc = di("u_sc", [NG, NSL], F32)
    d_sc = di("d_sc", [NGD, D], F32)
    g_qz = di("g_qz", [NG, NSLP], I32)
    u_qz = di("u_qz", [NG, NSLP], I32)
    d_qz = di("d_qz", [NGD, D // 8], I32)
    g_gi = di("g_gi", [1, D], I32)
    u_gi = di("u_gi", [1, D], I32)
    d_gi = di("d_gi", [1, NSL], I32)
    y_out = nc.dram_tensor("y", [TOK, D], F32, kind="ExternalOutput").ap()

    dt = lambda n, s, d: nc.dram_tensor(
        n, s, d, kind=("ExternalOutput" if n in dbg else "Internal")).ap()
    q_d = dt("q_d", [HPC, 128, S], BF16)
    k_d = dt("k_d", [HPC, 128, S], BF16)
    v_d = dt("v_d", [HPC, 128, S // 128, HD], BF16)
    a2a_in = dt("a2a_in", [W, HPC, 128, TOK], BF16)
    a2a_out = dt("a2a_out", [W, HPC, 128, TOK], BF16)
    s1_d = dt("s1_d", [32, 128, TOK], F32)
    agx_in = dt("agx_in", [32, 128, TOK], BF16)
    agx_out = [nc.dram_tensor(f"agx_out{i}", [W, 8, 128, TOK], BF16,
                              kind="Internal", addr_space="Shared").ap()
               for i in range(4)]
    wg_d = dt("wg_d", [32, 128, NSL], BF16)
    wu_d = dt("wu_d", [32, 128, NSL], BF16)
    wd_d = dt("wd_d", [NKT, 128, D], BF16)
    hg_d = dt("hg_d", [NKT, 128, S], BF16)
    h_d = dt("h_d", [NKT, 128, S], BF16)
    rs_in = [dt(f"rs_in{j}", [W, 1024, TOK], F32) for j in range(4)]
    rs_out = [dt(f"rs_out{j}", [1024, TOK], F32) for j in range(4)]
    if "rs_dbg" in dbg:
        rsi_dbg = nc.dram_tensor("rsi_dbg", [W, 1024, TOK], F32, kind="ExternalOutput").ap()
        rso_dbg = nc.dram_tensor("rso_dbg", [1024, TOK], F32, kind="ExternalOutput").ap()
    WMAT = [wg_d, wu_d, wd_d]

    es = contextlib.ExitStack()
    mk = lambda n: es.enter_context(nc.semaphore(n))
    mk2 = lambda n: [mk(n + "0"), mk(n + "1")]
    (dPre, tQ, tK, vRope, tV, sEv, tS, sExp, vZ, tPv,
     vZf, tZ, vRec, tZb, sZb, vO, cA2A, dCtxIn, tY, vS1,
     tSt, vSt, sRstd, tBc, vBc, vXf, cAGX, dDq, dSc, tOh, vOh,
     vOh2, tG, vDq, dWres, tH, sSilu, tH2, vH2,
     tY3, vE3, cRS, vF, tTr, sTe, vLn, sLn, vLn2) = [mk(f"s{i}") for i in range(48)]
    dX = mk2("dX")
    dXv = mk2("dXv")
    dCS = mk2("dCS")
    dSpQ = mk2("dSpQ")
    dSpK = mk2("dSpK")
    dSpV = mk2("dSpV")
    dKV = mk2("dKV")
    dCtx = mk2("dCtx")
    dWo = mk2("dWo")
    dXo = mk2("dXo")
    dS1 = mk2("dS1")
    dS1r = mk2("dS1r")
    dAgx = mk2("dAgx")
    dQw = mk2("dQw")
    dWbf = mk2("dWbf")
    dXf = mk2("dXf")
    dHg = mk2("dHg")
    dHgr = mk2("dHgr")
    dH2 = mk2("dH2")
    dH3 = mk2("dH3")
    dRs4 = [mk(f"dRs{j}") for j in range(4)]
    dF1 = mk2("dF1")
    dYs = mk2("dYs")
    dOut = mk2("dOut")
    vCh = mk("vCh")
    sCh = mk("sCh")

    AL = nc.alloc_sbuf_tensor
    ident_sb = AL("ident_sb", [128, 128], F32).ap()
    ones_row_sb = AL("ones_row_sb", [1, 128], F32).ap()
    ones_col_f_sb = AL("ones_col_f_sb", [128, 1], F32).ap()
    ones_col_b_sb = AL("ones_col_b_sb", [128, 1], BF16).ap()
    iota32_sb = AL("iota32_sb", [32, 1], F32).ap()
    iota86_sb = AL("iota86_sb", [86, 1], F32).ap()
    shifts_sb = AL("shifts_sb", [128, 1], I32).ap()
    cv = AL("cv_sb", [128, NCONST], F32).ap()
    mub_sb = AL("mub_sb", [128, TOK], F32).ap()
    rstdb_sb = AL("rstdb_sb", [128, TOK], F32).ap()
    red_sb = AL("red_sb", [128, 6], F32).ap()

    ABASE = nc.sbuf_base
    ATOP = nc.sbuf_top

    class Ph:
        def __init__(self):
            self.off = ABASE

        def t(self, name, shape, dtype):
            per = int(np.prod(shape[1:])) * np.dtype(mybir.dt.np(dtype)).itemsize
            off = (self.off + 31) & ~31
            h = nc.alloc_sbuf_tensor_at(name, shape, dtype, offset=off)
            self.off = off + per
            assert self.off <= ATOP, f"{name}: arena overflow {self.off} > {ATOP}"
            return h.ap()

    ph = Ph()
    wqT_sb = ph.t("wqT_sb", [128, 32, EOWN], BF16)
    wkT_sb = ph.t("wkT_sb", [128, 32, EOWN], BF16)
    xt_sb = [ph.t(f"xt{i}", [128, 32, 512], BF16) for i in range(2)]
    cos_sb = [ph.t(f"cos{i}", [128, 512], F32) for i in range(2)]
    sin_sb = [ph.t(f"sin{i}", [128, 512], F32) for i in range(2)]
    uv_sb = ph.t("uv_sb", [128, 1024], F32)
    qst_sb = [ph.t(f"qst{i}", [128, HPC, 512], BF16) for i in range(2)]
    kst_sb = [ph.t(f"kst{i}", [128, HPC, 512], BF16) for i in range(2)]
    wvT_sb = ph.t("wvT_sb", [128, 32, EOWN], BF16)
    vst_sb = [ph.t(f"vst{i}", [128, 4 * EOWN], BF16) for i in range(2)]

    pa = Ph()
    k_sb = [pa.t(f"k_sb{i}", [128, S], BF16) for i in range(2)]
    v_sb = [pa.t(f"v_sb{i}", [128, S // 128, HD], BF16) for i in range(2)]
    q_sb = [pa.t(f"q_sb{i}", [128, S], BF16) for i in range(2)]
    pring = pa.t("pring", [128, 8, 1024], BF16)
    accb = pa.t("accb", [128, 1024], BF16)
    acc32 = pa.t("acc32", [128, 1024], F32)
    accf = pa.t("accf", [128, 512], F32)
    recip_sb = pa.t("recip_sb", [1, 512], F32)
    zb_sb = pa.t("zb_sb", [128, 512], F32)
    ctxst = [pa.t(f"ctxst{i}", [128, 512], BF16) for i in range(2)]

    po = Ph()
    ctx_sb = po.t("ctx_sb", [128, 32, TOK], BF16)
    woT_sb = [po.t(f"woT{i}", [128, 32, 128], BF16) for i in range(2)]
    xo_sb = [po.t(f"xo{i}", [128, TOK], F32) for i in range(2)]
    s1_st = [po.t(f"s1st{i}", [128, TOK], F32) for i in range(2)]
    s1b_sb = po.t("s1b_sb", [128, TOK], BF16)
    sqb_sb = po.t("sqb_sb", [128, TOK], BF16)
    mu_sb = po.t("mu_sb", [1, TOK], F32)
    msq_sb = po.t("msq_sb", [1, TOK], F32)
    var_sb = po.t("var_sb", [1, TOK], F32)
    var2_sb = po.t("var2_sb", [1, TOK], F32)
    rstd_sb = po.t("rstd_sb", [1, TOK], F32)
    s1r_sb = [po.t(f"s1r{i}", [128, TOK], F32) for i in range(2)]
    xfb_st = [po.t(f"xfb{i}", [128, TOK], BF16) for i in range(2)]

    pd = Ph()
    gi_i = pd.t("gi_i", [1, D], I32)
    gi_f = pd.t("gi_f", [1, D], F32)
    oh_sb = pd.t("oh_sb", [86, D], BF16)
    z_sb = pd.t("z_sb", [86, D], I32)
    zp1_sb = pd.t("zp1_sb", [86, D], BF16)
    sc_sb = pd.t("sc_sb", [86, D], BF16)
    qz_sb = pd.t("qz_sb", [86, 512], I32)
    qw_sb = [pd.t(f"qw{i}", [128, D], I32) for i in range(2)]
    w4f_sb = pd.t("w4f_sb", [128, D], F32)
    tdq_sb = pd.t("tdq_sb", [128, 512], F32)
    wst_sb = [pd.t(f"wst{i}", [128, D], BF16) for i in range(2)]

    pf = Ph()
    wres_raw = pf.t("wres_sb", [128, NKT * D], BF16)
    wres_gu = wres_raw[:, 0:32 * NSL].rearrange("p (k n) -> p k n", n=NSL)
    wres_d = wres_raw.rearrange("p (k n) -> p k n", n=D)
    xf_off = pf.off
    xf_sb = [pf.t(f"xf{i}", [128, 32, 512], BF16) for i in range(2)]
    hst_sb = [pf.t(f"hst{i}", [128, NKT, 512], BF16) for i in range(2)]
    hgr_sb = [pf.t(f"hgr{i}", [128, NKT, 512], BF16) for i in range(2)]
    yst_sb = [nc.alloc_sbuf_tensor_at(
        f"yst{i}", [128, 8, 512], F32,
        offset=((xf_off + 31) & ~31) + i * 8 * 512 * 4).ap() for i in range(2)]

    pz = Ph()
    s1f_sb = [pz.t(f"s1f{i}", [128, 32, 128], F32) for i in range(2)]
    ysl_sb = [pz.t(f"ysl{i}", [128, 32, 128], F32) for i in range(2)]
    s2_sb = pz.t("s2_sb", [128, 32, 128], F32)
    tmaj_sb = [pz.t(f"tmaj{i}", [128, D], F32) for i in range(2)]
    t3_sb = pz.t("t3_sb", [128, D], F32)
    out_st = [pz.t(f"outst{i}", [128, D], F32) for i in range(2)]

    ps_a = nc.alloc_psum_tensor("ps_a", [128, 2048], F32).ap()
    ps_b = nc.alloc_psum_tensor("ps_b", [128, 2048], F32).ap()
    ps_s = [ps_a[:, 0:1024], ps_a[:, 1024:2048], ps_b[:, 0:1024]]
    ps_o = ps_b[:, 1024:1536]
    ps_zzb = ps_b[:, 1536:2048]
    ps_q = ps_a
    ps_k = ps_b
    ps_y = [ps_a[:, 0:1024], ps_a[:, 1024:2048]]
    ps_mu = ps_b[0:1, 0:1024]
    ps_sq = ps_b[0:1, 1024:2048]
    ps_bc = ps_a
    ps_dq = [[ps_a[:, 0:512], ps_a[:, 512:1024]],
             [ps_a[:, 1024:1536], ps_a[:, 1536:2048]]]
    ps_ohb = ps_b[0:86, 0:512]
    ps_h = [ps_a[:, 0:512], ps_a[:, 512:1024], ps_a[:, 1024:1536], ps_a[:, 1536:2048]]
    ps_y3 = [ps_b[:, 0:512], ps_b[:, 512:1024]]
    ps_tr = [ps_b[:, 0:128], ps_b[:, 512:640]]

    NPRE = 10
    vch = [0]
    sch = [0]

    with es:
        with nc.Block() as block:

            @block.sync
            def _(sync):
                sync.dma_start(out=ident_sb, in_=ident).then_inc(dPre, 16)
                sync.dma_start(out=ones_row_sb, in_=ones_row).then_inc(dPre, 16)
                sync.dma_start(out=ones_col_f_sb, in_=ones_col_f).then_inc(dPre, 16)
                sync.dma_start(out=ones_col_b_sb, in_=ones_col_b).then_inc(dPre, 16)
                sync.dma_start(out=iota32_sb, in_=iota32).then_inc(dPre, 16)
                sync.dma_start(out=iota86_sb, in_=iota86).then_inc(dPre, 16)
                sync.dma_start(out=shifts_sb, in_=shifts).then_inc(dPre, 16)
                sync.dma_start(out=cv, in_=cvec).then_inc(dPre, 16)

                for c in range(SC if upto >= 1 else 0):
                    sl = slice(c * 512, (c + 1) * 512)
                    if c >= 2:
                        sync.wait_ge(vRope, 2 * (c - 1))
                    sync.dma_start(out=cos_sb[c % 2], in_=cos2[:, sl]).then_inc(dCS[c % 2], 16)
                    sync.dma_start(out=sin_sb[c % 2], in_=sin2[:, sl]).then_inc(dCS[c % 2], 16)
                    sync.wait_ge(vRope, 2 * c + 1)
                    sync.dma_start(out=q_d[:, :, sl].rearrange("h p s -> p h s"),
                                   in_=qst_sb[c % 2]).then_inc(dSpQ[c % 2], 16)
                    sync.wait_ge(vRope, 2 * c + 2)
                    sync.dma_start(out=k_d[:, :, sl].rearrange("h p s -> p h s"),
                                   in_=kst_sb[c % 2]).then_inc(dSpK[c % 2], 16)
                for c in range(SC if upto >= 2 else 0):
                    sl = slice(c * 512, (c + 1) * 512)
                    sync.wait_ge(sEv, c + 1)
                    vv = vst_sb[c % 2].rearrange("p (t h d) -> p t h d", t=4, h=HPC)
                    for hh in range(HPC):
                        sync.dma_start(
                            out=v_d[hh, :, 4 * c:4 * c + 4, :].rearrange("p t d -> p t d"),
                            in_=vv[:, :, hh, :]).then_inc(dSpV[c % 2], 16)

                for h in range(HPC if upto >= 3 else 0):
                    if h == 0:
                        for sm_ in (dSpQ[0], dSpQ[1], dSpK[0], dSpK[1]):
                            sync.wait_ge(sm_, 16 * 8)
                        sync.wait_ge(dSpV[0], 64 * 8)
                        sync.wait_ge(dSpV[1], 64 * 8)
                    elif h >= 2:
                        sync.wait_ge(tPv, 32 * QC * (h - 1))
                    sync.dma_start(out=k_sb[h % 2], in_=k_d[h]).then_inc(dKV[h % 2], 16)
                    sync.dma_start(out=v_sb[h % 2], in_=v_d[h]).then_inc(dKV[h % 2], 16)
                    sync.dma_start(out=q_sb[h % 2], in_=q_d[h]).then_inc(dKV[h % 2], 16)

                if upto < 4:
                    return
                sync.wait_ge(cA2A, 1)
                sync.dma_start(out=ctx_sb,
                               in_=a2a_out.rearrange("r h p t -> p (r h) t")
                               ).then_inc(dCtxIn, 16)
                for ec in range(32):
                    if ec >= 2:
                        sync.wait_ge(vS1, ec - 1)
                    sync.dma_start(out=xo_sb[ec % 2], in_=x_own[:, ec, :]).then_inc(dXo[ec % 2], 16)
                    sync.wait_ge(vS1, ec + 1)
                    sync.dma_start(out=s1_d[ec], in_=s1_st[ec % 2]).then_inc(dS1[ec % 2], 16)
                for ec in range(32):
                    if ec >= 2:
                        sync.wait_ge(vXf, ec - 1)
                    sync.dma_start(out=s1r_sb[ec % 2], in_=s1_d[ec]).then_inc(dS1r[ec % 2], 16)
                    sync.wait_ge(vXf, ec + 1)
                    sync.dma_start(out=agx_in[ec], in_=xfb_st[ec % 2]).then_inc(dAgx[ec % 2], 16)

                if upto < 6:
                    return
                sync.wait_ge(vXf, 32)
                sync.dma_start(out=qz_sb[0:NG, 0:NSLP], in_=g_qz).then_inc(dDq, 16)
                sync.dma_start(out=gi_i, in_=g_gi).then_inc(dDq, 16)
                for ktn, (mat, kt, rows, nwid, ngr, klen) in enumerate(DQT):
                    if ktn >= 2:
                        sync.wait_ge(vDq, UOFF[ktn - 1])
                    sync.dma_start(
                        out=qw_sb[ktn % 2][0:rows, 0:nwid],
                        in_=(g_qw, u_qw, d_qw)[mat][kt * 128: kt * 128 + rows, :]
                    ).then_inc(dQw[ktn % 2], 16)
                    if ktn >= 1:
                        pm, pk, prow, pnw, _, _ = DQT[ktn - 1]
                        sync.wait_ge(vDq, UOFF[ktn])
                        sync.dma_start(
                            out=WMAT[pm][pk][0:prow, :],
                            in_=wst_sb[(ktn - 1) % 2][0:prow, 0:pnw]).then_inc(dWbf[(ktn - 1) % 2], 16)
                    if mat < 2 and kt == 31:
                        sync.wait_ge(vOh, mat + 1)
                        sync.wait_ge(tOh, OHOFF[mat + 1])
                        nxt_qz, nxt_gi = (u_qz, u_gi) if mat == 0 else (d_qz, d_gi)
                        sync.dma_start(out=qz_sb[0:(NG if mat == 0 else NGD),
                                                 0:(NSLP if mat == 0 else D // 8)],
                                       in_=nxt_qz).then_inc(dDq, 16)
                        sync.dma_start(out=gi_i[:, 0:(D if mat == 0 else NSL)],
                                       in_=nxt_gi).then_inc(dDq, 16)
                pm, pk, prow, pnw, _, _ = DQT[NDQ - 1]
                sync.wait_ge(vDq, NDQU)
                sync.dma_start(out=WMAT[pm][pk][0:prow, :],
                               in_=wst_sb[(NDQ - 1) % 2][0:prow, 0:pnw]).then_inc(dWbf[(NDQ - 1) % 2], 16)

                if upto < 7:
                    return
                sync.wait_ge(dWbf[0], 16 * 38)
                sync.wait_ge(dWbf[1], 16 * 37)
                sync.dma_start(out=wres_gu,
                               in_=wg_d.rearrange("k p n -> p k n")).then_inc(dWres, 16)
                for tc in range(16):
                    r, hf = tc // 2, tc % 2
                    if tc == 0:
                        sync.wait_ge(cAGX, 4)
                    if tc >= 2:
                        sync.wait_ge(tH, NKT * (tc - 1))
                    for i in range(4):
                        sync.dma_start(
                            out=xf_sb[tc % 2][:, 8 * i:8 * (i + 1), :],
                            in_=agx_out[i][r, :, :, hf * 512:(hf + 1) * 512].rearrange(
                                "m p t -> p m t")).then_inc(dXf[tc % 2], 16)
                    sync.wait_ge(sSilu, NKT * (tc + 1))
                    sync.dma_start(
                        out=hg_d[:, :, tc * 512:(tc + 1) * 512].rearrange("n p t -> p n t"),
                        in_=hst_sb[tc % 2]).then_inc(dHg[tc % 2], 16)
                sync.wait_ge(tH, NKT * 16)
                sync.dma_start(out=wres_gu,
                               in_=wu_d.rearrange("k p n -> p k n")).then_inc(dWres, 16)
                for tc in range(16):
                    r, hf = tc // 2, tc % 2
                    if tc >= 2:
                        sync.wait_ge(tH2, NKT * (tc - 1))
                        sync.wait_ge(dH2[tc % 2], 16 * (tc // 2))
                    for i in range(4):
                        sync.dma_start(
                            out=xf_sb[tc % 2][:, 8 * i:8 * (i + 1), :],
                            in_=agx_out[i][r, :, :, hf * 512:(hf + 1) * 512].rearrange(
                                "m p t -> p m t")).then_inc(dXf[tc % 2], 16)
                    sync.dma_start(
                        out=hgr_sb[tc % 2],
                        in_=hg_d[:, :, tc * 512:(tc + 1) * 512].rearrange(
                            "n p t -> p n t")).then_inc(dHgr[tc % 2], 16)
                    sync.wait_ge(vH2, NKT * (tc + 1))
                    sync.dma_start(
                        out=h_d[:, :, tc * 512:(tc + 1) * 512].rearrange("n p t -> p n t"),
                        in_=hgr_sb[tc % 2]).then_inc(dH2[tc % 2], 16)
                if upto < 8:
                    return
                sync.wait_ge(tH2, NKT * 16)
                sync.wait_ge(dH2[0], 16 * 8)
                sync.wait_ge(dH2[1], 16 * 8)
                sync.dma_start(out=wres_d,
                               in_=wd_d.rearrange("k p n -> p k n")).then_inc(dWres, 16)
                i3 = 0
                for j in range(4):
                    for tc in range(16):
                        r, hf = tc // 2, tc % 2
                        if i3 >= 2:
                            sync.wait_ge(tY3, 8 * (i3 - 1))
                        sync.dma_start(
                            out=hgr_sb[i3 % 2],
                            in_=h_d[:, :, tc * 512:(tc + 1) * 512].rearrange(
                                "n p t -> p n t")).then_inc(dH3[i3 % 2], 16)
                        sync.wait_ge(vE3, 8 * (i3 + 1))
                        sync.dma_start(
                            out=rs_in[j][r, :, hf * 512:(hf + 1) * 512].rearrange(
                                "(ee p) t -> p ee t", p=128),
                            in_=yst_sb[i3 % 2]).then_inc(dRs4[j], 16)
                        i3 += 1

                if "rs_dbg" in dbg:
                    sync.wait_ge(dRs4[0], 16 * 16)
                    sync.dma_start(out=rsi_dbg, in_=rs_in[0]).then_inc(dPre, 16)
                    sync.wait_ge(cRS, 1)
                    sync.dma_start(out=rso_dbg, in_=rs_out[0]).then_inc(dPre, 16)
                if upto < 9:
                    return
                for tt in range(NTT):
                    tsl = slice(tt * 128, (tt + 1) * 128)
                    if tt == 0:
                        sync.wait_ge(cRS, 4)
                    if tt >= 2:
                        sync.wait_ge(vF, tt - 1)
                    sync.dma_start(
                        out=s1f_sb[tt % 2],
                        in_=s1_d[:, :, tsl].rearrange("e p t -> p e t")).then_inc(dF1[tt % 2], 16)
                    for j in range(4):
                        sync.dma_start(
                            out=ysl_sb[tt % 2][:, j * 8:(j + 1) * 8, :],
                            in_=rs_out[j][:, tsl].rearrange(
                                "(ee p) t -> p ee t", p=128)).then_inc(dYs[tt % 2], 16)

            @block.gpsimd
            def _(gpsimd):
                gpsimd.dma_start(out=wqT_sb, in_=wqT).then_inc(dPre, 16)
                gpsimd.dma_start(out=wkT_sb, in_=wkT).then_inc(dPre, 16)
                for c in range(SC if upto >= 1 else 0):
                    if c >= 2:
                        gpsimd.wait_ge(tK, c - 1)
                    gpsimd.dma_start(out=xt_sb[c % 2],
                                     in_=xh[:, :, c * 512:(c + 1) * 512]).then_inc(dX[c % 2], 16)
                if upto < 2:
                    return
                gpsimd.wait_ge(tK, SC)
                gpsimd.wait_ge(vRope, 2 * SC)
                for sm_ in (dSpQ[0], dSpQ[1], dSpK[0], dSpK[1]):
                    gpsimd.wait_ge(sm_, 16 * 8)
                gpsimd.dma_start(out=wvT_sb, in_=wvT).then_inc(dPre, 16)
                for c in range(SC):
                    if c >= 2:
                        gpsimd.wait_ge(tV, c - 1)
                    gpsimd.dma_start(out=xt_sb[c % 2],
                                     in_=xh[:, :, c * 512:(c + 1) * 512]).then_inc(dXv[c % 2], 16)
                if upto < 3:
                    return
                for ci in range(NCH):
                    h, qc = divmod(ci, QC)
                    gpsimd.wait_ge(vO, ci + 1)
                    gpsimd.dma_start(
                        out=a2a_in[qc // 2, h, :, (qc % 2) * 512:((qc % 2) + 1) * 512],
                        in_=ctxst[ci % 2]).then_inc(dCtx[ci % 2], 16)
                if upto < 4:
                    return
                gpsimd.wait_ge(dCtx[0], 16 * NCH // 2)
                gpsimd.wait_ge(dCtx[1], 16 * NCH // 2)
                gpsimd.collective_compute(
                    "AllToAll", ALU.bypass, replica_groups=RG,
                    ins=[a2a_in.opt()], outs=[a2a_out.opt()]).then_inc(cA2A, 1)
                for ec in range(32):
                    if ec >= 2:
                        gpsimd.wait_ge(tY, ec - 1)
                    gpsimd.dma_start(out=woT_sb[ec % 2],
                                     in_=woT[:, :, ec * 128:(ec + 1) * 128]).then_inc(dWo[ec % 2], 16)
                if upto < 5:
                    return
                for i in range(4):
                    gpsimd.wait_ge(dAgx[0], 16 * 4 * (i + 1))
                    gpsimd.wait_ge(dAgx[1], 16 * 4 * (i + 1))
                    gpsimd.collective_compute(
                        "AllGather", ALU.bypass, replica_groups=RG,
                        ins=[agx_in[8 * i:8 * (i + 1)].opt()],
                        outs=[agx_out[i].opt()]).then_inc(cAGX, 1)
                if upto < 6:
                    return
                gpsimd.dma_start(out=sc_sb[0:NG, 0:NSL], in_=g_sc).then_inc(dSc, 16)
                gpsimd.wait_ge(tG, UMAT[1])
                gpsimd.dma_start(out=sc_sb[0:NG, 0:NSL], in_=u_sc).then_inc(dSc, 16)
                gpsimd.wait_ge(tG, UMAT[2])
                gpsimd.dma_start(out=sc_sb[0:NGD, 0:D], in_=d_sc).then_inc(dSc, 16)
                if upto < 8:
                    return
                for j in range(4):
                    gpsimd.wait_ge(dRs4[j], 16 * 16)
                    gpsimd.collective_compute(
                        "ReduceScatter", ALU.add, replica_groups=RG,
                        ins=[rs_in[j].opt()], outs=[rs_out[j].opt()]).then_inc(cRS, 1)

            @block.tensor
            def _(tensor):
                tensor.wait_ge(dPre, 16 * NPRE)
                for c in range(SC if upto >= 1 else 0):
                    tensor.wait_ge(dX[c % 2], 16 * (c // 2 + 1))
                    if c >= 1:
                        tensor.wait_ge(vRope, 2 * c - 1)
                    for et in range(HPC):
                        for m in range(32):
                            last = tensor.matmul(ps_q[:, et * 512:(et + 1) * 512],
                                          wqT_sb[:, m, et * 128:(et + 1) * 128],
                                          xt_sb[c % 2][:, m, :],
                                          start=(m == 0), stop=(m == 31))
                    last.then_inc(tQ, 1)
                    if c >= 1:
                        tensor.wait_ge(vRope, 2 * c)
                    for et in range(HPC):
                        for m in range(32):
                            last = tensor.matmul(ps_k[:, et * 512:(et + 1) * 512],
                                          wkT_sb[:, m, et * 128:(et + 1) * 128],
                                          xt_sb[c % 2][:, m, :],
                                          start=(m == 0), stop=(m == 31))
                    last.then_inc(tK, 1)
                if upto < 2:
                    return
                tensor.wait_ge(vRope, 2 * SC)
                for c in range(SC):
                    tensor.wait_ge(dXv[c % 2], 16 * (c // 2 + 1))
                    tensor.wait_ge(dPre, 16 * (NPRE + 1))
                    if c >= 1:
                        tensor.wait_ge(sEv, c)
                    for st in range(4):
                        for m in range(32):
                            last = tensor.matmul(ps_q[:, st * 512:(st + 1) * 512],
                                          xt_sb[c % 2][:, m, st * 128:(st + 1) * 128],
                                          wvT_sb[:, m, :],
                                          start=(m == 0), stop=(m == 31))
                    last.then_inc(tV, 1)

                if upto < 3:
                    return
                def z_mm(ci_):
                    tensor.wait_ge(vZf, ci_ + 1)
                    tensor.wait_ge(sZb, ci_)
                    tensor.matmul(ps_zzb[0:1, :], ones_col_f_sb, accf,
                                  start=True, stop=True).then_inc(tZ, 1)

                def zb_mm(ci_):
                    tensor.wait_ge(vRec, ci_ + 1)
                    tensor.matmul(ps_zzb, ones_row_sb, recip_sb,
                                  start=True, stop=True).then_inc(tZb, 1)

                for h in range(HPC):
                    tensor.wait_ge(dKV[h % 2], 16 * 3 * (h // 2 + 1))
                    for qc in range(QC):
                        ci = h * QC + qc
                        qsl = slice(qc * 512, (qc + 1) * 512)
                        for g in range(32):
                            gg = 32 * ci + g
                            if ci >= 1 and g == 0:
                                z_mm(ci - 1)
                            if ci >= 1 and g == 2:
                                zb_mm(ci - 1)
                            tensor.wait_ge(sExp, max(0, gg - 2))
                            if ci == 0 and g == 0:
                                tensor.wait_ge(sEv, SC)
                            for jj in range(2):
                                kt = 2 * g + jj
                                last = tensor.matmul(ps_s[g % 3][:, jj * 512:(jj + 1) * 512],
                                              k_sb[h % 2][:, kt * 128:(kt + 1) * 128],
                                              q_sb[h % 2][:, qsl],
                                              start=True, stop=True)
                            last.then_inc(tS, 1)
                            if g >= 4:
                                tensor.wait_ge(sExp, gg - 3)
                                if g == 4:
                                    tensor.wait_ge(vO, ci)
                                for jj in range(2):
                                    kt = 2 * (g - 4) + jj
                                    last = tensor.matmul(ps_o, v_sb[h % 2][:, kt, :],
                                                  pring[:, (g - 4) % 8,
                                                        jj * 512:(jj + 1) * 512],
                                                  start=(kt == 0), stop=False)
                                last.then_inc(tPv, 1)
                        for gl in (28, 29, 30, 31):
                            gg = 32 * ci + gl
                            tensor.wait_ge(sExp, gg + 1)
                            for jj in range(2):
                                kt = 2 * gl + jj
                                last = tensor.matmul(ps_o, v_sb[h % 2][:, kt, :],
                                              pring[:, gl % 8, jj * 512:(jj + 1) * 512],
                                              start=False, stop=(kt == 63))
                            last.then_inc(tPv, 1)
                z_mm(NCH - 1)
                zb_mm(NCH - 1)

                if upto < 4:
                    return
                tensor.wait_ge(dCtxIn, 16)
                for ec in range(32):
                    tensor.wait_ge(dWo[ec % 2], 16 * (ec // 2 + 1))
                    if ec < 2:
                        tensor.wait_ge(sZb, NCH)
                        tensor.wait_ge(vO, NCH)
                    else:
                        tensor.wait_ge(vS1, ec - 1)
                    for ts in range(2):
                        for m in range(32):
                            last = tensor.matmul(ps_y[ec % 2][:, ts * 512:(ts + 1) * 512],
                                          woT_sb[ec % 2][:, m, :],
                                          ctx_sb[:, m, ts * 512:(ts + 1) * 512],
                                          start=(m == 0), stop=(m == 31))
                    last.then_inc(tY, 1)
                    tensor.wait_ge(vS1, ec + 1)
                    for ts in range(2):
                        tensor.matmul(ps_mu[:, ts * 512:(ts + 1) * 512], ones_col_b_sb,
                                      s1b_sb[:, ts * 512:(ts + 1) * 512],
                                      start=(ec == 0), stop=(ec == 31))
                        last = tensor.matmul(ps_sq[:, ts * 512:(ts + 1) * 512], ones_col_b_sb,
                                      sqb_sb[:, ts * 512:(ts + 1) * 512],
                                      start=(ec == 0), stop=(ec == 31))
                    if ec == 31:
                        last.then_inc(tSt, 1)
                tensor.wait_ge(sRstd, 1)
                for ts in range(2):
                    tensor.matmul(ps_bc[:, ts * 512:(ts + 1) * 512], ones_row_sb,
                                  mu_sb[:, ts * 512:(ts + 1) * 512], start=True, stop=True)
                    last = tensor.matmul(ps_bc[:, 1024 + ts * 512:1024 + (ts + 1) * 512],
                                  ones_row_sb, rstd_sb[:, ts * 512:(ts + 1) * 512],
                                  start=True, stop=True)
                last.then_inc(tBc, 1)

                if upto < 6:
                    return
                tensor.wait_ge(vBc, 1)
                for (ktn, mat, kt, rows, nwid, ngr, klen, chk, wid) in DQU:
                    if kt == 0 and chk == 0:
                        tensor.wait_ge(vOh, mat + 1)
                        for ochk in range(OHC[mat]):
                            owid = min(512, klen - ochk * 512)
                            if OHOFF[mat] + ochk >= 1:
                                tensor.wait_ge(vOh2, OHOFF[mat] + ochk)
                            tensor.matmul(ps_ohb[0:ngr, 0:owid], ones_row_sb[:, 0:ngr],
                                          gi_f[:, ochk * 512:ochk * 512 + owid],
                                          start=True, stop=True).then_inc(tOh, 1)
                        tensor.wait_ge(vOh2, OHOFF[mat] + OHC[mat])
                        tensor.wait_ge(dSc, 16 * (mat + 1))
                    u = UOFF[ktn] + chk
                    if u >= 2:
                        tensor.wait_ge(vDq, u - 1)
                    pss = ps_dq[u % 2]
                    tensor.matmul(pss[0][0:rows, 0:wid],
                                  oh_sb[0:ngr, kt * 128:kt * 128 + rows],
                                  sc_sb[0:ngr, chk * 512:chk * 512 + wid],
                                  start=True, stop=True)
                    tensor.matmul(pss[1][0:rows, 0:wid],
                                  oh_sb[0:ngr, kt * 128:kt * 128 + rows],
                                  zp1_sb[0:ngr, chk * 512:chk * 512 + wid],
                                  start=True, stop=True).then_inc(tG, 1)

                if upto < 7:
                    return
                tensor.wait_ge(vDq, NDQU)
                tensor.wait_ge(dWres, 16)
                for tc in range(16):
                    tensor.wait_ge(dXf[tc % 2], 64 * (tc // 2 + 1))
                    for nt in range(NKT):
                        rows = 128 if nt < 10 else 96
                        it = NKT * tc + nt
                        if it >= 4:
                            tensor.wait_ge(sSilu, it - 3)
                        for kt in range(32):
                            last = tensor.matmul(ps_h[it % 4][0:rows, :],
                                          wres_gu[:, kt, nt * 128:nt * 128 + rows],
                                          xf_sb[tc % 2][:, kt, :],
                                          start=(kt == 0), stop=(kt == 31))
                        last.then_inc(tH, 1)
                tensor.wait_ge(dWres, 32)
                tensor.wait_ge(sSilu, NKT * 16)
                for tc in range(16):
                    tensor.wait_ge(dXf[tc % 2], 64 * (8 + tc // 2 + 1))
                    for nt in range(NKT):
                        rows = 128 if nt < 10 else 96
                        it = NKT * tc + nt
                        if it >= 4:
                            tensor.wait_ge(vH2, it - 3)
                        for kt in range(32):
                            last = tensor.matmul(ps_h[it % 4][0:rows, :],
                                          wres_gu[:, kt, nt * 128:nt * 128 + rows],
                                          xf_sb[tc % 2][:, kt, :],
                                          start=(kt == 0), stop=(kt == 31))
                        last.then_inc(tH2, 1)
                if upto < 8:
                    return
                tensor.wait_ge(dWres, 48)
                tensor.wait_ge(vH2, NKT * 16)
                i3 = 0
                for j in range(4):
                    for tc in range(16):
                        tensor.wait_ge(dH3[i3 % 2], 16 * (i3 // 2 + 1))
                        for ecc in range(8):
                            ie = 8 * i3 + ecc
                            if ie >= 2:
                                tensor.wait_ge(vE3, ie - 1)
                            esl = slice((j * 8 + ecc) * 128, (j * 8 + ecc + 1) * 128)
                            for nt in range(NKT):
                                rows = 128 if nt < 10 else 96
                                last = tensor.matmul(ps_y3[ecc % 2],
                                              wres_d[0:rows, nt, esl],
                                              hgr_sb[i3 % 2][0:rows, nt, :],
                                              start=(nt == 0), stop=(nt == NKT - 1))
                            last.then_inc(tY3, 1)
                        i3 += 1

                if upto < 9:
                    return
                for tt in range(NTT):
                    tensor.wait_ge(vF, tt + 1)
                    for ec in range(32):
                        ie = 32 * tt + ec
                        if ie >= 2:
                            tensor.wait_ge(sTe, ie - 1)
                        tensor.transpose(ps_tr[ec % 2], s2_sb[:, ec, :],
                                         ident_sb).then_inc(tTr, 1)

            @block.vector
            def _(vector):
                vector.wait_ge(dPre, 16 * NPRE)
                for c in range(SC if upto >= 1 else 0):
                    vector.wait_ge(dCS[c % 2], 32 * (c // 2 + 1))
                    if c >= 2:
                        vector.wait_ge(dSpQ[c % 2], 16 * (c // 2))
                        vector.wait_ge(dSpK[c % 2], 16 * (c // 2))
                    vector.wait_ge(tQ, c + 1)
                    for et in range(HPC):
                        psl = ps_q[:, et * 512:(et + 1) * 512]
                        vector.tensor_mul(uv_sb[:, 0:512], psl, cos_sb[c % 2])
                        vector.tensor_mul(psl, psl, sin_sb[c % 2])
                        vector.tensor_sub(qst_sb[c % 2][0:64, et, :],
                                          uv_sb[0:64, 0:512], psl[64:128, :])
                        last = vector.tensor_add(qst_sb[c % 2][64:128, et, :],
                                          uv_sb[64:128, 0:512], psl[0:64, :])
                    last.then_inc(vRope, 1)
                    vector.wait_ge(tK, c + 1)
                    for et in range(HPC):
                        psl = ps_k[:, et * 512:(et + 1) * 512]
                        vector.tensor_mul(uv_sb[:, 0:512], psl, cos_sb[c % 2])
                        vector.tensor_mul(psl, psl, sin_sb[c % 2])
                        vector.tensor_sub(kst_sb[c % 2][0:64, et, :],
                                          uv_sb[0:64, 0:512], psl[64:128, :])
                        last = vector.tensor_add(kst_sb[c % 2][64:128, et, :],
                                          uv_sb[64:128, 0:512], psl[0:64, :])
                    last.then_inc(vRope, 1)

                if upto < 3:
                    return
                def v_recip(ci_):
                    vector.wait_ge(tZ, ci_ + 1)
                    vector.reciprocal(recip_sb, ps_zzb[0:1, :]).then_inc(vRec, 1)

                def v_zbcopy(ci_):
                    vector.wait_ge(tZb, ci_ + 1)
                    vector.tensor_copy(zb_sb, ps_zzb).then_inc(sZb, 1)

                def v_oevac(ci_):
                    vector.wait_ge(tPv, 32 * (ci_ + 1))
                    if ci_ >= 2:
                        vector.wait_ge(dCtx[ci_ % 2], 16 * (ci_ // 2))
                    vector.tensor_mul(ctxst[ci_ % 2], ps_o, zb_sb).then_inc(vO, 1)

                for ci in range(NCH):
                    for g in range(32):
                        gg = 32 * ci + g
                        if ci >= 1 and g == 2:
                            v_recip(ci - 1)
                        if ci >= 1 and g == 3:
                            v_zbcopy(ci - 1)
                        if ci >= 1 and g == 4:
                            v_oevac(ci - 1)
                        vector.wait_ge(sExp, gg + 1)
                        pg = pring[:, g % 8, :]
                        if g % 4 == 0:
                            last = vector.tensor_copy(accb, pg)
                        else:
                            last = vector.tensor_add(accb, accb, pg)
                        if g % 4 == 3:
                            if g == 3:
                                last = vector.tensor_copy(acc32, accb)
                            else:
                                last = vector.tensor_add(acc32, acc32, accb)
                        last.then_inc(vZ, 1)
                    vector.wait_ge(tZ, ci)
                    vector.tensor_add(accf, acc32[:, 0:512], acc32[:, 512:1024]).then_inc(vZf, 1)
                v_recip(NCH - 1)
                v_zbcopy(NCH - 1)
                v_oevac(NCH - 1)

                if upto < 4:
                    return
                for ec in range(32):
                    vector.wait_ge(tY, ec + 1)
                    vector.wait_ge(dXo[ec % 2], 16 * (ec // 2 + 1))
                    if ec >= 2:
                        vector.wait_ge(dS1[ec % 2], 16 * (ec // 2))
                    vector.tensor_add(s1_st[ec % 2], ps_y[ec % 2], xo_sb[ec % 2])
                    vector.tensor_copy(s1b_sb, s1_st[ec % 2])
                    vector.tensor_mul(sqb_sb, s1b_sb, s1b_sb).then_inc(vS1, 1)
                vector.wait_ge(tSt, 1)
                vector.tensor_scalar(mu_sb, ps_mu, 1.0 / D, None, ALU.mult)
                vector.tensor_scalar(msq_sb, ps_sq, 1.0 / D,
                                     None, ALU.mult).then_inc(vCh, 1)
                vector.wait_ge(vCh, vch[0] + 1)
                vch[0] += 1
                vector.tensor_mul(var_sb, mu_sb, mu_sb).then_inc(vCh, 1)
                vector.wait_ge(vCh, vch[0] + 1)
                vch[0] += 1
                vector.tensor_sub(var2_sb, msq_sb, var_sb).then_inc(vSt, 1)
                vector.wait_ge(tBc, 1)
                vector.tensor_copy(mub_sb, ps_bc[:, 0:1024])
                vector.tensor_copy(rstdb_sb, ps_bc[:, 1024:2048]).then_inc(vBc, 1)
                for ec in range(32):
                    vector.wait_ge(dS1r[ec % 2], 16 * (ec // 2 + 1))
                    if ec >= 2:
                        vector.wait_ge(dAgx[ec % 2], 16 * (ec // 2))
                    vector.tensor_sub(s1r_sb[ec % 2], s1r_sb[ec % 2], mub_sb)
                    vector.tensor_mul(xfb_st[ec % 2], s1r_sb[ec % 2], rstdb_sb).then_inc(vXf, 1)

                if upto < 6:
                    return
                for (ktn, mat, kt, rows, nwid, ngr, klen, chk, wid) in DQU:
                    u = UOFF[ktn] + chk
                    pss = ps_dq[u % 2]
                    if kt == 0 and chk == 0:
                        npk = NSLP if mat < 2 else D // 8
                        vector.wait_ge(dDq, 16 * 2 * (mat + 1))
                        vector.wait_ge(dSc, 16 * (mat + 1))
                        if mat >= 1:
                            vector.wait_ge(tOh, OHOFF[mat])
                            vector.wait_ge(tG, UMAT[mat])
                        vector.tensor_copy(gi_f[:, 0:klen], gi_i[:, 0:klen])
                        for jjj in range(8):
                            vector.tensor_scalar(z_sb[0:ngr, jjj:nwid:8],
                                                 qz_sb[0:ngr, 0:npk],
                                                 4 * jjj, 15,
                                                 ALU.logical_shift_right, ALU.bitwise_and)
                        vector.tensor_copy(w4f_sb[0:ngr, 0:nwid], z_sb[0:ngr, 0:nwid])
                        vector.tensor_scalar(zp1_sb[0:ngr, 0:nwid], w4f_sb[0:ngr, 0:nwid],
                                             1.0, None, ALU.add).then_inc(vOh, 1)
                        for ochk in range(OHC[mat]):
                            owid = min(512, klen - ochk * 512)
                            vector.wait_ge(tOh, OHOFF[mat] + ochk + 1)
                            vector.tensor_scalar(
                                oh_sb[0:ngr, ochk * 512:ochk * 512 + owid],
                                ps_ohb[0:ngr, 0:owid],
                                iota32_sb[0:ngr, 0:1] if mat < 2 else iota86_sb[0:ngr, 0:1],
                                None, ALU.is_equal).then_inc(vOh2, 1)
                    if chk == 0:
                        vector.wait_ge(dQw[ktn % 2], 16 * (ktn // 2 + 1))
                        if ktn >= 2:
                            vector.wait_ge(dWbf[ktn % 2], 16 * (ktn // 2))
                        vector.tensor_scalar(qw_sb[ktn % 2][0:rows, 0:nwid],
                                             qw_sb[ktn % 2][0:rows, 0:nwid],
                                             shifts_sb[0:rows, 0:1], 15,
                                             ALU.logical_shift_right, ALU.bitwise_and)
                        vector.tensor_copy(w4f_sb[0:rows, 0:nwid],
                                           qw_sb[ktn % 2][0:rows, 0:nwid])
                    vector.wait_ge(tG, u + 1)
                    csl = slice(chk * 512, chk * 512 + wid)
                    vector.tensor_sub(tdq_sb[0:rows, 0:wid],
                                      w4f_sb[0:rows, csl], pss[1][0:rows, 0:wid])
                    vector.tensor_mul(wst_sb[ktn % 2][0:rows, csl],
                                      tdq_sb[0:rows, 0:wid],
                                      pss[0][0:rows, 0:wid]).then_inc(vDq, 1)

                if upto < 7:
                    return
                for tc in range(16):
                    vector.wait_ge(dHgr[tc % 2], 16 * (tc // 2 + 1))
                    for nt in range(NKT):
                        rows = 128 if nt < 10 else 96
                        it = NKT * tc + nt
                        vector.wait_ge(tH2, it + 1)
                        vector.tensor_mul(hgr_sb[tc % 2][0:rows, nt, :],
                                          ps_h[it % 4][0:rows, :],
                                          hgr_sb[tc % 2][0:rows, nt, :]).then_inc(vH2, 1)
                if upto < 8:
                    return
                i3 = 0
                for j in range(4):
                    for tc in range(16):
                        if i3 >= 2:
                            pj, pt = divmod(i3 - 2, 16)
                            vector.wait_ge(dRs4[pj], 16 * (pt + 1))
                        for ecc in range(8):
                            ie = 8 * i3 + ecc
                            vector.wait_ge(tY3, ie + 1)
                            vector.tensor_copy(yst_sb[i3 % 2][:, ecc, :],
                                               ps_y3[ecc % 2]).then_inc(vE3, 1)
                        i3 += 1

                if upto < 9:
                    return
                for tt in range(NTT):
                    vector.wait_ge(dF1[tt % 2], 16 * (tt // 2 + 1))
                    vector.wait_ge(dYs[tt % 2], 64 * (tt // 2 + 1))
                    tsl = slice(tt * 128, (tt + 1) * 128)
                    for ec in range(32):
                        vector.tensor_sub(s2_sb[:, ec, :], s1f_sb[tt % 2][:, ec, :],
                                          mub_sb[:, tsl]).then_inc(vCh, 1)
                        vector.wait_ge(vCh, vch[0] + 1)
                        vch[0] += 1
                        vector.tensor_mul(s2_sb[:, ec, :], s2_sb[:, ec, :],
                                          rstdb_sb[:, tsl]).then_inc(vCh, 1)
                        vector.wait_ge(vCh, vch[0] + 1)
                        vch[0] += 1
                        last = vector.tensor_add(s2_sb[:, ec, :], s2_sb[:, ec, :],
                                          ysl_sb[tt % 2][:, ec, :])
                    last.then_inc(vF, 1)
                    vector.wait_ge(sTe, 32 * (tt + 1))
                    vector.tensor_mul(t3_sb, tmaj_sb[tt % 2], tmaj_sb[tt % 2])
                    vector.reduce_sum(red_sb[:, 0:1], tmaj_sb[tt % 2],
                                      mybir.AxisListType.X)
                    vector.reduce_sum(red_sb[:, 1:2], t3_sb,
                                      mybir.AxisListType.X).then_inc(vCh, 1)
                    vector.wait_ge(vCh, vch[0] + 1)
                    vch[0] += 1
                    vector.tensor_scalar(red_sb[:, 2:3], red_sb[:, 0:1],
                                         -1.0 / D, None, ALU.mult)
                    vector.tensor_scalar(red_sb[:, 4:5], red_sb[:, 1:2],
                                         1.0 / D, None, ALU.mult).then_inc(vCh, 1)
                    vector.wait_ge(vCh, vch[0] + 1)
                    vch[0] += 1
                    vector.tensor_mul(red_sb[:, 3:4], red_sb[:, 2:3],
                                      red_sb[:, 2:3]).then_inc(vCh, 1)
                    vector.wait_ge(vCh, vch[0] + 1)
                    vch[0] += 1
                    vector.tensor_sub(red_sb[:, 5:6], red_sb[:, 4:5],
                                      red_sb[:, 3:4]).then_inc(vLn, 1)
                    vector.wait_ge(sLn, tt + 1)
                    if tt >= 2:
                        vector.wait_ge(dOut[tt % 2], 16 * (tt // 2))
                    vector.tensor_scalar(out_st[tt % 2], tmaj_sb[tt % 2],
                                         red_sb[:, 2:3], red_sb[:, 3:4],
                                         ALU.add, ALU.mult).then_inc(vLn2, 1)

            @block.scalar
            def _(scalar):
                scalar.wait_ge(dPre, 16 * NPRE)
                for c in range(SC if upto >= 2 else 0):
                    scalar.wait_ge(tV, c + 1)
                    if c >= 2:
                        scalar.wait_ge(dSpV[c % 2], 64 * (c // 2))
                    scalar.copy(vst_sb[c % 2], ps_q).then_inc(sEv, 1)

                if upto < 3:
                    return
                for ci in range(NCH):
                    for g in range(32):
                        gg = 32 * ci + g
                        scalar.wait_ge(tS, gg + 1)
                        if gg >= 8:
                            scalar.wait_ge(tPv, gg - 7)
                            scalar.wait_ge(vZ, gg - 7)
                        scalar.activation(pring[:, g % 8, :], ps_s[g % 3], AF.Exp,
                                          bias=cv[:, C_EXPB:C_EXPB + 1],
                                          scale=cv[:, C_QKSC:C_QKSC + 1]).then_inc(sExp, 1)

                if upto < 4:
                    return
                scalar.wait_ge(vSt, 1)
                scalar.activation(var_sb, var2_sb, AF.Ln,
                                  bias=cv[0:1, C_EPS:C_EPS + 1]).then_inc(sCh, 1)
                scalar.wait_ge(sCh, sch[0] + 1)
                sch[0] += 1
                scalar.activation(rstd_sb, var_sb, AF.Exp,
                                  scale=cv[0:1, C_NHALF:C_NHALF + 1]).then_inc(sRstd, 1)

                if upto < 7:
                    return
                for tc in range(16):
                    for nt in range(NKT):
                        rows = 128 if nt < 10 else 96
                        it = NKT * tc + nt
                        scalar.wait_ge(tH, it + 1)
                        if tc >= 2 and nt == 0:
                            scalar.wait_ge(dHg[tc % 2], 16 * (tc // 2))
                        scalar.activation(hst_sb[tc % 2][0:rows, nt, :],
                                          ps_h[it % 4][0:rows, :],
                                          AF.Silu).then_inc(sSilu, 1)

                if upto < 9:
                    return
                for tt in range(NTT):
                    for ec in range(32):
                        scalar.wait_ge(tTr, 32 * tt + ec + 1)
                        scalar.copy(tmaj_sb[tt % 2][:, ec * 128:(ec + 1) * 128],
                                    ps_tr[ec % 2]).then_inc(sTe, 1)
                    scalar.wait_ge(vLn, tt + 1)
                    scalar.activation(red_sb[:, 4:5], red_sb[:, 5:6], AF.Ln,
                                      bias=cv[:, C_EPS:C_EPS + 1]).then_inc(sCh, 1)
                    scalar.wait_ge(sCh, sch[0] + 1)
                    sch[0] += 1
                    scalar.activation(red_sb[:, 3:4], red_sb[:, 4:5], AF.Exp,
                                      scale=cv[:, C_NHALF:C_NHALF + 1]).then_inc(sLn, 1)
                    scalar.wait_ge(vLn2, tt + 1)
                    scalar.dma_start(out=y_out[tt * 128:(tt + 1) * 128, :],
                                     in_=out_st[tt % 2]).then_inc(dOut[tt % 2], 16)
                scalar.wait_ge(dOut[0], 16 * NTT // 2)
                scalar.wait_ge(dOut[1], 16 * NTT // 2)

    return nc


def _prep_inputs(inputs):
    bf = ml_dtypes.bfloat16
    x = np.asarray(inputs["L0_attn_x"])
    Wq = np.asarray(inputs["L0_attn_Wq"])
    Wk = np.asarray(inputs["L0_attn_Wk"])
    Wv = np.asarray(inputs["L0_attn_Wv"])
    Wo = np.asarray(inputs["L0_attn_Wo"])

    x0 = np.ascontiguousarray(x.reshape(S, D))
    xh = np.ascontiguousarray(x0.T.reshape(32, 128, S).transpose(1, 0, 2))

    def tiled_T(Wm):
        return np.ascontiguousarray(Wm.T.reshape(32, 128, D).transpose(1, 0, 2))

    wqT, wkT, wvT, woT = map(tiled_T, (Wq, Wk, Wv, Wo))

    inv = 1.0 / (10000.0 ** (np.arange(0, HD, 2, dtype=np.float64) / HD))
    fr = inv[:, None] * np.arange(S, dtype=np.float64)[None, :]
    cos1 = np.cos(fr).astype(np.float32)
    sin1 = np.sin(fr).astype(np.float32)
    cos2 = np.ascontiguousarray(np.concatenate([cos1, cos1], axis=0))
    sin2 = np.ascontiguousarray(np.concatenate([sin1, sin1], axis=0))

    cvec = np.zeros((128, NCONST), dtype=np.float32)
    cvec[:, C_EXPB] = EXP_BIAS
    cvec[:, C_QKSC] = QK_SCALE
    cvec[:, C_EPS] = EPS
    cvec[:, C_NHALF] = -0.5

    shared = {
        "xh": xh, "cos2": cos2, "sin2": sin2,
        "ident": np.eye(128, dtype=np.float32),
        "ones_row": np.ones((1, 128), dtype=np.float32),
        "ones_col_f": np.ones((128, 1), dtype=np.float32),
        "ones_col_b": np.ones((128, 1), dtype=bf),
        "iota32": np.arange(32, dtype=np.float32).reshape(32, 1),
        "iota86": np.arange(86, dtype=np.float32).reshape(86, 1),
        "shifts": (4 * (np.arange(128) % 8)).astype(np.int32).reshape(128, 1),
        "cvec": cvec, "woT": woT,
        "g_gi": np.ascontiguousarray(np.asarray(inputs["L0_ffn_gate_g_idx"]).reshape(1, D)),
        "u_gi": np.ascontiguousarray(np.asarray(inputs["L0_ffn_up_g_idx"]).reshape(1, D)),
        "d_sc": np.ascontiguousarray(np.asarray(inputs["L0_ffn_down_scales"])),
        "d_qz": np.ascontiguousarray(np.asarray(inputs["L0_ffn_down_qzeros"])),
    }
    g_qw_full = np.asarray(inputs["L0_ffn_gate_qweight"])
    u_qw_full = np.asarray(inputs["L0_ffn_up_qweight"])
    d_qw_full = np.asarray(inputs["L0_ffn_down_qweight"])
    g_sc_full = np.asarray(inputs["L0_ffn_gate_scales"])
    u_sc_full = np.asarray(inputs["L0_ffn_up_scales"])
    g_qz_full = np.asarray(inputs["L0_ffn_gate_qzeros"])
    u_qz_full = np.asarray(inputs["L0_ffn_up_qzeros"])
    d_gi_full = np.asarray(inputs["L0_ffn_down_g_idx"])

    in_maps = []
    for c in range(W):
        esl = slice(EOWN * c, EOWN * (c + 1))
        nsl = slice(NSL * c, NSL * (c + 1))
        nslp = slice(NSLP * c, NSLP * (c + 1))
        m = dict(shared)
        m["x_own"] = np.ascontiguousarray(xh[:, :, TOK * c:TOK * (c + 1)])
        m["wqT"] = np.ascontiguousarray(wqT[:, :, esl])
        m["wkT"] = np.ascontiguousarray(wkT[:, :, esl])
        m["wvT"] = np.ascontiguousarray(wvT[:, :, esl])
        m["g_qw"] = np.ascontiguousarray(np.repeat(g_qw_full[:, nsl], 8, axis=0))
        m["u_qw"] = np.ascontiguousarray(np.repeat(u_qw_full[:, nsl], 8, axis=0))
        m["d_qw"] = np.ascontiguousarray(
            np.repeat(d_qw_full[NSLP * c:NSLP * (c + 1), :], 8, axis=0))
        m["g_sc"] = np.ascontiguousarray(g_sc_full[:, nsl])
        m["u_sc"] = np.ascontiguousarray(u_sc_full[:, nsl])
        m["g_qz"] = np.ascontiguousarray(g_qz_full[:, nslp])
        m["u_qz"] = np.ascontiguousarray(u_qz_full[:, nslp])
        m["d_gi"] = np.ascontiguousarray(d_gi_full[nsl].reshape(1, NSL))
        in_maps.append(m)
    return in_maps


def kernel(**inputs):
    in_maps = _prep_inputs(inputs)
    nc = build_nc()
    res = run_bass_kernel_spmd(nc, in_maps, core_ids=list(range(W)))
    out = np.concatenate([res.results[c]["y"] for c in range(W)], axis=0)
    return out.reshape(1, S, D).astype(np.float32)


# version 06
# speedup vs baseline: 1.1408x; 1.1221x over previous
import os
import contextlib
import numpy as np
import ml_dtypes

os.environ.setdefault("TRN_ALLOW_ICI_ALL_TO_ALL", "1")

import concourse.bass as bass
import concourse.mybir as mybir
from concourse.bass_utils import run_bass_kernel_spmd

F32 = mybir.dt.float32
BF16 = mybir.dt.bfloat16
I32 = mybir.dt.int32
AF = mybir.ActivationFunctionType
ALU = mybir.AluOpType

W = 8
S = 8192
D = 4096
HD = 128
HPC = 4
EOWN = HPC * HD
TOK = S // W
INTER = 11008
NSL = INTER // W
NSLP = NSL // 8
NG = 32
NGD = 86
NKT = 11
SC = 16
QC = 16
NCH = HPC * QC
NTT = TOK // 128

EXP_BIAS = -12.0
QK_SCALE = 1.0 / float(np.sqrt(HD))
EPS = 1e-5

C_EXPB, C_QKSC, C_EPS, C_NHALF = range(4)
NCONST = 4
RG = [list(range(W))]

DQT = []
for _mat in range(3):
    _nkt = 32 if _mat < 2 else NKT
    for _kt in range(_nkt):
        _rows = 96 if (_mat == 2 and _kt == NKT - 1) else 128
        DQT.append((_mat, _kt, _rows,
                    NSL if _mat < 2 else D,
                    NG if _mat < 2 else NGD,
                    D if _mat < 2 else NSL))
NDQ = len(DQT)
OHC = [(D + 511) // 512, (D + 511) // 512, (NSL + 511) // 512]
OHOFF = [0, 8, 16]
DQOFF = [0, 32, 64]
DQU = []
UOFF = []
for _ktn, (_mat, _kt, _rows, _nwid, _ngr, _klen) in enumerate(DQT):
    UOFF.append(len(DQU))
    for _chk in range(-(-_nwid // 512)):
        _wid = min(512, _nwid - _chk * 512)
        DQU.append((_ktn, _mat, _kt, _rows, _nwid, _ngr, _klen, _chk, _wid))
NDQU = len(DQU)
UOFF.append(NDQU)
UMAT = [0, UOFF[32], UOFF[64]]


def _ceil(a, b):
    return -(-a // b)


def build_nc(upto=9, dbg=()):
    nc = bass.Bass("TRN2", target_bir_lowering=False, num_devices=W)

    di = lambda n, s, d: nc.dram_tensor(n, s, d, kind="ExternalInput").ap()
    xh = di("xh", [128, 32, S], F32)
    x_own = di("x_own", [128, 32, TOK], F32)
    wqT = di("wqT", [128, 32, EOWN], F32)
    wkT = di("wkT", [128, 32, EOWN], F32)
    wvT = di("wvT", [128, 32, EOWN], F32)
    woT = di("woT", [128, 32, D], F32)
    cos2 = di("cos2", [128, S], F32)
    sin2 = di("sin2", [128, S], F32)
    ident = di("ident", [128, 128], F32)
    ones_row = di("ones_row", [1, 128], F32)
    ones_col_f = di("ones_col_f", [128, 1], F32)
    ones_col_b = di("ones_col_b", [128, 1], BF16)
    iota32 = di("iota32", [32, 1], F32)
    iota86 = di("iota86", [86, 1], F32)
    shifts = di("shifts", [128, 1], I32)
    cvec = di("cvec", [128, NCONST], F32)
    g_qw = di("g_qw", [D, NSL], I32)
    u_qw = di("u_qw", [D, NSL], I32)
    d_qw = di("d_qw", [NSL, D], I32)
    g_sc = di("g_sc", [NG, NSL], F32)
    u_sc = di("u_sc", [NG, NSL], F32)
    d_sc = di("d_sc", [NGD, D], F32)
    g_qz = di("g_qz", [NG, NSLP], I32)
    u_qz = di("u_qz", [NG, NSLP], I32)
    d_qz = di("d_qz", [NGD, D // 8], I32)
    g_gi = di("g_gi", [1, D], I32)
    u_gi = di("u_gi", [1, D], I32)
    d_gi = di("d_gi", [1, NSL], I32)
    y_out = nc.dram_tensor("y", [TOK, D], F32, kind="ExternalOutput").ap()

    dt = lambda n, s, d: nc.dram_tensor(
        n, s, d, kind=("ExternalOutput" if n in dbg else "Internal")).ap()
    q_d = dt("q_d", [HPC, 128, S], BF16)
    k_d = dt("k_d", [HPC, 128, S], BF16)
    v_d = dt("v_d", [HPC, 128, S // 128, HD], BF16)
    a2a_in = [dt(f"a2a_in{h}", [W, 128, TOK], BF16) for h in range(HPC)]
    a2a_out = [dt(f"a2a_out{h}", [W, 128, TOK], BF16) for h in range(HPC)]
    s1_d = dt("s1_d", [32, 128, TOK], F32)
    agx_in = dt("agx_in", [32, 128, TOK], BF16)
    agx_out = [nc.dram_tensor(f"agx_out{i}", [W, 8, 128, TOK], BF16,
                              kind="Internal", addr_space="Shared").ap()
               for i in range(4)]
    wg_d = dt("wg_d", [32, 128, NSL], BF16)
    wu_d = dt("wu_d", [32, 128, NSL], BF16)
    wd_d = dt("wd_d", [NKT, 128, D], BF16)
    hg_d = dt("hg_d", [NKT, 128, S], BF16)
    h_d = dt("h_d", [NKT, 128, S], BF16)
    rs_in = [dt(f"rs_in{j}", [W, 1024, TOK], BF16) for j in range(4)]
    rs_out = [dt(f"rs_out{j}", [1024, TOK], BF16) for j in range(4)]
    if "rs_dbg" in dbg:
        rsi_dbg = nc.dram_tensor("rsi_dbg", [W, 1024, TOK], BF16, kind="ExternalOutput").ap()
        rso_dbg = nc.dram_tensor("rso_dbg", [1024, TOK], BF16, kind="ExternalOutput").ap()
    WMAT = [wg_d, wu_d, wd_d]

    es = contextlib.ExitStack()
    mk = lambda n: es.enter_context(nc.semaphore(n))
    mk2 = lambda n: [mk(n + "0"), mk(n + "1")]
    (dPre, tQ, tK, vRope, tV, sEv, tS, sExp, vZ, tPv,
     vZf, tZ, vRec, tZb, sZb, vO, cA2A, dCtxIn, tY, vS1,
     tSt, vSt, sRstd, tBc, vBc, vXf, cAGX, dDq, dSc, tOh, vOh,
     vOh2, tG, vDq, dWres, tH, sSilu, tH2, vH2,
     tY3, vE3, cRS, vF, tTr, sTe, vLn, sLn, vLn2) = [mk(f"s{i}") for i in range(48)]
    dX = mk2("dX")
    dXv = mk2("dXv")
    dCS = mk2("dCS")
    dSpQ = mk2("dSpQ")
    dSpK = mk2("dSpK")
    dSpV = mk2("dSpV")
    dKV = mk2("dKV")
    dCtx = mk2("dCtx")
    dWo = mk2("dWo")
    dXo = mk2("dXo")
    dS1 = mk2("dS1")
    dS1r = mk2("dS1r")
    dAgx = mk2("dAgx")
    dQw = mk2("dQw")
    dWbf = mk2("dWbf")
    dXf = mk2("dXf")
    dHg = mk2("dHg")
    dHgr = mk2("dHgr")
    dH2 = mk2("dH2")
    dH3 = mk2("dH3")
    dRs8 = [[mk(f"dRs{j}_{p}") for p in range(2)] for j in range(4)]
    dF1 = dCS
    dYs = dKV
    dOut = dWo
    vCh = mk("vCh")
    sCh = mk("sCh")

    AL = nc.alloc_sbuf_tensor
    ident_sb = AL("ident_sb", [128, 128], F32).ap()
    ones_row_sb = AL("ones_row_sb", [1, 128], F32).ap()
    ones_col_f_sb = AL("ones_col_f_sb", [128, 1], F32).ap()
    ones_col_b_sb = AL("ones_col_b_sb", [128, 1], BF16).ap()
    iota32_sb = AL("iota32_sb", [32, 1], F32).ap()
    iota86_sb = AL("iota86_sb", [86, 1], F32).ap()
    shifts_sb = AL("shifts_sb", [128, 1], I32).ap()
    cv = AL("cv_sb", [128, NCONST], F32).ap()
    mub_sb = AL("mub_sb", [128, TOK], F32).ap()
    rstdb_sb = AL("rstdb_sb", [128, TOK], F32).ap()
    red_sb = AL("red_sb", [128, 6], F32).ap()

    ABASE = nc.sbuf_base
    ATOP = nc.sbuf_top

    class Ph:
        def __init__(self):
            self.off = ABASE

        def t(self, name, shape, dtype):
            per = int(np.prod(shape[1:])) * np.dtype(mybir.dt.np(dtype)).itemsize
            off = (self.off + 31) & ~31
            h = nc.alloc_sbuf_tensor_at(name, shape, dtype, offset=off)
            self.off = off + per
            assert self.off <= ATOP, f"{name}: arena overflow {self.off} > {ATOP}"
            return h.ap()

    ph = Ph()
    wqT_sb = ph.t("wqT_sb", [128, 32, EOWN], BF16)
    wkT_sb = ph.t("wkT_sb", [128, 32, EOWN], BF16)
    xt_sb = [ph.t(f"xt{i}", [128, 32, 512], BF16) for i in range(2)]
    cos_sb = [ph.t(f"cos{i}", [128, 512], F32) for i in range(2)]
    sin_sb = [ph.t(f"sin{i}", [128, 512], F32) for i in range(2)]
    uv_sb = ph.t("uv_sb", [128, 1024], F32)
    qst_sb = [ph.t(f"qst{i}", [128, HPC, 512], BF16) for i in range(2)]
    kst_sb = [ph.t(f"kst{i}", [128, HPC, 512], BF16) for i in range(2)]
    wvT_sb = ph.t("wvT_sb", [128, 32, EOWN], BF16)
    vst_sb = [ph.t(f"vst{i}", [128, 4 * EOWN], BF16) for i in range(2)]

    pa = Ph()
    k_sb = [pa.t(f"k_sb{i}", [128, S], BF16) for i in range(2)]
    v_sb = [pa.t(f"v_sb{i}", [128, S // 128, HD], BF16) for i in range(2)]
    q_sb = [pa.t(f"q_sb{i}", [128, S], BF16) for i in range(2)]
    pring = pa.t("pring", [128, 8, 1024], BF16)
    accb = pa.t("accb", [128, 1024], BF16)
    acc32 = pa.t("acc32", [128, 1024], F32)
    accf = pa.t("accf", [128, 512], F32)
    recip_sb = pa.t("recip_sb", [1, 512], F32)
    zb_sb = pa.t("zb_sb", [128, 512], F32)
    ctxst = [pa.t(f"ctxst{i}", [128, 512], BF16) for i in range(2)]

    po = Ph()
    ctx_sb = po.t("ctx_sb", [128, 32, TOK], BF16)
    woT_sb = [po.t(f"woT{i}", [128, 32, 128], BF16) for i in range(2)]
    xo_sb = [po.t(f"xo{i}", [128, TOK], F32) for i in range(2)]
    s1_st = [po.t(f"s1st{i}", [128, TOK], F32) for i in range(2)]
    s1b_sb = po.t("s1b_sb", [128, TOK], BF16)
    sqb_sb = po.t("sqb_sb", [128, TOK], BF16)
    mu_sb = po.t("mu_sb", [1, TOK], F32)
    msq_sb = po.t("msq_sb", [1, TOK], F32)
    var_sb = po.t("var_sb", [1, TOK], F32)
    var2_sb = po.t("var2_sb", [1, TOK], F32)
    rstd_sb = po.t("rstd_sb", [1, TOK], F32)
    s1r_sb = [po.t(f"s1r{i}", [128, TOK], F32) for i in range(2)]
    xfb_st = [po.t(f"xfb{i}", [128, TOK], BF16) for i in range(2)]

    pd = Ph()
    gi_i = pd.t("gi_i", [1, D], I32)
    gi_f = pd.t("gi_f", [1, D], F32)
    oh_sb = pd.t("oh_sb", [86, D], BF16)
    z_sb = pd.t("z_sb", [86, D], I32)
    zp1_sb = pd.t("zp1_sb", [86, D], BF16)
    sc_sb = pd.t("sc_sb", [86, D], BF16)
    qz_sb = pd.t("qz_sb", [86, 512], I32)
    qw_sb = [pd.t(f"qw{i}", [128, D], I32) for i in range(2)]
    w4f_sb = pd.t("w4f_sb", [128, D], F32)
    tdq_sb = pd.t("tdq_sb", [128, 512], F32)
    wst_sb = [pd.t(f"wst{i}", [128, D], BF16) for i in range(2)]

    pf = Ph()
    wres_raw = pf.t("wres_sb", [128, NKT * D], BF16)
    wres_gu = wres_raw[:, 0:32 * NSL].rearrange("p (k n) -> p k n", n=NSL)
    wres_d = wres_raw.rearrange("p (k n) -> p k n", n=D)
    xf_off = pf.off
    xf_sb = [pf.t(f"xf{i}", [128, 32, 512], BF16) for i in range(2)]
    hst_sb = [pf.t(f"hst{i}", [128, NKT, 512], BF16) for i in range(2)]
    hgr_sb = [pf.t(f"hgr{i}", [128, NKT, 512], BF16) for i in range(2)]
    yst_sb = [nc.alloc_sbuf_tensor_at(
        f"yst{i}", [128, 8, 512], BF16,
        offset=((xf_off + 31) & ~31) + i * 8 * 512 * 2).ap() for i in range(2)]

    pz = Ph()
    s1f_sb = [pz.t(f"s1f{i}", [128, 32, 128], F32) for i in range(2)]
    ysl_sb = [pz.t(f"ysl{i}", [128, 32, 128], F32) for i in range(2)]
    s2_sb = pz.t("s2_sb", [128, 32, 128], F32)
    tmaj_sb = [pz.t(f"tmaj{i}", [128, D], F32) for i in range(2)]
    t3_sb = pz.t("t3_sb", [128, D], F32)
    out_st = [pz.t(f"outst{i}", [128, D], F32) for i in range(2)]

    ps_a = nc.alloc_psum_tensor("ps_a", [128, 2048], F32).ap()
    ps_b = nc.alloc_psum_tensor("ps_b", [128, 2048], F32).ap()
    ps_s = [ps_a[:, 0:1024], ps_a[:, 1024:2048], ps_b[:, 0:1024]]
    ps_o = ps_b[:, 1024:1536]
    ps_zzb = ps_b[:, 1536:2048]
    ps_q = ps_a
    ps_k = ps_b
    ps_y = [ps_a[:, 0:1024], ps_a[:, 1024:2048]]
    ps_mu = ps_b[0:1, 0:1024]
    ps_sq = ps_b[0:1, 1024:2048]
    ps_bc = ps_a
    ps_dq = [[ps_a[:, 0:512], ps_a[:, 512:1024]],
             [ps_a[:, 1024:1536], ps_a[:, 1536:2048]]]
    ps_ohb = ps_b[0:86, 0:512]
    ps_h = [ps_a[:, 0:512], ps_a[:, 512:1024], ps_a[:, 1024:1536], ps_a[:, 1536:2048]]
    ps_y3 = [ps_b[:, 0:512], ps_b[:, 512:1024]]
    ps_tr = [ps_b[:, 0:128], ps_b[:, 512:640]]

    NPRE = 10
    vch = [0]
    sch = [0]

    with es:
        with nc.Block() as block:

            @block.sync
            def _(sync):
                sync.dma_start(out=ident_sb, in_=ident).then_inc(dPre, 16)
                sync.dma_start(out=ones_row_sb, in_=ones_row).then_inc(dPre, 16)
                sync.dma_start(out=ones_col_f_sb, in_=ones_col_f).then_inc(dPre, 16)
                sync.dma_start(out=ones_col_b_sb, in_=ones_col_b).then_inc(dPre, 16)
                sync.dma_start(out=iota32_sb, in_=iota32).then_inc(dPre, 16)
                sync.dma_start(out=iota86_sb, in_=iota86).then_inc(dPre, 16)
                sync.dma_start(out=shifts_sb, in_=shifts).then_inc(dPre, 16)
                sync.dma_start(out=cv, in_=cvec).then_inc(dPre, 16)

                for c in range(SC if upto >= 1 else 0):
                    sl = slice(c * 512, (c + 1) * 512)
                    if c >= 2:
                        sync.wait_ge(vRope, 2 * (c - 1))
                    sync.dma_start(out=cos_sb[c % 2], in_=cos2[:, sl]).then_inc(dCS[c % 2], 16)
                    sync.dma_start(out=sin_sb[c % 2], in_=sin2[:, sl]).then_inc(dCS[c % 2], 16)
                    sync.wait_ge(vRope, 2 * c + 1)
                    sync.dma_start(out=q_d[:, :, sl].rearrange("h p s -> p h s"),
                                   in_=qst_sb[c % 2]).then_inc(dSpQ[c % 2], 16)
                    sync.wait_ge(vRope, 2 * c + 2)
                    sync.dma_start(out=k_d[:, :, sl].rearrange("h p s -> p h s"),
                                   in_=kst_sb[c % 2]).then_inc(dSpK[c % 2], 16)
                for c in range(SC if upto >= 2 else 0):
                    sl = slice(c * 512, (c + 1) * 512)
                    sync.wait_ge(sEv, c + 1)
                    vv = vst_sb[c % 2].rearrange("p (t h d) -> p t h d", t=4, h=HPC)
                    for hh in range(HPC):
                        sync.dma_start(
                            out=v_d[hh, :, 4 * c:4 * c + 4, :].rearrange("p t d -> p t d"),
                            in_=vv[:, :, hh, :]).then_inc(dSpV[c % 2], 16)

                for h in range(HPC if upto >= 3 else 0):
                    if h == 0:
                        for sm_ in (dSpQ[0], dSpQ[1], dSpK[0], dSpK[1]):
                            sync.wait_ge(sm_, 16 * 8)
                        sync.wait_ge(dSpV[0], 64 * 8)
                        sync.wait_ge(dSpV[1], 64 * 8)
                    elif h >= 2:
                        sync.wait_ge(tPv, 32 * QC * (h - 1))
                    sync.dma_start(out=k_sb[h % 2], in_=k_d[h]).then_inc(dKV[h % 2], 16)
                    sync.dma_start(out=v_sb[h % 2], in_=v_d[h]).then_inc(dKV[h % 2], 16)
                    sync.dma_start(out=q_sb[h % 2], in_=q_d[h]).then_inc(dKV[h % 2], 16)

                if upto < 4:
                    return
                sync.wait_ge(cA2A, HPC)
                for h in range(HPC):
                    sync.dma_start(out=ctx_sb[:, h::HPC, :],
                                   in_=a2a_out[h].rearrange("r p t -> p r t")
                                   ).then_inc(dCtxIn, 16)
                for ec in range(32):
                    if ec >= 2:
                        sync.wait_ge(vS1, ec - 1)
                    sync.dma_start(out=xo_sb[ec % 2], in_=x_own[:, ec, :]).then_inc(dXo[ec % 2], 16)
                    sync.wait_ge(vS1, ec + 1)
                    sync.dma_start(out=s1_d[ec], in_=s1_st[ec % 2]).then_inc(dS1[ec % 2], 16)
                for ec in range(32):
                    if ec >= 2:
                        sync.wait_ge(vXf, ec - 1)
                    sync.dma_start(out=s1r_sb[ec % 2], in_=s1_d[ec]).then_inc(dS1r[ec % 2], 16)
                    sync.wait_ge(vXf, ec + 1)
                    sync.dma_start(out=agx_in[ec], in_=xfb_st[ec % 2]).then_inc(dAgx[ec % 2], 16)

                if upto < 6:
                    return
                sync.wait_ge(vXf, 32)
                sync.dma_start(out=qz_sb[0:NG, 0:NSLP], in_=g_qz).then_inc(dDq, 16)
                sync.dma_start(out=gi_i, in_=g_gi).then_inc(dDq, 16)
                for ktn, (mat, kt, rows, nwid, ngr, klen) in enumerate(DQT):
                    if ktn >= 2:
                        sync.wait_ge(vDq, UOFF[ktn - 1])
                    sync.dma_start(
                        out=qw_sb[ktn % 2][0:rows, 0:nwid],
                        in_=(g_qw, u_qw, d_qw)[mat][kt * 128: kt * 128 + rows, :]
                    ).then_inc(dQw[ktn % 2], 16)
                    if ktn >= 1:
                        pm, pk, prow, pnw, _, _ = DQT[ktn - 1]
                        sync.wait_ge(vDq, UOFF[ktn])
                        sync.dma_start(
                            out=WMAT[pm][pk][0:prow, :],
                            in_=wst_sb[(ktn - 1) % 2][0:prow, 0:pnw]).then_inc(dWbf[(ktn - 1) % 2], 16)
                    if mat < 2 and kt == 31:
                        sync.wait_ge(vOh, mat + 1)
                        sync.wait_ge(tOh, OHOFF[mat + 1])
                        nxt_qz, nxt_gi = (u_qz, u_gi) if mat == 0 else (d_qz, d_gi)
                        sync.dma_start(out=qz_sb[0:(NG if mat == 0 else NGD),
                                                 0:(NSLP if mat == 0 else D // 8)],
                                       in_=nxt_qz).then_inc(dDq, 16)
                        sync.dma_start(out=gi_i[:, 0:(D if mat == 0 else NSL)],
                                       in_=nxt_gi).then_inc(dDq, 16)
                pm, pk, prow, pnw, _, _ = DQT[NDQ - 1]
                sync.wait_ge(vDq, NDQU)
                sync.dma_start(out=WMAT[pm][pk][0:prow, :],
                               in_=wst_sb[(NDQ - 1) % 2][0:prow, 0:pnw]).then_inc(dWbf[(NDQ - 1) % 2], 16)

                if upto < 7:
                    return
                sync.wait_ge(dWbf[0], 16 * 38)
                sync.wait_ge(dWbf[1], 16 * 37)
                sync.dma_start(out=wres_gu,
                               in_=wg_d.rearrange("k p n -> p k n")).then_inc(dWres, 16)
                for tc in range(16):
                    r, hf = tc // 2, tc % 2
                    if tc == 0:
                        sync.wait_ge(cAGX, 4)
                    if tc >= 2:
                        sync.wait_ge(tH, NKT * (tc - 1))
                    for i in range(4):
                        sync.dma_start(
                            out=xf_sb[tc % 2][:, 8 * i:8 * (i + 1), :],
                            in_=agx_out[i][r, :, :, hf * 512:(hf + 1) * 512].rearrange(
                                "m p t -> p m t")).then_inc(dXf[tc % 2], 16)
                    sync.wait_ge(sSilu, NKT * (tc + 1))
                    sync.dma_start(
                        out=hg_d[:, :, tc * 512:(tc + 1) * 512].rearrange("n p t -> p n t"),
                        in_=hst_sb[tc % 2]).then_inc(dHg[tc % 2], 16)
                sync.wait_ge(tH, NKT * 16)
                sync.dma_start(out=wres_gu,
                               in_=wu_d.rearrange("k p n -> p k n")).then_inc(dWres, 16)
                for tc in range(16):
                    r, hf = tc // 2, tc % 2
                    if tc >= 2:
                        sync.wait_ge(tH2, NKT * (tc - 1))
                        sync.wait_ge(dH2[tc % 2], 16 * (tc // 2))
                    for i in range(4):
                        sync.dma_start(
                            out=xf_sb[tc % 2][:, 8 * i:8 * (i + 1), :],
                            in_=agx_out[i][r, :, :, hf * 512:(hf + 1) * 512].rearrange(
                                "m p t -> p m t")).then_inc(dXf[tc % 2], 16)
                    sync.dma_start(
                        out=hgr_sb[tc % 2],
                        in_=hg_d[:, :, tc * 512:(tc + 1) * 512].rearrange(
                            "n p t -> p n t")).then_inc(dHgr[tc % 2], 16)
                    sync.wait_ge(vH2, NKT * (tc + 1))
                    sync.dma_start(
                        out=h_d[:, :, tc * 512:(tc + 1) * 512].rearrange("n p t -> p n t"),
                        in_=hgr_sb[tc % 2]).then_inc(dH2[tc % 2], 16)
                if upto < 8:
                    return
                sync.wait_ge(tH2, NKT * 16)
                sync.wait_ge(dH2[0], 16 * 8)
                sync.wait_ge(dH2[1], 16 * 8)
                sync.dma_start(out=wres_d,
                               in_=wd_d.rearrange("k p n -> p k n")).then_inc(dWres, 16)
                i3 = 0
                for j in range(4):
                    for tc in range(16):
                        r, hf = tc // 2, tc % 2
                        if i3 >= 2:
                            sync.wait_ge(tY3, 8 * (i3 - 1))
                        sync.dma_start(
                            out=hgr_sb[i3 % 2],
                            in_=h_d[:, :, tc * 512:(tc + 1) * 512].rearrange(
                                "n p t -> p n t")).then_inc(dH3[i3 % 2], 16)
                        i3 += 1

                if "rs_dbg" in dbg:
                    sync.wait_ge(dRs4[0], 16 * 16)
                    sync.dma_start(out=rsi_dbg, in_=rs_in[0]).then_inc(dPre, 16)
                    sync.wait_ge(cRS, 1)
                    sync.dma_start(out=rso_dbg, in_=rs_out[0]).then_inc(dPre, 16)
                if upto < 9:
                    return
                for tt in range(NTT):
                    tsl = slice(tt * 128, (tt + 1) * 128)
                    if tt == 0:
                        sync.wait_ge(tY3, 8 * 64)
                    if tt >= 2:
                        sync.wait_ge(vF, tt - 1)
                    sync.dma_start(
                        out=s1f_sb[tt % 2],
                        in_=s1_d[:, :, tsl].rearrange("e p t -> p e t")).then_inc(dF1[tt % 2], 16)

            @block.gpsimd
            def _(gpsimd):
                gpsimd.dma_start(out=wqT_sb, in_=wqT).then_inc(dPre, 16)
                gpsimd.dma_start(out=wkT_sb, in_=wkT).then_inc(dPre, 16)
                gpsimd.dma_start(out=wvT_sb, in_=wvT).then_inc(dPre, 16)
                for c in range(SC if upto >= 1 else 0):
                    if c >= 2:
                        gpsimd.wait_ge(tK, c - 1)
                    gpsimd.dma_start(out=xt_sb[c % 2],
                                     in_=xh[:, :, c * 512:(c + 1) * 512]).then_inc(dX[c % 2], 16)
                if upto < 2:
                    return
                gpsimd.wait_ge(tK, SC)
                for c in range(SC):
                    if c >= 2:
                        gpsimd.wait_ge(tV, c - 1)
                    gpsimd.dma_start(out=xt_sb[c % 2],
                                     in_=xh[:, :, c * 512:(c + 1) * 512]).then_inc(dXv[c % 2], 16)
                if upto < 3:
                    return
                for ci in range(NCH):
                    h, qc = divmod(ci, QC)
                    gpsimd.wait_ge(vO, ci + 1)
                    gpsimd.dma_start(
                        out=a2a_in[h][qc // 2, :, (qc % 2) * 512:((qc % 2) + 1) * 512],
                        in_=ctxst[ci % 2]).then_inc(dCtx[ci % 2], 16)
                    if qc == QC - 1 and upto >= 4:
                        gpsimd.wait_ge(dCtx[0], 16 * 8 * (h + 1))
                        gpsimd.wait_ge(dCtx[1], 16 * 8 * (h + 1))
                        gpsimd.collective_compute(
                            "AllToAll", ALU.bypass, replica_groups=RG,
                            ins=[a2a_in[h].opt()],
                            outs=[a2a_out[h].opt()]).then_inc(cA2A, 1)
                if upto < 4:
                    return
                for ec in range(32):
                    if ec >= 2:
                        gpsimd.wait_ge(tY, ec - 1)
                    gpsimd.dma_start(out=woT_sb[ec % 2],
                                     in_=woT[:, :, ec * 128:(ec + 1) * 128]).then_inc(dWo[ec % 2], 16)
                if upto < 5:
                    return
                for i in range(4):
                    gpsimd.wait_ge(dAgx[0], 16 * 4 * (i + 1))
                    gpsimd.wait_ge(dAgx[1], 16 * 4 * (i + 1))
                    gpsimd.collective_compute(
                        "AllGather", ALU.bypass, replica_groups=RG,
                        ins=[agx_in[8 * i:8 * (i + 1)].opt()],
                        outs=[agx_out[i].opt()]).then_inc(cAGX, 1)
                if upto < 6:
                    return
                gpsimd.dma_start(out=sc_sb[0:NG, 0:NSL], in_=g_sc).then_inc(dSc, 16)
                gpsimd.wait_ge(tG, UMAT[1])
                gpsimd.dma_start(out=sc_sb[0:NG, 0:NSL], in_=u_sc).then_inc(dSc, 16)
                gpsimd.wait_ge(tG, UMAT[2])
                gpsimd.dma_start(out=sc_sb[0:NGD, 0:D], in_=d_sc).then_inc(dSc, 16)
                if upto < 8:
                    return
                for j in range(4):
                    gpsimd.wait_ge(dRs8[j][0], 16 * 8)
                    gpsimd.wait_ge(dRs8[j][1], 16 * 8)
                    gpsimd.collective_compute(
                        "ReduceScatter", ALU.add, replica_groups=RG,
                        ins=[rs_in[j].opt()], outs=[rs_out[j].opt()]).then_inc(cRS, 1)
                if upto < 9:
                    return
                for tt in range(NTT):
                    tsl = slice(tt * 128, (tt + 1) * 128)
                    if tt == 0:
                        gpsimd.wait_ge(cRS, 4)
                    if tt >= 2:
                        gpsimd.wait_ge(vF, tt - 1)
                    for j in range(4):
                        gpsimd.dma_start(
                            out=ysl_sb[tt % 2][:, j * 8:(j + 1) * 8, :],
                            in_=rs_out[j][:, tsl].rearrange(
                                "(ee p) t -> p ee t", p=128)).then_inc(dYs[tt % 2], 16)

            @block.tensor
            def _(tensor):
                tensor.wait_ge(dPre, 16 * NPRE)
                for c in range(SC if upto >= 1 else 0):
                    tensor.wait_ge(dX[c % 2], 16 * (c // 2 + 1))
                    if c >= 1:
                        tensor.wait_ge(vRope, 2 * c - 1)
                    for et in range(HPC):
                        for m in range(32):
                            last = tensor.matmul(ps_q[:, et * 512:(et + 1) * 512],
                                          wqT_sb[:, m, et * 128:(et + 1) * 128],
                                          xt_sb[c % 2][:, m, :],
                                          start=(m == 0), stop=(m == 31))
                    last.then_inc(tQ, 1)
                    if c >= 1:
                        tensor.wait_ge(vRope, 2 * c)
                    for et in range(HPC):
                        for m in range(32):
                            last = tensor.matmul(ps_k[:, et * 512:(et + 1) * 512],
                                          wkT_sb[:, m, et * 128:(et + 1) * 128],
                                          xt_sb[c % 2][:, m, :],
                                          start=(m == 0), stop=(m == 31))
                    last.then_inc(tK, 1)
                if upto < 2:
                    return
                tensor.wait_ge(vRope, 2 * SC)
                for c in range(SC):
                    tensor.wait_ge(dXv[c % 2], 16 * (c // 2 + 1))
                    tensor.wait_ge(dPre, 16 * (NPRE + 1))
                    if c >= 1:
                        tensor.wait_ge(sEv, c)
                    for st in range(4):
                        for m in range(32):
                            last = tensor.matmul(ps_q[:, st * 512:(st + 1) * 512],
                                          xt_sb[c % 2][:, m, st * 128:(st + 1) * 128],
                                          wvT_sb[:, m, :],
                                          start=(m == 0), stop=(m == 31))
                    last.then_inc(tV, 1)

                if upto < 3:
                    return
                def z_mm(ci_):
                    tensor.wait_ge(vZf, ci_ + 1)
                    tensor.wait_ge(sZb, ci_)
                    tensor.matmul(ps_zzb[0:1, :], ones_col_f_sb, accf,
                                  start=True, stop=True).then_inc(tZ, 1)

                def zb_mm(ci_):
                    tensor.wait_ge(vRec, ci_ + 1)
                    tensor.matmul(ps_zzb, ones_row_sb, recip_sb,
                                  start=True, stop=True).then_inc(tZb, 1)

                for h in range(HPC):
                    tensor.wait_ge(dKV[h % 2], 16 * 3 * (h // 2 + 1))
                    for qc in range(QC):
                        ci = h * QC + qc
                        qsl = slice(qc * 512, (qc + 1) * 512)
                        for g in range(32):
                            gg = 32 * ci + g
                            if ci >= 1 and g == 0:
                                z_mm(ci - 1)
                            if ci >= 1 and g == 2:
                                zb_mm(ci - 1)
                            tensor.wait_ge(sExp, max(0, gg - 2))
                            if ci == 0 and g == 0:
                                tensor.wait_ge(sEv, SC)
                            for jj in range(2):
                                kt = 2 * g + jj
                                last = tensor.matmul(ps_s[g % 3][:, jj * 512:(jj + 1) * 512],
                                              k_sb[h % 2][:, kt * 128:(kt + 1) * 128],
                                              q_sb[h % 2][:, qsl],
                                              start=True, stop=True)
                            last.then_inc(tS, 1)
                            if g >= 4:
                                tensor.wait_ge(sExp, gg - 3)
                                if g == 4:
                                    tensor.wait_ge(vO, ci)
                                for jj in range(2):
                                    kt = 2 * (g - 4) + jj
                                    last = tensor.matmul(ps_o, v_sb[h % 2][:, kt, :],
                                                  pring[:, (g - 4) % 8,
                                                        jj * 512:(jj + 1) * 512],
                                                  start=(kt == 0), stop=False)
                                last.then_inc(tPv, 1)
                        for gl in (28, 29, 30, 31):
                            gg = 32 * ci + gl
                            tensor.wait_ge(sExp, gg + 1)
                            for jj in range(2):
                                kt = 2 * gl + jj
                                last = tensor.matmul(ps_o, v_sb[h % 2][:, kt, :],
                                              pring[:, gl % 8, jj * 512:(jj + 1) * 512],
                                              start=False, stop=(kt == 63))
                            last.then_inc(tPv, 1)
                z_mm(NCH - 1)
                zb_mm(NCH - 1)

                if upto < 4:
                    return
                tensor.wait_ge(dCtxIn, 16 * HPC)
                for ec in range(32):
                    tensor.wait_ge(dWo[ec % 2], 16 * (ec // 2 + 1))
                    if ec < 2:
                        tensor.wait_ge(sZb, NCH)
                        tensor.wait_ge(vO, NCH)
                    else:
                        tensor.wait_ge(vS1, ec - 1)
                    for ts in range(2):
                        for m in range(32):
                            last = tensor.matmul(ps_y[ec % 2][:, ts * 512:(ts + 1) * 512],
                                          woT_sb[ec % 2][:, m, :],
                                          ctx_sb[:, m, ts * 512:(ts + 1) * 512],
                                          start=(m == 0), stop=(m == 31))
                    last.then_inc(tY, 1)
                    tensor.wait_ge(vS1, ec + 1)
                    for ts in range(2):
                        tensor.matmul(ps_mu[:, ts * 512:(ts + 1) * 512], ones_col_b_sb,
                                      s1b_sb[:, ts * 512:(ts + 1) * 512],
                                      start=(ec == 0), stop=(ec == 31))
                        last = tensor.matmul(ps_sq[:, ts * 512:(ts + 1) * 512], ones_col_b_sb,
                                      sqb_sb[:, ts * 512:(ts + 1) * 512],
                                      start=(ec == 0), stop=(ec == 31))
                    if ec == 31:
                        last.then_inc(tSt, 1)
                tensor.wait_ge(sRstd, 1)
                for ts in range(2):
                    tensor.matmul(ps_bc[:, ts * 512:(ts + 1) * 512], ones_row_sb,
                                  mu_sb[:, ts * 512:(ts + 1) * 512], start=True, stop=True)
                    last = tensor.matmul(ps_bc[:, 1024 + ts * 512:1024 + (ts + 1) * 512],
                                  ones_row_sb, rstd_sb[:, ts * 512:(ts + 1) * 512],
                                  start=True, stop=True)
                last.then_inc(tBc, 1)

                if upto < 6:
                    return
                tensor.wait_ge(vBc, 1)
                for (ktn, mat, kt, rows, nwid, ngr, klen, chk, wid) in DQU:
                    if kt == 0 and chk == 0:
                        tensor.wait_ge(vOh, mat + 1)
                        for ochk in range(OHC[mat]):
                            owid = min(512, klen - ochk * 512)
                            if OHOFF[mat] + ochk >= 1:
                                tensor.wait_ge(vOh2, OHOFF[mat] + ochk)
                            tensor.matmul(ps_ohb[0:ngr, 0:owid], ones_row_sb[:, 0:ngr],
                                          gi_f[:, ochk * 512:ochk * 512 + owid],
                                          start=True, stop=True).then_inc(tOh, 1)
                        tensor.wait_ge(vOh2, OHOFF[mat] + OHC[mat])
                        tensor.wait_ge(dSc, 16 * (mat + 1))
                    u = UOFF[ktn] + chk
                    if u >= 2:
                        tensor.wait_ge(vDq, u - 1)
                    pss = ps_dq[u % 2]
                    tensor.matmul(pss[0][0:rows, 0:wid],
                                  oh_sb[0:ngr, kt * 128:kt * 128 + rows],
                                  sc_sb[0:ngr, chk * 512:chk * 512 + wid],
                                  start=True, stop=True)
                    tensor.matmul(pss[1][0:rows, 0:wid],
                                  oh_sb[0:ngr, kt * 128:kt * 128 + rows],
                                  zp1_sb[0:ngr, chk * 512:chk * 512 + wid],
                                  start=True, stop=True).then_inc(tG, 1)

                if upto < 7:
                    return
                tensor.wait_ge(vDq, NDQU)
                tensor.wait_ge(dWres, 16)
                for tc in range(16):
                    tensor.wait_ge(dXf[tc % 2], 64 * (tc // 2 + 1))
                    for nt in range(NKT):
                        rows = 128 if nt < 10 else 96
                        it = NKT * tc + nt
                        if it >= 4:
                            tensor.wait_ge(sSilu, it - 3)
                        for kt in range(32):
                            last = tensor.matmul(ps_h[it % 4][0:rows, :],
                                          wres_gu[:, kt, nt * 128:nt * 128 + rows],
                                          xf_sb[tc % 2][:, kt, :],
                                          start=(kt == 0), stop=(kt == 31))
                        last.then_inc(tH, 1)
                tensor.wait_ge(dWres, 32)
                tensor.wait_ge(sSilu, NKT * 16)
                for tc in range(16):
                    tensor.wait_ge(dXf[tc % 2], 64 * (8 + tc // 2 + 1))
                    for nt in range(NKT):
                        rows = 128 if nt < 10 else 96
                        it = NKT * tc + nt
                        if it >= 4:
                            tensor.wait_ge(vH2, it - 3)
                        for kt in range(32):
                            last = tensor.matmul(ps_h[it % 4][0:rows, :],
                                          wres_gu[:, kt, nt * 128:nt * 128 + rows],
                                          xf_sb[tc % 2][:, kt, :],
                                          start=(kt == 0), stop=(kt == 31))
                        last.then_inc(tH2, 1)
                if upto < 8:
                    return
                tensor.wait_ge(dWres, 48)
                tensor.wait_ge(vH2, NKT * 16)
                i3 = 0
                for j in range(4):
                    for tc in range(16):
                        tensor.wait_ge(dH3[i3 % 2], 16 * (i3 // 2 + 1))
                        for ecc in range(8):
                            ie = 8 * i3 + ecc
                            if ie >= 2:
                                tensor.wait_ge(vE3, ie - 1)
                            esl = slice((j * 8 + ecc) * 128, (j * 8 + ecc + 1) * 128)
                            for nt in range(NKT):
                                rows = 128 if nt < 10 else 96
                                last = tensor.matmul(ps_y3[ecc % 2],
                                              wres_d[0:rows, nt, esl],
                                              hgr_sb[i3 % 2][0:rows, nt, :],
                                              start=(nt == 0), stop=(nt == NKT - 1))
                            last.then_inc(tY3, 1)
                        i3 += 1

                if upto < 9:
                    return
                for tt in range(NTT):
                    tensor.wait_ge(vF, tt + 1)
                    for ec in range(32):
                        ie = 32 * tt + ec
                        if ie >= 2:
                            tensor.wait_ge(sTe, ie - 1)
                        tensor.transpose(ps_tr[ec % 2], s2_sb[:, ec, :],
                                         ident_sb).then_inc(tTr, 1)

            @block.vector
            def _(vector):
                vector.wait_ge(dPre, 16 * NPRE)
                for c in range(SC if upto >= 1 else 0):
                    vector.wait_ge(dCS[c % 2], 32 * (c // 2 + 1))
                    if c >= 2:
                        vector.wait_ge(dSpQ[c % 2], 16 * (c // 2))
                        vector.wait_ge(dSpK[c % 2], 16 * (c // 2))
                    vector.wait_ge(tQ, c + 1)
                    for et in range(HPC):
                        psl = ps_q[:, et * 512:(et + 1) * 512]
                        vector.tensor_mul(uv_sb[:, 0:512], psl, cos_sb[c % 2])
                        vector.tensor_mul(psl, psl, sin_sb[c % 2])
                        vector.tensor_sub(qst_sb[c % 2][0:64, et, :],
                                          uv_sb[0:64, 0:512], psl[64:128, :])
                        last = vector.tensor_add(qst_sb[c % 2][64:128, et, :],
                                          uv_sb[64:128, 0:512], psl[0:64, :])
                    last.then_inc(vRope, 1)
                    vector.wait_ge(tK, c + 1)
                    for et in range(HPC):
                        psl = ps_k[:, et * 512:(et + 1) * 512]
                        vector.tensor_mul(uv_sb[:, 0:512], psl, cos_sb[c % 2])
                        vector.tensor_mul(psl, psl, sin_sb[c % 2])
                        vector.tensor_sub(kst_sb[c % 2][0:64, et, :],
                                          uv_sb[0:64, 0:512], psl[64:128, :])
                        last = vector.tensor_add(kst_sb[c % 2][64:128, et, :],
                                          uv_sb[64:128, 0:512], psl[0:64, :])
                    last.then_inc(vRope, 1)

                if upto < 3:
                    return
                def v_recip(ci_):
                    vector.wait_ge(tZ, ci_ + 1)
                    vector.reciprocal(recip_sb, ps_zzb[0:1, :]).then_inc(vRec, 1)

                def v_zbcopy(ci_):
                    vector.wait_ge(tZb, ci_ + 1)
                    vector.tensor_copy(zb_sb, ps_zzb).then_inc(sZb, 1)

                def v_oevac(ci_):
                    vector.wait_ge(tPv, 32 * (ci_ + 1))
                    if ci_ >= 2:
                        vector.wait_ge(dCtx[ci_ % 2], 16 * (ci_ // 2))
                    vector.tensor_mul(ctxst[ci_ % 2], ps_o, zb_sb).then_inc(vO, 1)

                for ci in range(NCH):
                    for g in range(32):
                        gg = 32 * ci + g
                        if ci >= 1 and g == 2:
                            v_recip(ci - 1)
                        if ci >= 1 and g == 3:
                            v_zbcopy(ci - 1)
                        if ci >= 1 and g == 4:
                            v_oevac(ci - 1)
                        vector.wait_ge(sExp, gg + 1)
                        pg = pring[:, g % 8, :]
                        if g % 4 == 0:
                            last = vector.tensor_copy(accb, pg)
                        else:
                            last = vector.tensor_add(accb, accb, pg)
                        if g % 4 == 3:
                            if g == 3:
                                last = vector.tensor_copy(acc32, accb)
                            else:
                                last = vector.tensor_add(acc32, acc32, accb)
                        last.then_inc(vZ, 1)
                    vector.wait_ge(tZ, ci)
                    vector.tensor_add(accf, acc32[:, 0:512], acc32[:, 512:1024]).then_inc(vZf, 1)
                v_recip(NCH - 1)
                v_zbcopy(NCH - 1)
                v_oevac(NCH - 1)

                if upto < 4:
                    return
                for ec in range(32):
                    vector.wait_ge(tY, ec + 1)
                    vector.wait_ge(dXo[ec % 2], 16 * (ec // 2 + 1))
                    if ec >= 2:
                        vector.wait_ge(dS1[ec % 2], 16 * (ec // 2))
                    vector.tensor_add(s1_st[ec % 2], ps_y[ec % 2], xo_sb[ec % 2])
                    vector.tensor_copy(s1b_sb, s1_st[ec % 2])
                    vector.tensor_mul(sqb_sb, s1b_sb, s1b_sb).then_inc(vS1, 1)
                vector.wait_ge(tSt, 1)
                vector.tensor_scalar(mu_sb, ps_mu, 1.0 / D, None, ALU.mult)
                vector.tensor_scalar(msq_sb, ps_sq, 1.0 / D,
                                     None, ALU.mult).then_inc(vCh, 1)
                vector.wait_ge(vCh, vch[0] + 1)
                vch[0] += 1
                vector.tensor_mul(var_sb, mu_sb, mu_sb).then_inc(vCh, 1)
                vector.wait_ge(vCh, vch[0] + 1)
                vch[0] += 1
                vector.tensor_sub(var2_sb, msq_sb, var_sb).then_inc(vSt, 1)
                vector.wait_ge(tBc, 1)
                vector.tensor_copy(mub_sb, ps_bc[:, 0:1024])
                vector.tensor_copy(rstdb_sb, ps_bc[:, 1024:2048]).then_inc(vBc, 1)
                for ec in range(32):
                    vector.wait_ge(dS1r[ec % 2], 16 * (ec // 2 + 1))
                    if ec >= 2:
                        vector.wait_ge(dAgx[ec % 2], 16 * (ec // 2))
                    vector.tensor_sub(s1r_sb[ec % 2], s1r_sb[ec % 2], mub_sb)
                    vector.tensor_mul(xfb_st[ec % 2], s1r_sb[ec % 2], rstdb_sb).then_inc(vXf, 1)

                if upto < 6:
                    return
                for (ktn, mat, kt, rows, nwid, ngr, klen, chk, wid) in DQU:
                    u = UOFF[ktn] + chk
                    pss = ps_dq[u % 2]
                    if kt == 0 and chk == 0:
                        npk = NSLP if mat < 2 else D // 8
                        vector.wait_ge(dDq, 16 * 2 * (mat + 1))
                        vector.wait_ge(dSc, 16 * (mat + 1))
                        if mat >= 1:
                            vector.wait_ge(tOh, OHOFF[mat])
                            vector.wait_ge(tG, UMAT[mat])
                        vector.tensor_copy(gi_f[:, 0:klen], gi_i[:, 0:klen])
                        for jjj in range(8):
                            vector.tensor_scalar(z_sb[0:ngr, jjj:nwid:8],
                                                 qz_sb[0:ngr, 0:npk],
                                                 4 * jjj, 15,
                                                 ALU.logical_shift_right, ALU.bitwise_and)
                        vector.tensor_copy(w4f_sb[0:ngr, 0:nwid], z_sb[0:ngr, 0:nwid])
                        vector.tensor_scalar(zp1_sb[0:ngr, 0:nwid], w4f_sb[0:ngr, 0:nwid],
                                             1.0, None, ALU.add).then_inc(vOh, 1)
                        for ochk in range(OHC[mat]):
                            owid = min(512, klen - ochk * 512)
                            vector.wait_ge(tOh, OHOFF[mat] + ochk + 1)
                            vector.tensor_scalar(
                                oh_sb[0:ngr, ochk * 512:ochk * 512 + owid],
                                ps_ohb[0:ngr, 0:owid],
                                iota32_sb[0:ngr, 0:1] if mat < 2 else iota86_sb[0:ngr, 0:1],
                                None, ALU.is_equal).then_inc(vOh2, 1)
                    if chk == 0:
                        vector.wait_ge(dQw[ktn % 2], 16 * (ktn // 2 + 1))
                        if ktn >= 2:
                            vector.wait_ge(dWbf[ktn % 2], 16 * (ktn // 2))
                        vector.tensor_scalar(qw_sb[ktn % 2][0:rows, 0:nwid],
                                             qw_sb[ktn % 2][0:rows, 0:nwid],
                                             shifts_sb[0:rows, 0:1], 15,
                                             ALU.logical_shift_right, ALU.bitwise_and)
                        vector.tensor_copy(w4f_sb[0:rows, 0:nwid],
                                           qw_sb[ktn % 2][0:rows, 0:nwid])
                    vector.wait_ge(tG, u + 1)
                    csl = slice(chk * 512, chk * 512 + wid)
                    vector.tensor_sub(tdq_sb[0:rows, 0:wid],
                                      w4f_sb[0:rows, csl], pss[1][0:rows, 0:wid])
                    vector.tensor_mul(wst_sb[ktn % 2][0:rows, csl],
                                      tdq_sb[0:rows, 0:wid],
                                      pss[0][0:rows, 0:wid]).then_inc(vDq, 1)

                if upto < 7:
                    return
                for tc in range(16):
                    vector.wait_ge(dHgr[tc % 2], 16 * (tc // 2 + 1))
                    for nt in range(NKT):
                        rows = 128 if nt < 10 else 96
                        it = NKT * tc + nt
                        vector.wait_ge(tH2, it + 1)
                        vector.tensor_mul(hgr_sb[tc % 2][0:rows, nt, :],
                                          ps_h[it % 4][0:rows, :],
                                          hgr_sb[tc % 2][0:rows, nt, :]).then_inc(vH2, 1)
                if upto < 8:
                    return
                i3 = 0
                for j in range(4):
                    for tc in range(16):
                        if i3 >= 2:
                            pj, pt = divmod(i3 - 2, 16)
                            vector.wait_ge(dRs8[pj][pt % 2], 16 * (pt // 2 + 1))
                        for ecc in range(8):
                            ie = 8 * i3 + ecc
                            vector.wait_ge(tY3, ie + 1)
                            vector.tensor_copy(yst_sb[i3 % 2][:, ecc, :],
                                               ps_y3[ecc % 2]).then_inc(vE3, 1)
                        i3 += 1

                if upto < 9:
                    return
                for tt in range(NTT):
                    vector.wait_ge(dF1[tt % 2], 256 + 16 * (tt // 2 + 1))
                    vector.wait_ge(dYs[tt % 2], 96 + 64 * (tt // 2 + 1))
                    tsl = slice(tt * 128, (tt + 1) * 128)
                    for ec in range(32):
                        vector.tensor_sub(s2_sb[:, ec, :], s1f_sb[tt % 2][:, ec, :],
                                          mub_sb[:, tsl]).then_inc(vCh, 1)
                        vector.wait_ge(vCh, vch[0] + 1)
                        vch[0] += 1
                        vector.tensor_mul(s2_sb[:, ec, :], s2_sb[:, ec, :],
                                          rstdb_sb[:, tsl]).then_inc(vCh, 1)
                        vector.wait_ge(vCh, vch[0] + 1)
                        vch[0] += 1
                        last = vector.tensor_add(s2_sb[:, ec, :], s2_sb[:, ec, :],
                                          ysl_sb[tt % 2][:, ec, :])
                    last.then_inc(vF, 1)
                    vector.wait_ge(sTe, 32 * (tt + 1))
                    vector.tensor_mul(t3_sb, tmaj_sb[tt % 2], tmaj_sb[tt % 2])
                    vector.reduce_sum(red_sb[:, 0:1], tmaj_sb[tt % 2],
                                      mybir.AxisListType.X)
                    vector.reduce_sum(red_sb[:, 1:2], t3_sb,
                                      mybir.AxisListType.X).then_inc(vCh, 1)
                    vector.wait_ge(vCh, vch[0] + 1)
                    vch[0] += 1
                    vector.tensor_scalar(red_sb[:, 2:3], red_sb[:, 0:1],
                                         -1.0 / D, None, ALU.mult)
                    vector.tensor_scalar(red_sb[:, 4:5], red_sb[:, 1:2],
                                         1.0 / D, None, ALU.mult).then_inc(vCh, 1)
                    vector.wait_ge(vCh, vch[0] + 1)
                    vch[0] += 1
                    vector.tensor_mul(red_sb[:, 3:4], red_sb[:, 2:3],
                                      red_sb[:, 2:3]).then_inc(vCh, 1)
                    vector.wait_ge(vCh, vch[0] + 1)
                    vch[0] += 1
                    vector.tensor_sub(red_sb[:, 5:6], red_sb[:, 4:5],
                                      red_sb[:, 3:4]).then_inc(vLn, 1)
                    vector.wait_ge(sLn, tt + 1)
                    if tt >= 2:
                        vector.wait_ge(dOut[tt % 2], 256 + 16 * (tt // 2))
                    vector.tensor_scalar(out_st[tt % 2], tmaj_sb[tt % 2],
                                         red_sb[:, 2:3], red_sb[:, 3:4],
                                         ALU.add, ALU.mult).then_inc(vLn2, 1)

            @block.scalar
            def _(scalar):
                scalar.wait_ge(dPre, 16 * NPRE)
                for c in range(SC if upto >= 2 else 0):
                    scalar.wait_ge(tV, c + 1)
                    if c >= 2:
                        scalar.wait_ge(dSpV[c % 2], 64 * (c // 2))
                    scalar.copy(vst_sb[c % 2], ps_q).then_inc(sEv, 1)

                if upto < 3:
                    return
                for ci in range(NCH):
                    for g in range(32):
                        gg = 32 * ci + g
                        scalar.wait_ge(tS, gg + 1)
                        if gg >= 8:
                            scalar.wait_ge(tPv, gg - 7)
                            scalar.wait_ge(vZ, gg - 7)
                        scalar.activation(pring[:, g % 8, :], ps_s[g % 3], AF.Exp,
                                          bias=cv[:, C_EXPB:C_EXPB + 1],
                                          scale=cv[:, C_QKSC:C_QKSC + 1]).then_inc(sExp, 1)

                if upto < 4:
                    return
                scalar.wait_ge(vSt, 1)
                scalar.activation(var_sb, var2_sb, AF.Ln,
                                  bias=cv[0:1, C_EPS:C_EPS + 1]).then_inc(sCh, 1)
                scalar.wait_ge(sCh, sch[0] + 1)
                sch[0] += 1
                scalar.activation(rstd_sb, var_sb, AF.Exp,
                                  scale=cv[0:1, C_NHALF:C_NHALF + 1]).then_inc(sRstd, 1)

                if upto < 7:
                    return
                for tc in range(16):
                    for nt in range(NKT):
                        rows = 128 if nt < 10 else 96
                        it = NKT * tc + nt
                        scalar.wait_ge(tH, it + 1)
                        if tc >= 2 and nt == 0:
                            scalar.wait_ge(dHg[tc % 2], 16 * (tc // 2))
                        scalar.activation(hst_sb[tc % 2][0:rows, nt, :],
                                          ps_h[it % 4][0:rows, :],
                                          AF.Silu).then_inc(sSilu, 1)

                if upto < 8:
                    return
                i3s = 0
                for j in range(4):
                    for tc in range(16):
                        r, hf = tc // 2, tc % 2
                        scalar.wait_ge(vE3, 8 * (i3s + 1))
                        scalar.dma_start(
                            out=rs_in[j][r, :, hf * 512:(hf + 1) * 512].rearrange(
                                "(ee p) t -> p ee t", p=128),
                            in_=yst_sb[i3s % 2]).then_inc(dRs8[j][tc % 2], 16)
                        i3s += 1

                if upto < 9:
                    return
                for tt in range(NTT):
                    for ec in range(32):
                        scalar.wait_ge(tTr, 32 * tt + ec + 1)
                        scalar.copy(tmaj_sb[tt % 2][:, ec * 128:(ec + 1) * 128],
                                    ps_tr[ec % 2]).then_inc(sTe, 1)
                    scalar.wait_ge(vLn, tt + 1)
                    scalar.activation(red_sb[:, 4:5], red_sb[:, 5:6], AF.Ln,
                                      bias=cv[:, C_EPS:C_EPS + 1]).then_inc(sCh, 1)
                    scalar.wait_ge(sCh, sch[0] + 1)
                    sch[0] += 1
                    scalar.activation(red_sb[:, 3:4], red_sb[:, 4:5], AF.Exp,
                                      scale=cv[:, C_NHALF:C_NHALF + 1]).then_inc(sLn, 1)
                    scalar.wait_ge(vLn2, tt + 1)
                    scalar.dma_start(out=y_out[tt * 128:(tt + 1) * 128, :],
                                     in_=out_st[tt % 2]).then_inc(dOut[tt % 2], 16)
                scalar.wait_ge(dOut[0], 256 + 16 * NTT // 2)
                scalar.wait_ge(dOut[1], 256 + 16 * NTT // 2)

    return nc


def _prep_inputs(inputs):
    bf = ml_dtypes.bfloat16
    x = np.asarray(inputs["L0_attn_x"])
    Wq = np.asarray(inputs["L0_attn_Wq"])
    Wk = np.asarray(inputs["L0_attn_Wk"])
    Wv = np.asarray(inputs["L0_attn_Wv"])
    Wo = np.asarray(inputs["L0_attn_Wo"])

    x0 = np.ascontiguousarray(x.reshape(S, D))
    xh = np.ascontiguousarray(x0.T.reshape(32, 128, S).transpose(1, 0, 2))

    def tiled_T(Wm):
        return np.ascontiguousarray(Wm.T.reshape(32, 128, D).transpose(1, 0, 2))

    wqT, wkT, wvT, woT = map(tiled_T, (Wq, Wk, Wv, Wo))

    inv = 1.0 / (10000.0 ** (np.arange(0, HD, 2, dtype=np.float64) / HD))
    fr = inv[:, None] * np.arange(S, dtype=np.float64)[None, :]
    cos1 = np.cos(fr).astype(np.float32)
    sin1 = np.sin(fr).astype(np.float32)
    cos2 = np.ascontiguousarray(np.concatenate([cos1, cos1], axis=0))
    sin2 = np.ascontiguousarray(np.concatenate([sin1, sin1], axis=0))

    cvec = np.zeros((128, NCONST), dtype=np.float32)
    cvec[:, C_EXPB] = EXP_BIAS
    cvec[:, C_QKSC] = QK_SCALE
    cvec[:, C_EPS] = EPS
    cvec[:, C_NHALF] = -0.5

    shared = {
        "xh": xh, "cos2": cos2, "sin2": sin2,
        "ident": np.eye(128, dtype=np.float32),
        "ones_row": np.ones((1, 128), dtype=np.float32),
        "ones_col_f": np.ones((128, 1), dtype=np.float32),
        "ones_col_b": np.ones((128, 1), dtype=bf),
        "iota32": np.arange(32, dtype=np.float32).reshape(32, 1),
        "iota86": np.arange(86, dtype=np.float32).reshape(86, 1),
        "shifts": (4 * (np.arange(128) % 8)).astype(np.int32).reshape(128, 1),
        "cvec": cvec, "woT": woT,
        "g_gi": np.ascontiguousarray(np.asarray(inputs["L0_ffn_gate_g_idx"]).reshape(1, D)),
        "u_gi": np.ascontiguousarray(np.asarray(inputs["L0_ffn_up_g_idx"]).reshape(1, D)),
        "d_sc": np.ascontiguousarray(np.asarray(inputs["L0_ffn_down_scales"])),
        "d_qz": np.ascontiguousarray(np.asarray(inputs["L0_ffn_down_qzeros"])),
    }
    g_qw_full = np.asarray(inputs["L0_ffn_gate_qweight"])
    u_qw_full = np.asarray(inputs["L0_ffn_up_qweight"])
    d_qw_full = np.asarray(inputs["L0_ffn_down_qweight"])
    g_sc_full = np.asarray(inputs["L0_ffn_gate_scales"])
    u_sc_full = np.asarray(inputs["L0_ffn_up_scales"])
    g_qz_full = np.asarray(inputs["L0_ffn_gate_qzeros"])
    u_qz_full = np.asarray(inputs["L0_ffn_up_qzeros"])
    d_gi_full = np.asarray(inputs["L0_ffn_down_g_idx"])

    in_maps = []
    for c in range(W):
        esl = slice(EOWN * c, EOWN * (c + 1))
        nsl = slice(NSL * c, NSL * (c + 1))
        nslp = slice(NSLP * c, NSLP * (c + 1))
        m = dict(shared)
        m["x_own"] = np.ascontiguousarray(xh[:, :, TOK * c:TOK * (c + 1)])
        m["wqT"] = np.ascontiguousarray(wqT[:, :, esl])
        m["wkT"] = np.ascontiguousarray(wkT[:, :, esl])
        m["wvT"] = np.ascontiguousarray(wvT[:, :, esl])
        m["g_qw"] = np.ascontiguousarray(np.repeat(g_qw_full[:, nsl], 8, axis=0))
        m["u_qw"] = np.ascontiguousarray(np.repeat(u_qw_full[:, nsl], 8, axis=0))
        m["d_qw"] = np.ascontiguousarray(
            np.repeat(d_qw_full[NSLP * c:NSLP * (c + 1), :], 8, axis=0))
        m["g_sc"] = np.ascontiguousarray(g_sc_full[:, nsl])
        m["u_sc"] = np.ascontiguousarray(u_sc_full[:, nsl])
        m["g_qz"] = np.ascontiguousarray(g_qz_full[:, nslp])
        m["u_qz"] = np.ascontiguousarray(u_qz_full[:, nslp])
        m["d_gi"] = np.ascontiguousarray(d_gi_full[nsl].reshape(1, NSL))
        in_maps.append(m)
    return in_maps


def kernel(**inputs):
    in_maps = _prep_inputs(inputs)
    nc = build_nc()
    res = run_bass_kernel_spmd(nc, in_maps, core_ids=list(range(W)))
    out = np.concatenate([res.results[c]["y"] for c in range(W)], axis=0)
    return out.reshape(1, S, D).astype(np.float32)


# version 07
# speedup vs baseline: 1.2492x; 1.0949x over previous
import os
import contextlib
import numpy as np
import ml_dtypes

os.environ.setdefault("TRN_ALLOW_ICI_ALL_TO_ALL", "1")

import concourse.bass as bass
import concourse.mybir as mybir
from concourse.bass_utils import run_bass_kernel_spmd

F32 = mybir.dt.float32
BF16 = mybir.dt.bfloat16
I32 = mybir.dt.int32
AF = mybir.ActivationFunctionType
ALU = mybir.AluOpType

W = 8
S = 8192
D = 4096
HD = 128
HPC = 4
EOWN = HPC * HD
TOK = S // W
INTER = 11008
NSL = INTER // W
NSLP = NSL // 8
NG = 32
NGD = 86
NKT = 11
SC = 16
QC = 16
NCH = HPC * QC
NTT = TOK // 128

EXP_BIAS = -12.0
QK_SCALE = 1.0 / float(np.sqrt(HD))
EPS = 1e-5

C_EXPB, C_QKSC, C_EPS, C_NHALF = range(4)
NCONST = 4
RG = [list(range(W))]

DQT = []
for _mat in range(3):
    _nkt = 32 if _mat < 2 else NKT
    for _kt in range(_nkt):
        _rows = 96 if (_mat == 2 and _kt == NKT - 1) else 128
        DQT.append((_mat, _kt, _rows,
                    NSL if _mat < 2 else D,
                    NG if _mat < 2 else NGD,
                    D if _mat < 2 else NSL))
NDQ = len(DQT)
OHC = [(D + 511) // 512, (D + 511) // 512, (NSL + 511) // 512]
OHOFF = [0, 8, 16]
DQOFF = [0, 32, 64]
DQU = []
UOFF = []
for _ktn, (_mat, _kt, _rows, _nwid, _ngr, _klen) in enumerate(DQT):
    UOFF.append(len(DQU))
    for _chk in range(-(-_nwid // 512)):
        _wid = min(512, _nwid - _chk * 512)
        DQU.append((_ktn, _mat, _kt, _rows, _nwid, _ngr, _klen, _chk, _wid))
NDQU = len(DQU)
UOFF.append(NDQU)
UMAT = [0, UOFF[32], UOFF[64]]


def _ceil(a, b):
    return -(-a // b)


def build_nc(upto=9, dbg=()):
    nc = bass.Bass("TRN2", target_bir_lowering=False, num_devices=W)

    di = lambda n, s, d: nc.dram_tensor(n, s, d, kind="ExternalInput").ap()
    xh = di("xh", [128, 32, S], F32)
    x_own = di("x_own", [128, 32, TOK], F32)
    wqT = di("wqT", [128, 32, EOWN], F32)
    wkT = di("wkT", [128, 32, EOWN], F32)
    wvT = di("wvT", [128, 32, EOWN], F32)
    woT = di("woT", [128, 32, D], F32)
    cos2 = di("cos2", [128, S], F32)
    sin2 = di("sin2", [128, S], F32)
    ident = di("ident", [128, 128], F32)
    ones_row = di("ones_row", [1, 128], F32)
    ones_col_f = di("ones_col_f", [128, 1], F32)
    ones_col_b = di("ones_col_b", [128, 1], BF16)
    iota32 = di("iota32", [32, 1], F32)
    iota86 = di("iota86", [86, 1], F32)
    shifts = di("shifts", [128, 1], I32)
    cvec = di("cvec", [128, NCONST], F32)
    g_qw = di("g_qw", [D, NSL], I32)
    u_qw = di("u_qw", [D, NSL], I32)
    d_qw = di("d_qw", [NSL, D], I32)
    g_sc = di("g_sc", [NG, NSL], F32)
    u_sc = di("u_sc", [NG, NSL], F32)
    d_sc = di("d_sc", [NGD, D], F32)
    g_qz = di("g_qz", [NG, NSLP], I32)
    u_qz = di("u_qz", [NG, NSLP], I32)
    d_qz = di("d_qz", [NGD, D // 8], I32)
    g_gi = di("g_gi", [1, D], I32)
    u_gi = di("u_gi", [1, D], I32)
    d_gi = di("d_gi", [1, NSL], I32)
    y_out = nc.dram_tensor("y", [TOK, D], F32, kind="ExternalOutput").ap()

    dt = lambda n, s, d: nc.dram_tensor(
        n, s, d, kind=("ExternalOutput" if n in dbg else "Internal")).ap()
    q_d = dt("q_d", [HPC, 128, S], BF16)
    k_d = dt("k_d", [HPC, 128, S], BF16)
    v_d = dt("v_d", [HPC, 128, S // 128, HD], BF16)
    a2a_in = [dt(f"a2a_in{h}", [W, 128, TOK], BF16) for h in range(HPC)]
    a2a_out = [dt(f"a2a_out{h}", [W, 128, TOK], BF16) for h in range(HPC)]
    s1_d = dt("s1_d", [32, 128, TOK], F32)
    agx_in = dt("agx_in", [32, 128, TOK], BF16)
    agx_out = [nc.dram_tensor(f"agx_out{i}", [W, 8, 128, TOK], BF16,
                              kind="Internal", addr_space="Shared").ap()
               for i in range(4)]
    wg_d = dt("wg_d", [32, 128, NSL], BF16)
    wu_d = dt("wu_d", [32, 128, NSL], BF16)
    wd_d = dt("wd_d", [NKT, 128, D], BF16)
    hg_d = dt("hg_d", [NKT, 128, S], BF16)
    h_d = dt("h_d", [NKT, 128, S], BF16)
    rs_in = [dt(f"rs_in{j}", [W, 1024, TOK], BF16) for j in range(4)]
    rs_out = [dt(f"rs_out{j}", [1024, TOK], BF16) for j in range(4)]
    if "rs_dbg" in dbg:
        rsi_dbg = nc.dram_tensor("rsi_dbg", [W, 1024, TOK], BF16, kind="ExternalOutput").ap()
        rso_dbg = nc.dram_tensor("rso_dbg", [1024, TOK], BF16, kind="ExternalOutput").ap()
    WMAT = [wg_d, wu_d, wd_d]

    es = contextlib.ExitStack()
    mk = lambda n: es.enter_context(nc.semaphore(n))
    mk2 = lambda n: [mk(n + "0"), mk(n + "1")]
    (dPre, tQ, tK, vRope, tV, sEv, tS, sExp, vZ, tPv,
     vZf, tZ, vRec, tZb, sZb, vO, cA2A, dCtxIn, tY, vS1,
     tSt, vSt, sRstd, tBc, vBc, vXf, cAGX, dDq, dSc, tOh, vOh,
     vOh2, tG, vDq, dWres, tH, sSilu, tH2, vH2,
     tY3, vE3, cRS, vF, tTr, sTe, vLn, sLn, vLn2) = [mk(f"s{i}") for i in range(48)]
    dX = mk2("dX")
    dXv = mk2("dXv")
    dCS = mk2("dCS")
    dSpQ = mk2("dSpQ")
    dSpK = mk2("dSpK")
    dSpV = mk2("dSpV")
    dKV = mk2("dKV")
    dCtx = mk2("dCtx")
    dWo = mk2("dWo")
    dXo = mk2("dXo")
    dS1 = mk2("dS1")
    dS1r = mk2("dS1r")
    dAgx = mk2("dAgx")
    dQw = mk2("dQw")
    dWbf = mk2("dWbf")
    dXf = mk2("dXf")
    dHg = mk2("dHg")
    dHgr = mk2("dHgr")
    dH2 = mk2("dH2")
    dH3 = mk2("dH3")
    dRs8 = [[mk(f"dRs{j}_{p}") for p in range(2)] for j in range(4)]
    dF1 = dCS
    dYs = dKV
    dOut = dWo
    vCh = mk("vCh")
    sCh = mk("sCh")

    AL = nc.alloc_sbuf_tensor
    ident_sb = AL("ident_sb", [128, 128], F32).ap()
    ones_row_sb = AL("ones_row_sb", [1, 128], F32).ap()
    ones_col_f_sb = AL("ones_col_f_sb", [128, 1], F32).ap()
    ones_col_b_sb = AL("ones_col_b_sb", [128, 1], BF16).ap()
    iota32_sb = AL("iota32_sb", [32, 1], F32).ap()
    iota86_sb = AL("iota86_sb", [86, 1], F32).ap()
    shifts_sb = AL("shifts_sb", [128, 1], I32).ap()
    cv = AL("cv_sb", [128, NCONST], F32).ap()
    mub_sb = AL("mub_sb", [128, TOK], F32).ap()
    rstdb_sb = AL("rstdb_sb", [128, TOK], F32).ap()
    red_sb = AL("red_sb", [128, 6], F32).ap()

    ABASE = nc.sbuf_base
    ATOP = nc.sbuf_top

    class Ph:
        def __init__(self):
            self.off = ABASE

        def t(self, name, shape, dtype):
            per = int(np.prod(shape[1:])) * np.dtype(mybir.dt.np(dtype)).itemsize
            off = (self.off + 31) & ~31
            h = nc.alloc_sbuf_tensor_at(name, shape, dtype, offset=off)
            self.off = off + per
            assert self.off <= ATOP, f"{name}: arena overflow {self.off} > {ATOP}"
            return h.ap()

    ph = Ph()
    wqT_sb = ph.t("wqT_sb", [128, 32, EOWN], BF16)
    wkT_sb = ph.t("wkT_sb", [128, 32, EOWN], BF16)
    xt_sb = [ph.t(f"xt{i}", [128, 32, 512], BF16) for i in range(2)]
    cos_sb = [ph.t(f"cos{i}", [128, 512], F32) for i in range(2)]
    sin_sb = [ph.t(f"sin{i}", [128, 512], F32) for i in range(2)]
    uv_sb = ph.t("uv_sb", [128, 1024], F32)
    qst_sb = [ph.t(f"qst{i}", [128, HPC, 512], BF16) for i in range(2)]
    kst_sb = [ph.t(f"kst{i}", [128, HPC, 512], BF16) for i in range(2)]
    wvT_sb = ph.t("wvT_sb", [128, 32, EOWN], BF16)
    vst_sb = [ph.t(f"vst{i}", [128, 4 * EOWN], BF16) for i in range(2)]

    pa = Ph()
    k_sb = [pa.t(f"k_sb{i}", [128, S], BF16) for i in range(2)]
    v_sb = [pa.t(f"v_sb{i}", [128, S // 128, HD], BF16) for i in range(2)]
    q_sb = [pa.t(f"q_sb{i}", [128, S], BF16) for i in range(2)]
    pring = pa.t("pring", [128, 8, 1024], BF16)
    accb = pa.t("accb", [128, 1024], BF16)
    acc32 = pa.t("acc32", [128, 1024], F32)
    accf = pa.t("accf", [128, 512], F32)
    recip_sb = pa.t("recip_sb", [1, 512], F32)
    zb_sb = pa.t("zb_sb", [128, 512], F32)
    ctxst = [pa.t(f"ctxst{i}", [128, 512], BF16) for i in range(2)]

    po = Ph()
    s1r_sb = [po.t(f"s1r{i}", [128, TOK], F32) for i in range(2)]
    xfb_st = [po.t(f"xfb{i}", [128, TOK], BF16) for i in range(2)]
    norm_top = po.off
    ctx_sb = po.t("ctx_sb", [128, 32, TOK], BF16)
    woT_sb = [po.t(f"woT{i}", [128, 32, 128], BF16) for i in range(2)]
    xo_sb = [po.t(f"xo{i}", [128, TOK], F32) for i in range(2)]
    s1_st = [po.t(f"s1st{i}", [128, TOK], F32) for i in range(2)]
    s1b_sb = po.t("s1b_sb", [128, TOK], BF16)
    sqb_sb = po.t("sqb_sb", [128, TOK], BF16)
    mu_sb = po.t("mu_sb", [1, TOK], F32)
    msq_sb = po.t("msq_sb", [1, TOK], F32)
    var_sb = po.t("var_sb", [1, TOK], F32)
    var2_sb = po.t("var2_sb", [1, TOK], F32)
    rstd_sb = po.t("rstd_sb", [1, TOK], F32)

    pd = Ph()
    pd.off = norm_top
    gi_i = pd.t("gi_i", [1, D], I32)
    gi_f = pd.t("gi_f", [1, D], F32)
    oh_sb = pd.t("oh_sb", [86, D], BF16)
    z_sb = pd.t("z_sb", [86, D], I32)
    zp1_sb = pd.t("zp1_sb", [86, D], BF16)
    sc_sb = pd.t("sc_sb", [86, D], BF16)
    qz_sb = pd.t("qz_sb", [86, 512], I32)
    qw_sb = [pd.t(f"qw{i}", [128, D], I32) for i in range(2)]
    w4f_sb = pd.t("w4f_sb", [128, D], F32)
    tdq_sb = pd.t("tdq_sb", [128, 512], F32)
    wst_sb = [pd.t(f"wst{i}", [128, D], BF16) for i in range(2)]

    pf = Ph()
    wres_raw = pf.t("wres_sb", [128, NKT * D], BF16)
    wres_gu = wres_raw[:, 0:32 * NSL].rearrange("p (k n) -> p k n", n=NSL)
    wres_d = wres_raw.rearrange("p (k n) -> p k n", n=D)
    xf_off = pf.off
    xf_sb = [pf.t(f"xf{i}", [128, 32, 512], BF16) for i in range(2)]
    hst_sb = [pf.t(f"hst{i}", [128, NKT, 512], BF16) for i in range(2)]
    hgr_sb = [pf.t(f"hgr{i}", [128, NKT, 512], BF16) for i in range(2)]
    yst_sb = [nc.alloc_sbuf_tensor_at(
        f"yst{i}", [128, 8, 512], BF16,
        offset=((xf_off + 31) & ~31) + i * 8 * 512 * 2).ap() for i in range(2)]

    pz = Ph()
    s1f_sb = [pz.t(f"s1f{i}", [128, 32, 128], F32) for i in range(2)]
    ysl_sb = [pz.t(f"ysl{i}", [128, 32, 128], F32) for i in range(2)]
    s2_sb = pz.t("s2_sb", [128, 32, 128], F32)
    tmaj_sb = [pz.t(f"tmaj{i}", [128, D], F32) for i in range(2)]
    t3_sb = pz.t("t3_sb", [128, D], F32)
    out_st = [pz.t(f"outst{i}", [128, D], F32) for i in range(2)]

    ps_a = nc.alloc_psum_tensor("ps_a", [128, 2048], F32).ap()
    ps_b = nc.alloc_psum_tensor("ps_b", [128, 2048], F32).ap()
    ps_s = [ps_a[:, 0:1024], ps_a[:, 1024:2048], ps_b[:, 0:1024]]
    ps_o = ps_b[:, 1024:1536]
    ps_zzb = ps_b[:, 1536:2048]
    ps_q = ps_a
    ps_k = ps_b
    ps_y = [ps_a[:, 0:1024], ps_a[:, 1024:2048]]
    ps_mu = ps_b[0:1, 0:1024]
    ps_sq = ps_b[0:1, 1024:2048]
    ps_bc = ps_a
    ps_dq = [[ps_a[:, 0:512], ps_a[:, 512:1024]],
             [ps_a[:, 1024:1536], ps_a[:, 1536:2048]]]
    ps_ohb = ps_b[0:86, 0:512]
    ps_h = [ps_a[:, 0:512], ps_a[:, 512:1024], ps_a[:, 1024:1536], ps_a[:, 1536:2048]]
    ps_y3 = [ps_b[:, 0:512], ps_b[:, 512:1024]]
    ps_tr = [ps_b[:, 0:128], ps_b[:, 512:640]]

    NPRE = 10
    vch = [0]
    sch = [0]

    with es:
        with nc.Block() as block:

            @block.sync
            def _(sync):
                sync.dma_start(out=ident_sb, in_=ident).then_inc(dPre, 16)
                sync.dma_start(out=ones_row_sb, in_=ones_row).then_inc(dPre, 16)
                sync.dma_start(out=ones_col_f_sb, in_=ones_col_f).then_inc(dPre, 16)
                sync.dma_start(out=ones_col_b_sb, in_=ones_col_b).then_inc(dPre, 16)
                sync.dma_start(out=iota32_sb, in_=iota32).then_inc(dPre, 16)
                sync.dma_start(out=iota86_sb, in_=iota86).then_inc(dPre, 16)
                sync.dma_start(out=shifts_sb, in_=shifts).then_inc(dPre, 16)
                sync.dma_start(out=cv, in_=cvec).then_inc(dPre, 16)

                for c in range(SC if upto >= 1 else 0):
                    sl = slice(c * 512, (c + 1) * 512)
                    if c >= 2:
                        sync.wait_ge(vRope, 2 * (c - 1))
                    sync.dma_start(out=cos_sb[c % 2], in_=cos2[:, sl]).then_inc(dCS[c % 2], 16)
                    sync.dma_start(out=sin_sb[c % 2], in_=sin2[:, sl]).then_inc(dCS[c % 2], 16)
                    sync.wait_ge(vRope, 2 * c + 1)
                    sync.dma_start(out=q_d[:, :, sl].rearrange("h p s -> p h s"),
                                   in_=qst_sb[c % 2]).then_inc(dSpQ[c % 2], 16)
                    sync.wait_ge(vRope, 2 * c + 2)
                    sync.dma_start(out=k_d[:, :, sl].rearrange("h p s -> p h s"),
                                   in_=kst_sb[c % 2]).then_inc(dSpK[c % 2], 16)
                for c in range(SC if upto >= 2 else 0):
                    sl = slice(c * 512, (c + 1) * 512)
                    sync.wait_ge(sEv, c + 1)
                    vv = vst_sb[c % 2].rearrange("p (t h d) -> p t h d", t=4, h=HPC)
                    for hh in range(HPC):
                        sync.dma_start(
                            out=v_d[hh, :, 4 * c:4 * c + 4, :].rearrange("p t d -> p t d"),
                            in_=vv[:, :, hh, :]).then_inc(dSpV[c % 2], 16)

                for h in range(HPC if upto >= 3 else 0):
                    if h == 0:
                        for sm_ in (dSpQ[0], dSpQ[1], dSpK[0], dSpK[1]):
                            sync.wait_ge(sm_, 16 * 8)
                        sync.wait_ge(dSpV[0], 64 * 8)
                        sync.wait_ge(dSpV[1], 64 * 8)
                    elif h >= 2:
                        sync.wait_ge(tPv, 32 * QC * (h - 1))
                    sync.dma_start(out=k_sb[h % 2], in_=k_d[h]).then_inc(dKV[h % 2], 16)
                    sync.dma_start(out=v_sb[h % 2], in_=v_d[h]).then_inc(dKV[h % 2], 16)
                    sync.dma_start(out=q_sb[h % 2], in_=q_d[h]).then_inc(dKV[h % 2], 16)

                if upto < 4:
                    return
                sync.wait_ge(cA2A, HPC)
                for h in range(HPC):
                    sync.dma_start(out=ctx_sb[:, h::HPC, :],
                                   in_=a2a_out[h].rearrange("r p t -> p r t")
                                   ).then_inc(dCtxIn, 16)
                for ec in range(32):
                    if ec >= 2:
                        sync.wait_ge(vS1, ec - 1)
                    sync.dma_start(out=xo_sb[ec % 2], in_=x_own[:, ec, :]).then_inc(dXo[ec % 2], 16)
                    sync.wait_ge(vS1, ec + 1)
                    sync.dma_start(out=s1_d[ec], in_=s1_st[ec % 2]).then_inc(dS1[ec % 2], 16)
                def sy_norm(ec):
                    if ec >= 2:
                        sync.wait_ge(vXf, ec - 1)
                    sync.dma_start(out=s1r_sb[ec % 2], in_=s1_d[ec]).then_inc(dS1r[ec % 2], 16)
                    sync.wait_ge(vXf, ec + 1)
                    sync.dma_start(out=agx_in[ec], in_=xfb_st[ec % 2]).then_inc(dAgx[ec % 2], 16)

                if upto < 6:
                    for ec in range(32):
                        sy_norm(ec)
                    return
                sync.wait_ge(tY, 32)
                sync.dma_start(out=qz_sb[0:NG, 0:NSLP], in_=g_qz).then_inc(dDq, 16)
                sync.dma_start(out=gi_i, in_=g_gi).then_inc(dDq, 16)
                for ktn, (mat, kt, rows, nwid, ngr, klen) in enumerate(DQT):
                    if ktn == 0:
                        sync.wait_ge(vS1, 32)
                        sync.wait_ge(dS1[0], 256)
                        sync.wait_ge(dS1[1], 256)
                        sync.wait_ge(tBc, 1)
                    if ktn >= 2:
                        sync.wait_ge(vDq, UOFF[ktn - 1])
                    sync.dma_start(
                        out=qw_sb[ktn % 2][0:rows, 0:nwid],
                        in_=(g_qw, u_qw, d_qw)[mat][kt * 128: kt * 128 + rows, :]
                    ).then_inc(dQw[ktn % 2], 16)
                    if ktn < 32:
                        sy_norm(ktn)
                    if ktn >= 1:
                        pm, pk, prow, pnw, _, _ = DQT[ktn - 1]
                        sync.wait_ge(vDq, UOFF[ktn])
                        sync.dma_start(
                            out=WMAT[pm][pk][0:prow, :],
                            in_=wst_sb[(ktn - 1) % 2][0:prow, 0:pnw]).then_inc(dWbf[(ktn - 1) % 2], 16)
                    if mat < 2 and kt == 31:
                        sync.wait_ge(vOh, mat + 1)
                        sync.wait_ge(tOh, OHOFF[mat + 1])
                        nxt_qz, nxt_gi = (u_qz, u_gi) if mat == 0 else (d_qz, d_gi)
                        sync.dma_start(out=qz_sb[0:(NG if mat == 0 else NGD),
                                                 0:(NSLP if mat == 0 else D // 8)],
                                       in_=nxt_qz).then_inc(dDq, 16)
                        sync.dma_start(out=gi_i[:, 0:(D if mat == 0 else NSL)],
                                       in_=nxt_gi).then_inc(dDq, 16)
                pm, pk, prow, pnw, _, _ = DQT[NDQ - 1]
                sync.wait_ge(vDq, NDQU)
                sync.dma_start(out=WMAT[pm][pk][0:prow, :],
                               in_=wst_sb[(NDQ - 1) % 2][0:prow, 0:pnw]).then_inc(dWbf[(NDQ - 1) % 2], 16)

                if upto < 7:
                    return
                sync.wait_ge(dWbf[0], 16 * 38)
                sync.wait_ge(dWbf[1], 16 * 37)
                sync.wait_ge(dAgx[0], 16 * 16)
                sync.wait_ge(dAgx[1], 16 * 16)
                sync.dma_start(out=wres_gu,
                               in_=wg_d.rearrange("k p n -> p k n")).then_inc(dWres, 16)
                for tc in range(16):
                    r, hf = tc // 2, tc % 2
                    if tc >= 2:
                        sync.wait_ge(tH, NKT * (tc - 1))
                    for i in range(4):
                        if tc == 0:
                            sync.wait_ge(cAGX, i + 1)
                        sync.dma_start(
                            out=xf_sb[tc % 2][:, 8 * i:8 * (i + 1), :],
                            in_=agx_out[i][r, :, :, hf * 512:(hf + 1) * 512].rearrange(
                                "m p t -> p m t")).then_inc(dXf[tc % 2], 16)
                    sync.wait_ge(sSilu, NKT * (tc + 1))
                    sync.dma_start(
                        out=hg_d[:, :, tc * 512:(tc + 1) * 512].rearrange("n p t -> p n t"),
                        in_=hst_sb[tc % 2]).then_inc(dHg[tc % 2], 16)
                sync.wait_ge(tH, NKT * 16)
                sync.dma_start(out=wres_gu,
                               in_=wu_d.rearrange("k p n -> p k n")).then_inc(dWres, 16)
                for tc in range(16):
                    r, hf = tc // 2, tc % 2
                    if tc >= 2:
                        sync.wait_ge(tH2, NKT * (tc - 1))
                        sync.wait_ge(dH2[tc % 2], 16 * (tc // 2))
                    for i in range(4):
                        sync.dma_start(
                            out=xf_sb[tc % 2][:, 8 * i:8 * (i + 1), :],
                            in_=agx_out[i][r, :, :, hf * 512:(hf + 1) * 512].rearrange(
                                "m p t -> p m t")).then_inc(dXf[tc % 2], 16)
                    sync.dma_start(
                        out=hgr_sb[tc % 2],
                        in_=hg_d[:, :, tc * 512:(tc + 1) * 512].rearrange(
                            "n p t -> p n t")).then_inc(dHgr[tc % 2], 16)
                    sync.wait_ge(vH2, NKT * (tc + 1))
                    sync.dma_start(
                        out=h_d[:, :, tc * 512:(tc + 1) * 512].rearrange("n p t -> p n t"),
                        in_=hgr_sb[tc % 2]).then_inc(dH2[tc % 2], 16)
                if upto < 8:
                    return
                sync.wait_ge(tH2, NKT * 16)
                sync.wait_ge(dH2[0], 16 * 8)
                sync.wait_ge(dH2[1], 16 * 8)
                sync.dma_start(out=wres_d,
                               in_=wd_d.rearrange("k p n -> p k n")).then_inc(dWres, 16)
                i3 = 0
                for j in range(4):
                    for tc in range(16):
                        r, hf = tc // 2, tc % 2
                        if i3 >= 2:
                            sync.wait_ge(tY3, 8 * (i3 - 1))
                        sync.dma_start(
                            out=hgr_sb[i3 % 2],
                            in_=h_d[:, :, tc * 512:(tc + 1) * 512].rearrange(
                                "n p t -> p n t")).then_inc(dH3[i3 % 2], 16)
                        i3 += 1

                if "rs_dbg" in dbg:
                    sync.wait_ge(dRs4[0], 16 * 16)
                    sync.dma_start(out=rsi_dbg, in_=rs_in[0]).then_inc(dPre, 16)
                    sync.wait_ge(cRS, 1)
                    sync.dma_start(out=rso_dbg, in_=rs_out[0]).then_inc(dPre, 16)
                if upto < 9:
                    return
                for tt in range(NTT):
                    tsl = slice(tt * 128, (tt + 1) * 128)
                    if tt == 0:
                        sync.wait_ge(tY3, 8 * 64)
                    if tt >= 2:
                        sync.wait_ge(vF, tt - 1)
                    sync.dma_start(
                        out=s1f_sb[tt % 2],
                        in_=s1_d[:, :, tsl].rearrange("e p t -> p e t")).then_inc(dF1[tt % 2], 16)

            @block.gpsimd
            def _(gpsimd):
                gpsimd.dma_start(out=wqT_sb, in_=wqT).then_inc(dPre, 16)
                gpsimd.dma_start(out=wkT_sb, in_=wkT).then_inc(dPre, 16)
                gpsimd.dma_start(out=wvT_sb, in_=wvT).then_inc(dPre, 16)
                for c in range(SC if upto >= 1 else 0):
                    if c >= 2:
                        gpsimd.wait_ge(tK, c - 1)
                    gpsimd.dma_start(out=xt_sb[c % 2],
                                     in_=xh[:, :, c * 512:(c + 1) * 512]).then_inc(dX[c % 2], 16)
                if upto < 2:
                    return
                gpsimd.wait_ge(tK, SC)
                for c in range(SC):
                    if c >= 2:
                        gpsimd.wait_ge(tV, c - 1)
                    gpsimd.dma_start(out=xt_sb[c % 2],
                                     in_=xh[:, :, c * 512:(c + 1) * 512]).then_inc(dXv[c % 2], 16)
                if upto < 3:
                    return
                for ci in range(NCH):
                    h, qc = divmod(ci, QC)
                    gpsimd.wait_ge(vO, ci + 1)
                    gpsimd.dma_start(
                        out=a2a_in[h][qc // 2, :, (qc % 2) * 512:((qc % 2) + 1) * 512],
                        in_=ctxst[ci % 2]).then_inc(dCtx[ci % 2], 16)
                    if qc == QC - 1 and upto >= 4:
                        gpsimd.wait_ge(dCtx[0], 16 * 8 * (h + 1))
                        gpsimd.wait_ge(dCtx[1], 16 * 8 * (h + 1))
                        gpsimd.collective_compute(
                            "AllToAll", ALU.bypass, replica_groups=RG,
                            ins=[a2a_in[h].opt()],
                            outs=[a2a_out[h].opt()]).then_inc(cA2A, 1)
                if upto < 4:
                    return
                for ec in range(32):
                    if ec >= 2:
                        gpsimd.wait_ge(tY, ec - 1)
                    gpsimd.dma_start(out=woT_sb[ec % 2],
                                     in_=woT[:, :, ec * 128:(ec + 1) * 128]).then_inc(dWo[ec % 2], 16)
                if upto >= 6:
                    gpsimd.wait_ge(tY, 32)
                    gpsimd.dma_start(out=sc_sb[0:NG, 0:NSL], in_=g_sc).then_inc(dSc, 16)
                if upto < 5:
                    return
                for i in range(4):
                    gpsimd.wait_ge(dAgx[0], 16 * 4 * (i + 1))
                    gpsimd.wait_ge(dAgx[1], 16 * 4 * (i + 1))
                    gpsimd.collective_compute(
                        "AllGather", ALU.bypass, replica_groups=RG,
                        ins=[agx_in[8 * i:8 * (i + 1)].opt()],
                        outs=[agx_out[i].opt()]).then_inc(cAGX, 1)
                if upto < 6:
                    return
                gpsimd.wait_ge(tG, UMAT[1])
                gpsimd.dma_start(out=sc_sb[0:NG, 0:NSL], in_=u_sc).then_inc(dSc, 16)
                gpsimd.wait_ge(tG, UMAT[2])
                gpsimd.dma_start(out=sc_sb[0:NGD, 0:D], in_=d_sc).then_inc(dSc, 16)
                if upto < 8:
                    return
                for j in range(4):
                    gpsimd.wait_ge(dRs8[j][0], 16 * 8)
                    gpsimd.wait_ge(dRs8[j][1], 16 * 8)
                    gpsimd.collective_compute(
                        "ReduceScatter", ALU.add, replica_groups=RG,
                        ins=[rs_in[j].opt()], outs=[rs_out[j].opt()]).then_inc(cRS, 1)
                if upto < 9:
                    return
                for tt in range(NTT):
                    tsl = slice(tt * 128, (tt + 1) * 128)
                    if tt == 0:
                        gpsimd.wait_ge(cRS, 4)
                    if tt >= 2:
                        gpsimd.wait_ge(vF, tt - 1)
                    for j in range(4):
                        gpsimd.dma_start(
                            out=ysl_sb[tt % 2][:, j * 8:(j + 1) * 8, :],
                            in_=rs_out[j][:, tsl].rearrange(
                                "(ee p) t -> p ee t", p=128)).then_inc(dYs[tt % 2], 16)

            @block.tensor
            def _(tensor):
                tensor.wait_ge(dPre, 16 * NPRE)
                for c in range(SC if upto >= 1 else 0):
                    tensor.wait_ge(dX[c % 2], 16 * (c // 2 + 1))
                    if c >= 1:
                        tensor.wait_ge(vRope, 2 * c - 1)
                    for et in range(HPC):
                        for m in range(32):
                            last = tensor.matmul(ps_q[:, et * 512:(et + 1) * 512],
                                          wqT_sb[:, m, et * 128:(et + 1) * 128],
                                          xt_sb[c % 2][:, m, :],
                                          start=(m == 0), stop=(m == 31))
                    last.then_inc(tQ, 1)
                    if c >= 1:
                        tensor.wait_ge(vRope, 2 * c)
                    for et in range(HPC):
                        for m in range(32):
                            last = tensor.matmul(ps_k[:, et * 512:(et + 1) * 512],
                                          wkT_sb[:, m, et * 128:(et + 1) * 128],
                                          xt_sb[c % 2][:, m, :],
                                          start=(m == 0), stop=(m == 31))
                    last.then_inc(tK, 1)
                if upto < 2:
                    return
                tensor.wait_ge(vRope, 2 * SC)
                for c in range(SC):
                    tensor.wait_ge(dXv[c % 2], 16 * (c // 2 + 1))
                    tensor.wait_ge(dPre, 16 * (NPRE + 1))
                    if c >= 1:
                        tensor.wait_ge(sEv, c)
                    for st in range(4):
                        for m in range(32):
                            last = tensor.matmul(ps_q[:, st * 512:(st + 1) * 512],
                                          xt_sb[c % 2][:, m, st * 128:(st + 1) * 128],
                                          wvT_sb[:, m, :],
                                          start=(m == 0), stop=(m == 31))
                    last.then_inc(tV, 1)

                if upto < 3:
                    return
                def z_mm(ci_):
                    tensor.wait_ge(vZf, ci_ + 1)
                    tensor.wait_ge(sZb, ci_)
                    tensor.matmul(ps_zzb[0:1, :], ones_col_f_sb, accf,
                                  start=True, stop=True).then_inc(tZ, 1)

                def zb_mm(ci_):
                    tensor.wait_ge(vRec, ci_ + 1)
                    tensor.matmul(ps_zzb, ones_row_sb, recip_sb,
                                  start=True, stop=True).then_inc(tZb, 1)

                for h in range(HPC):
                    tensor.wait_ge(dKV[h % 2], 16 * 3 * (h // 2 + 1))
                    for qc in range(QC):
                        ci = h * QC + qc
                        qsl = slice(qc * 512, (qc + 1) * 512)
                        for g in range(32):
                            gg = 32 * ci + g
                            if ci >= 1 and g == 0:
                                z_mm(ci - 1)
                            if ci >= 1 and g == 2:
                                zb_mm(ci - 1)
                            tensor.wait_ge(sExp, max(0, gg - 2))
                            if ci == 0 and g == 0:
                                tensor.wait_ge(sEv, SC)
                            for jj in range(2):
                                kt = 2 * g + jj
                                last = tensor.matmul(ps_s[g % 3][:, jj * 512:(jj + 1) * 512],
                                              k_sb[h % 2][:, kt * 128:(kt + 1) * 128],
                                              q_sb[h % 2][:, qsl],
                                              start=True, stop=True)
                            last.then_inc(tS, 1)
                            if g >= 4:
                                tensor.wait_ge(sExp, gg - 3)
                                if g == 4:
                                    tensor.wait_ge(vO, ci)
                                for jj in range(2):
                                    kt = 2 * (g - 4) + jj
                                    last = tensor.matmul(ps_o, v_sb[h % 2][:, kt, :],
                                                  pring[:, (g - 4) % 8,
                                                        jj * 512:(jj + 1) * 512],
                                                  start=(kt == 0), stop=False)
                                last.then_inc(tPv, 1)
                        for gl in (28, 29, 30, 31):
                            gg = 32 * ci + gl
                            tensor.wait_ge(sExp, gg + 1)
                            for jj in range(2):
                                kt = 2 * gl + jj
                                last = tensor.matmul(ps_o, v_sb[h % 2][:, kt, :],
                                              pring[:, gl % 8, jj * 512:(jj + 1) * 512],
                                              start=False, stop=(kt == 63))
                            last.then_inc(tPv, 1)
                z_mm(NCH - 1)
                zb_mm(NCH - 1)

                if upto < 4:
                    return
                tensor.wait_ge(dCtxIn, 16 * HPC)
                for ec in range(32):
                    tensor.wait_ge(dWo[ec % 2], 16 * (ec // 2 + 1))
                    if ec < 2:
                        tensor.wait_ge(sZb, NCH)
                        tensor.wait_ge(vO, NCH)
                    else:
                        tensor.wait_ge(vS1, ec - 1)
                    for ts in range(2):
                        for m in range(32):
                            last = tensor.matmul(ps_y[ec % 2][:, ts * 512:(ts + 1) * 512],
                                          woT_sb[ec % 2][:, m, :],
                                          ctx_sb[:, m, ts * 512:(ts + 1) * 512],
                                          start=(m == 0), stop=(m == 31))
                    last.then_inc(tY, 1)
                    tensor.wait_ge(vS1, ec + 1)
                    for ts in range(2):
                        tensor.matmul(ps_mu[:, ts * 512:(ts + 1) * 512], ones_col_b_sb,
                                      s1b_sb[:, ts * 512:(ts + 1) * 512],
                                      start=(ec == 0), stop=(ec == 31))
                        last = tensor.matmul(ps_sq[:, ts * 512:(ts + 1) * 512], ones_col_b_sb,
                                      sqb_sb[:, ts * 512:(ts + 1) * 512],
                                      start=(ec == 0), stop=(ec == 31))
                    if ec == 31:
                        last.then_inc(tSt, 1)
                tensor.wait_ge(sRstd, 1)
                for ts in range(2):
                    tensor.matmul(ps_bc[:, ts * 512:(ts + 1) * 512], ones_row_sb,
                                  mu_sb[:, ts * 512:(ts + 1) * 512], start=True, stop=True)
                    last = tensor.matmul(ps_bc[:, 1024 + ts * 512:1024 + (ts + 1) * 512],
                                  ones_row_sb, rstd_sb[:, ts * 512:(ts + 1) * 512],
                                  start=True, stop=True)
                last.then_inc(tBc, 1)

                if upto < 6:
                    return
                tensor.wait_ge(vBc, 1)
                for (ktn, mat, kt, rows, nwid, ngr, klen, chk, wid) in DQU:
                    if kt == 0 and chk == 0:
                        tensor.wait_ge(vOh, mat + 1)
                        for ochk in range(OHC[mat]):
                            owid = min(512, klen - ochk * 512)
                            if OHOFF[mat] + ochk >= 1:
                                tensor.wait_ge(vOh2, OHOFF[mat] + ochk)
                            tensor.matmul(ps_ohb[0:ngr, 0:owid], ones_row_sb[:, 0:ngr],
                                          gi_f[:, ochk * 512:ochk * 512 + owid],
                                          start=True, stop=True).then_inc(tOh, 1)
                        tensor.wait_ge(vOh2, OHOFF[mat] + OHC[mat])
                        tensor.wait_ge(dSc, 16 * (mat + 1))
                    u = UOFF[ktn] + chk
                    if u >= 2:
                        tensor.wait_ge(vDq, u - 1)
                    pss = ps_dq[u % 2]
                    tensor.matmul(pss[0][0:rows, 0:wid],
                                  oh_sb[0:ngr, kt * 128:kt * 128 + rows],
                                  sc_sb[0:ngr, chk * 512:chk * 512 + wid],
                                  start=True, stop=True)
                    tensor.matmul(pss[1][0:rows, 0:wid],
                                  oh_sb[0:ngr, kt * 128:kt * 128 + rows],
                                  zp1_sb[0:ngr, chk * 512:chk * 512 + wid],
                                  start=True, stop=True).then_inc(tG, 1)

                if upto < 7:
                    return
                tensor.wait_ge(vDq, NDQU)
                tensor.wait_ge(dWres, 16)
                for tc in range(16):
                    for nt in range(NKT):
                        rows = 128 if nt < 10 else 96
                        it = NKT * tc + nt
                        if it >= 4:
                            tensor.wait_ge(sSilu, it - 3)
                        for kt in range(32):
                            if nt == 0 and kt % 8 == 0:
                                tensor.wait_ge(dXf[tc % 2],
                                               64 * (tc // 2) + 16 * (kt // 8 + 1))
                            last = tensor.matmul(ps_h[it % 4][0:rows, :],
                                          wres_gu[:, kt, nt * 128:nt * 128 + rows],
                                          xf_sb[tc % 2][:, kt, :],
                                          start=(kt == 0), stop=(kt == 31))
                        last.then_inc(tH, 1)
                tensor.wait_ge(dWres, 32)
                tensor.wait_ge(sSilu, NKT * 16)
                for tc in range(16):
                    tensor.wait_ge(dXf[tc % 2], 64 * (8 + tc // 2 + 1))
                    for nt in range(NKT):
                        rows = 128 if nt < 10 else 96
                        it = NKT * tc + nt
                        if it >= 4:
                            tensor.wait_ge(vH2, it - 3)
                        for kt in range(32):
                            last = tensor.matmul(ps_h[it % 4][0:rows, :],
                                          wres_gu[:, kt, nt * 128:nt * 128 + rows],
                                          xf_sb[tc % 2][:, kt, :],
                                          start=(kt == 0), stop=(kt == 31))
                        last.then_inc(tH2, 1)
                if upto < 8:
                    return
                tensor.wait_ge(dWres, 48)
                tensor.wait_ge(vH2, NKT * 16)
                i3 = 0
                for j in range(4):
                    for tc in range(16):
                        tensor.wait_ge(dH3[i3 % 2], 16 * (i3 // 2 + 1))
                        for ecc in range(8):
                            ie = 8 * i3 + ecc
                            if ie >= 2:
                                tensor.wait_ge(vE3, ie - 1)
                            esl = slice((j * 8 + ecc) * 128, (j * 8 + ecc + 1) * 128)
                            for nt in range(NKT):
                                rows = 128 if nt < 10 else 96
                                last = tensor.matmul(ps_y3[ecc % 2],
                                              wres_d[0:rows, nt, esl],
                                              hgr_sb[i3 % 2][0:rows, nt, :],
                                              start=(nt == 0), stop=(nt == NKT - 1))
                            last.then_inc(tY3, 1)
                        i3 += 1

                if upto < 9:
                    return
                for tt in range(NTT):
                    tensor.wait_ge(vF, tt + 1)
                    for ec in range(32):
                        ie = 32 * tt + ec
                        if ie >= 2:
                            tensor.wait_ge(sTe, ie - 1)
                        tensor.transpose(ps_tr[ec % 2], s2_sb[:, ec, :],
                                         ident_sb).then_inc(tTr, 1)

            @block.vector
            def _(vector):
                vector.wait_ge(dPre, 16 * NPRE)
                for c in range(SC if upto >= 1 else 0):
                    vector.wait_ge(dCS[c % 2], 32 * (c // 2 + 1))
                    if c >= 2:
                        vector.wait_ge(dSpQ[c % 2], 16 * (c // 2))
                        vector.wait_ge(dSpK[c % 2], 16 * (c // 2))
                    vector.wait_ge(tQ, c + 1)
                    for et in range(HPC):
                        psl = ps_q[:, et * 512:(et + 1) * 512]
                        vector.tensor_mul(uv_sb[:, 0:512], psl, cos_sb[c % 2])
                        vector.tensor_mul(psl, psl, sin_sb[c % 2])
                        vector.tensor_sub(qst_sb[c % 2][0:64, et, :],
                                          uv_sb[0:64, 0:512], psl[64:128, :])
                        last = vector.tensor_add(qst_sb[c % 2][64:128, et, :],
                                          uv_sb[64:128, 0:512], psl[0:64, :])
                    last.then_inc(vRope, 1)
                    vector.wait_ge(tK, c + 1)
                    for et in range(HPC):
                        psl = ps_k[:, et * 512:(et + 1) * 512]
                        vector.tensor_mul(uv_sb[:, 0:512], psl, cos_sb[c % 2])
                        vector.tensor_mul(psl, psl, sin_sb[c % 2])
                        vector.tensor_sub(kst_sb[c % 2][0:64, et, :],
                                          uv_sb[0:64, 0:512], psl[64:128, :])
                        last = vector.tensor_add(kst_sb[c % 2][64:128, et, :],
                                          uv_sb[64:128, 0:512], psl[0:64, :])
                    last.then_inc(vRope, 1)

                if upto < 3:
                    return
                def v_recip(ci_):
                    vector.wait_ge(tZ, ci_ + 1)
                    vector.reciprocal(recip_sb, ps_zzb[0:1, :]).then_inc(vRec, 1)

                def v_zbcopy(ci_):
                    vector.wait_ge(tZb, ci_ + 1)
                    vector.tensor_copy(zb_sb, ps_zzb).then_inc(sZb, 1)

                def v_oevac(ci_):
                    vector.wait_ge(tPv, 32 * (ci_ + 1))
                    if ci_ >= 2:
                        vector.wait_ge(dCtx[ci_ % 2], 16 * (ci_ // 2))
                    vector.tensor_mul(ctxst[ci_ % 2], ps_o, zb_sb).then_inc(vO, 1)

                for ci in range(NCH):
                    for g in range(32):
                        gg = 32 * ci + g
                        if ci >= 1 and g == 2:
                            v_recip(ci - 1)
                        if ci >= 1 and g == 3:
                            v_zbcopy(ci - 1)
                        if ci >= 1 and g == 4:
                            v_oevac(ci - 1)
                        vector.wait_ge(sExp, gg + 1)
                        pg = pring[:, g % 8, :]
                        if g % 8 == 0:
                            last = vector.tensor_copy(accb, pg)
                        else:
                            last = vector.tensor_add(accb, accb, pg)
                        if g % 8 == 7:
                            if g == 7:
                                last = vector.tensor_copy(acc32, accb)
                            else:
                                last = vector.tensor_add(acc32, acc32, accb)
                        last.then_inc(vZ, 1)
                    vector.wait_ge(tZ, ci)
                    vector.tensor_add(accf, acc32[:, 0:512], acc32[:, 512:1024]).then_inc(vZf, 1)
                v_recip(NCH - 1)
                v_zbcopy(NCH - 1)
                v_oevac(NCH - 1)

                if upto < 4:
                    return
                for ec in range(32):
                    vector.wait_ge(tY, ec + 1)
                    vector.wait_ge(dXo[ec % 2], 16 * (ec // 2 + 1))
                    if ec >= 2:
                        vector.wait_ge(dS1[ec % 2], 16 * (ec // 2))
                    vector.tensor_add(s1_st[ec % 2], ps_y[ec % 2], xo_sb[ec % 2])
                    vector.tensor_copy(s1b_sb, s1_st[ec % 2])
                    vector.tensor_mul(sqb_sb, s1b_sb, s1b_sb).then_inc(vS1, 1)
                vector.wait_ge(tSt, 1)
                vector.tensor_scalar(mu_sb, ps_mu, 1.0 / D, None, ALU.mult)
                vector.tensor_scalar(msq_sb, ps_sq, 1.0 / D,
                                     None, ALU.mult).then_inc(vCh, 1)
                vector.wait_ge(vCh, vch[0] + 1)
                vch[0] += 1
                vector.tensor_mul(var_sb, mu_sb, mu_sb).then_inc(vCh, 1)
                vector.wait_ge(vCh, vch[0] + 1)
                vch[0] += 1
                vector.tensor_sub(var2_sb, msq_sb, var_sb).then_inc(vSt, 1)
                vector.wait_ge(tBc, 1)
                vector.tensor_copy(mub_sb, ps_bc[:, 0:1024])
                vector.tensor_copy(rstdb_sb, ps_bc[:, 1024:2048]).then_inc(vBc, 1)
                def v_norm(ec):
                    vector.wait_ge(dS1r[ec % 2], 16 * (ec // 2 + 1))
                    if ec >= 2:
                        vector.wait_ge(dAgx[ec % 2], 16 * (ec // 2))
                    vector.tensor_sub(s1r_sb[ec % 2], s1r_sb[ec % 2], mub_sb)
                    vector.tensor_mul(xfb_st[ec % 2], s1r_sb[ec % 2], rstdb_sb).then_inc(vXf, 1)

                if upto < 6:
                    for ec in range(32):
                        v_norm(ec)
                    return
                nrm = [0]
                for (ktn, mat, kt, rows, nwid, ngr, klen, chk, wid) in DQU:
                    u = UOFF[ktn] + chk
                    if u % 3 == 0 and nrm[0] < 32:
                        v_norm(nrm[0])
                        nrm[0] += 1
                    pss = ps_dq[u % 2]
                    if kt == 0 and chk == 0:
                        npk = NSLP if mat < 2 else D // 8
                        vector.wait_ge(dDq, 16 * 2 * (mat + 1))
                        vector.wait_ge(dSc, 16 * (mat + 1))
                        if mat >= 1:
                            vector.wait_ge(tOh, OHOFF[mat])
                            vector.wait_ge(tG, UMAT[mat])
                        vector.tensor_copy(gi_f[:, 0:klen], gi_i[:, 0:klen])
                        for jjj in range(8):
                            vector.tensor_scalar(z_sb[0:ngr, jjj:nwid:8],
                                                 qz_sb[0:ngr, 0:npk],
                                                 4 * jjj, 15,
                                                 ALU.logical_shift_right, ALU.bitwise_and)
                        vector.tensor_copy(w4f_sb[0:ngr, 0:nwid], z_sb[0:ngr, 0:nwid])
                        vector.tensor_scalar(zp1_sb[0:ngr, 0:nwid], w4f_sb[0:ngr, 0:nwid],
                                             1.0, None, ALU.add).then_inc(vOh, 1)
                        for ochk in range(OHC[mat]):
                            owid = min(512, klen - ochk * 512)
                            vector.wait_ge(tOh, OHOFF[mat] + ochk + 1)
                            vector.tensor_scalar(
                                oh_sb[0:ngr, ochk * 512:ochk * 512 + owid],
                                ps_ohb[0:ngr, 0:owid],
                                iota32_sb[0:ngr, 0:1] if mat < 2 else iota86_sb[0:ngr, 0:1],
                                None, ALU.is_equal).then_inc(vOh2, 1)
                    if chk == 0:
                        vector.wait_ge(dQw[ktn % 2], 16 * (ktn // 2 + 1))
                        if ktn >= 2:
                            vector.wait_ge(dWbf[ktn % 2], 16 * (ktn // 2))
                        vector.tensor_scalar(qw_sb[ktn % 2][0:rows, 0:nwid],
                                             qw_sb[ktn % 2][0:rows, 0:nwid],
                                             shifts_sb[0:rows, 0:1], 15,
                                             ALU.logical_shift_right, ALU.bitwise_and)
                        vector.tensor_copy(w4f_sb[0:rows, 0:nwid],
                                           qw_sb[ktn % 2][0:rows, 0:nwid])
                    vector.wait_ge(tG, u + 1)
                    csl = slice(chk * 512, chk * 512 + wid)
                    vector.tensor_sub(tdq_sb[0:rows, 0:wid],
                                      w4f_sb[0:rows, csl], pss[1][0:rows, 0:wid])
                    vector.tensor_mul(wst_sb[ktn % 2][0:rows, csl],
                                      tdq_sb[0:rows, 0:wid],
                                      pss[0][0:rows, 0:wid]).then_inc(vDq, 1)

                while nrm[0] < 32:
                    v_norm(nrm[0])
                    nrm[0] += 1

                if upto < 7:
                    return
                for tc in range(16):
                    vector.wait_ge(dHgr[tc % 2], 16 * (tc // 2 + 1))
                    for nt in range(NKT):
                        rows = 128 if nt < 10 else 96
                        it = NKT * tc + nt
                        vector.wait_ge(tH2, it + 1)
                        vector.tensor_mul(hgr_sb[tc % 2][0:rows, nt, :],
                                          ps_h[it % 4][0:rows, :],
                                          hgr_sb[tc % 2][0:rows, nt, :]).then_inc(vH2, 1)
                if upto < 8:
                    return
                i3 = 0
                for j in range(4):
                    for tc in range(16):
                        if i3 >= 2:
                            pj, pt = divmod(i3 - 2, 16)
                            vector.wait_ge(dRs8[pj][pt % 2], 16 * (pt // 2 + 1))
                        for ecc in range(8):
                            ie = 8 * i3 + ecc
                            vector.wait_ge(tY3, ie + 1)
                            vector.tensor_copy(yst_sb[i3 % 2][:, ecc, :],
                                               ps_y3[ecc % 2]).then_inc(vE3, 1)
                        i3 += 1

                if upto < 9:
                    return
                for tt in range(NTT):
                    vector.wait_ge(dF1[tt % 2], 256 + 16 * (tt // 2 + 1))
                    vector.wait_ge(dYs[tt % 2], 96 + 64 * (tt // 2 + 1))
                    tsl = slice(tt * 128, (tt + 1) * 128)
                    bm = mub_sb[:, tsl].unsqueeze(1).broadcast_to([128, 32, 128])
                    br = rstdb_sb[:, tsl].unsqueeze(1).broadcast_to([128, 32, 128])
                    vector.tensor_sub(s2_sb, s1f_sb[tt % 2], bm).then_inc(vCh, 1)
                    vector.wait_ge(vCh, vch[0] + 1)
                    vch[0] += 1
                    vector.tensor_mul(s2_sb, s2_sb, br).then_inc(vCh, 1)
                    vector.wait_ge(vCh, vch[0] + 1)
                    vch[0] += 1
                    vector.tensor_add(s2_sb, s2_sb, ysl_sb[tt % 2]).then_inc(vF, 1)
                    vector.wait_ge(sTe, 32 * (tt + 1))
                    vector.tensor_mul(t3_sb, tmaj_sb[tt % 2], tmaj_sb[tt % 2])
                    vector.reduce_sum(red_sb[:, 0:1], tmaj_sb[tt % 2],
                                      mybir.AxisListType.X)
                    vector.reduce_sum(red_sb[:, 1:2], t3_sb,
                                      mybir.AxisListType.X).then_inc(vCh, 1)
                    vector.wait_ge(vCh, vch[0] + 1)
                    vch[0] += 1
                    vector.tensor_scalar(red_sb[:, 2:3], red_sb[:, 0:1],
                                         -1.0 / D, None, ALU.mult)
                    vector.tensor_scalar(red_sb[:, 4:5], red_sb[:, 1:2],
                                         1.0 / D, None, ALU.mult).then_inc(vCh, 1)
                    vector.wait_ge(vCh, vch[0] + 1)
                    vch[0] += 1
                    vector.tensor_mul(red_sb[:, 3:4], red_sb[:, 2:3],
                                      red_sb[:, 2:3]).then_inc(vCh, 1)
                    vector.wait_ge(vCh, vch[0] + 1)
                    vch[0] += 1
                    vector.tensor_sub(red_sb[:, 5:6], red_sb[:, 4:5],
                                      red_sb[:, 3:4]).then_inc(vLn, 1)
                    vector.wait_ge(sLn, tt + 1)
                    if tt >= 2:
                        vector.wait_ge(dOut[tt % 2], 256 + 16 * (tt // 2))
                    vector.tensor_scalar(out_st[tt % 2], tmaj_sb[tt % 2],
                                         red_sb[:, 2:3], red_sb[:, 3:4],
                                         ALU.add, ALU.mult).then_inc(vLn2, 1)

            @block.scalar
            def _(scalar):
                scalar.wait_ge(dPre, 16 * NPRE)
                for c in range(SC if upto >= 2 else 0):
                    scalar.wait_ge(tV, c + 1)
                    if c >= 2:
                        scalar.wait_ge(dSpV[c % 2], 64 * (c // 2))
                    scalar.copy(vst_sb[c % 2], ps_q).then_inc(sEv, 1)

                if upto < 3:
                    return
                for ci in range(NCH):
                    for g in range(32):
                        gg = 32 * ci + g
                        scalar.wait_ge(tS, gg + 1)
                        if gg >= 8:
                            scalar.wait_ge(tPv, gg - 7)
                            scalar.wait_ge(vZ, gg - 7)
                        scalar.activation(pring[:, g % 8, :], ps_s[g % 3], AF.Exp,
                                          bias=cv[:, C_EXPB:C_EXPB + 1],
                                          scale=cv[:, C_QKSC:C_QKSC + 1]).then_inc(sExp, 1)

                if upto < 4:
                    return
                scalar.wait_ge(vSt, 1)
                scalar.activation(var_sb, var2_sb, AF.Ln,
                                  bias=cv[0:1, C_EPS:C_EPS + 1]).then_inc(sCh, 1)
                scalar.wait_ge(sCh, sch[0] + 1)
                sch[0] += 1
                scalar.activation(rstd_sb, var_sb, AF.Exp,
                                  scale=cv[0:1, C_NHALF:C_NHALF + 1]).then_inc(sRstd, 1)

                if upto < 7:
                    return
                for tc in range(16):
                    for nt in range(NKT):
                        rows = 128 if nt < 10 else 96
                        it = NKT * tc + nt
                        scalar.wait_ge(tH, it + 1)
                        if tc >= 2 and nt == 0:
                            scalar.wait_ge(dHg[tc % 2], 16 * (tc // 2))
                        scalar.activation(hst_sb[tc % 2][0:rows, nt, :],
                                          ps_h[it % 4][0:rows, :],
                                          AF.Silu).then_inc(sSilu, 1)

                if upto < 8:
                    return
                i3s = 0
                for j in range(4):
                    for tc in range(16):
                        r, hf = tc // 2, tc % 2
                        scalar.wait_ge(vE3, 8 * (i3s + 1))
                        scalar.dma_start(
                            out=rs_in[j][r, :, hf * 512:(hf + 1) * 512].rearrange(
                                "(ee p) t -> p ee t", p=128),
                            in_=yst_sb[i3s % 2]).then_inc(dRs8[j][tc % 2], 16)
                        i3s += 1

                if upto < 9:
                    return
                for tt in range(NTT):
                    for ec in range(32):
                        scalar.wait_ge(tTr, 32 * tt + ec + 1)
                        scalar.copy(tmaj_sb[tt % 2][:, ec * 128:(ec + 1) * 128],
                                    ps_tr[ec % 2]).then_inc(sTe, 1)
                    scalar.wait_ge(vLn, tt + 1)
                    scalar.activation(red_sb[:, 4:5], red_sb[:, 5:6], AF.Ln,
                                      bias=cv[:, C_EPS:C_EPS + 1]).then_inc(sCh, 1)
                    scalar.wait_ge(sCh, sch[0] + 1)
                    sch[0] += 1
                    scalar.activation(red_sb[:, 3:4], red_sb[:, 4:5], AF.Exp,
                                      scale=cv[:, C_NHALF:C_NHALF + 1]).then_inc(sLn, 1)
                    scalar.wait_ge(vLn2, tt + 1)
                    scalar.dma_start(out=y_out[tt * 128:(tt + 1) * 128, :],
                                     in_=out_st[tt % 2]).then_inc(dOut[tt % 2], 16)
                scalar.wait_ge(dOut[0], 256 + 16 * NTT // 2)
                scalar.wait_ge(dOut[1], 256 + 16 * NTT // 2)

    return nc


def _prep_inputs(inputs):
    bf = ml_dtypes.bfloat16
    x = np.asarray(inputs["L0_attn_x"])
    Wq = np.asarray(inputs["L0_attn_Wq"])
    Wk = np.asarray(inputs["L0_attn_Wk"])
    Wv = np.asarray(inputs["L0_attn_Wv"])
    Wo = np.asarray(inputs["L0_attn_Wo"])

    x0 = np.ascontiguousarray(x.reshape(S, D))
    xh = np.ascontiguousarray(x0.T.reshape(32, 128, S).transpose(1, 0, 2))

    def tiled_T(Wm):
        return np.ascontiguousarray(Wm.T.reshape(32, 128, D).transpose(1, 0, 2))

    wqT, wkT, wvT, woT = map(tiled_T, (Wq, Wk, Wv, Wo))

    inv = 1.0 / (10000.0 ** (np.arange(0, HD, 2, dtype=np.float64) / HD))
    fr = inv[:, None] * np.arange(S, dtype=np.float64)[None, :]
    cos1 = np.cos(fr).astype(np.float32)
    sin1 = np.sin(fr).astype(np.float32)
    cos2 = np.ascontiguousarray(np.concatenate([cos1, cos1], axis=0))
    sin2 = np.ascontiguousarray(np.concatenate([sin1, sin1], axis=0))

    cvec = np.zeros((128, NCONST), dtype=np.float32)
    cvec[:, C_EXPB] = EXP_BIAS
    cvec[:, C_QKSC] = QK_SCALE
    cvec[:, C_EPS] = EPS
    cvec[:, C_NHALF] = -0.5

    shared = {
        "xh": xh, "cos2": cos2, "sin2": sin2,
        "ident": np.eye(128, dtype=np.float32),
        "ones_row": np.ones((1, 128), dtype=np.float32),
        "ones_col_f": np.ones((128, 1), dtype=np.float32),
        "ones_col_b": np.ones((128, 1), dtype=bf),
        "iota32": np.arange(32, dtype=np.float32).reshape(32, 1),
        "iota86": np.arange(86, dtype=np.float32).reshape(86, 1),
        "shifts": (4 * (np.arange(128) % 8)).astype(np.int32).reshape(128, 1),
        "cvec": cvec, "woT": woT,
        "g_gi": np.ascontiguousarray(np.asarray(inputs["L0_ffn_gate_g_idx"]).reshape(1, D)),
        "u_gi": np.ascontiguousarray(np.asarray(inputs["L0_ffn_up_g_idx"]).reshape(1, D)),
        "d_sc": np.ascontiguousarray(np.asarray(inputs["L0_ffn_down_scales"])),
        "d_qz": np.ascontiguousarray(np.asarray(inputs["L0_ffn_down_qzeros"])),
    }
    g_qw_full = np.asarray(inputs["L0_ffn_gate_qweight"])
    u_qw_full = np.asarray(inputs["L0_ffn_up_qweight"])
    d_qw_full = np.asarray(inputs["L0_ffn_down_qweight"])
    g_sc_full = np.asarray(inputs["L0_ffn_gate_scales"])
    u_sc_full = np.asarray(inputs["L0_ffn_up_scales"])
    g_qz_full = np.asarray(inputs["L0_ffn_gate_qzeros"])
    u_qz_full = np.asarray(inputs["L0_ffn_up_qzeros"])
    d_gi_full = np.asarray(inputs["L0_ffn_down_g_idx"])

    in_maps = []
    for c in range(W):
        esl = slice(EOWN * c, EOWN * (c + 1))
        nsl = slice(NSL * c, NSL * (c + 1))
        nslp = slice(NSLP * c, NSLP * (c + 1))
        m = dict(shared)
        m["x_own"] = np.ascontiguousarray(xh[:, :, TOK * c:TOK * (c + 1)])
        m["wqT"] = np.ascontiguousarray(wqT[:, :, esl])
        m["wkT"] = np.ascontiguousarray(wkT[:, :, esl])
        m["wvT"] = np.ascontiguousarray(wvT[:, :, esl])
        m["g_qw"] = np.ascontiguousarray(np.repeat(g_qw_full[:, nsl], 8, axis=0))
        m["u_qw"] = np.ascontiguousarray(np.repeat(u_qw_full[:, nsl], 8, axis=0))
        m["d_qw"] = np.ascontiguousarray(
            np.repeat(d_qw_full[NSLP * c:NSLP * (c + 1), :], 8, axis=0))
        m["g_sc"] = np.ascontiguousarray(g_sc_full[:, nsl])
        m["u_sc"] = np.ascontiguousarray(u_sc_full[:, nsl])
        m["g_qz"] = np.ascontiguousarray(g_qz_full[:, nslp])
        m["u_qz"] = np.ascontiguousarray(u_qz_full[:, nslp])
        m["d_gi"] = np.ascontiguousarray(d_gi_full[nsl].reshape(1, NSL))
        in_maps.append(m)
    return in_maps


def kernel(**inputs):
    in_maps = _prep_inputs(inputs)
    nc = build_nc()
    res = run_bass_kernel_spmd(nc, in_maps, core_ids=list(range(W)))
    out = np.concatenate([res.results[c]["y"] for c in range(W)], axis=0)
    return out.reshape(1, S, D).astype(np.float32)


# version 08
# speedup vs baseline: 1.2877x; 1.0308x over previous
import os
import contextlib
import numpy as np
import ml_dtypes

os.environ.setdefault("TRN_ALLOW_ICI_ALL_TO_ALL", "1")

import concourse.bass as bass
import concourse.mybir as mybir
from concourse.bass_utils import run_bass_kernel_spmd

F32 = mybir.dt.float32
BF16 = mybir.dt.bfloat16
I32 = mybir.dt.int32
AF = mybir.ActivationFunctionType
ALU = mybir.AluOpType

W = 8
S = 8192
D = 4096
HD = 128
HPC = 4
EOWN = HPC * HD
TOK = S // W
INTER = 11008
NSL = INTER // W
NSLP = NSL // 8
NG = 32
NGD = 86
NKT = 11
SC = 16
QC = 16
NCH = HPC * QC
NTT = TOK // 128

EXP_BIAS = -12.0
QK_SCALE = 1.0 / float(np.sqrt(HD))
EPS = 1e-5

C_EXPB, C_QKSC, C_EPS, C_NHALF = range(4)
NCONST = 4
RG = [list(range(W))]

DQT = []
for _mat in range(3):
    _nkt = 32 if _mat < 2 else NKT
    for _kt in range(_nkt):
        _rows = 96 if (_mat == 2 and _kt == NKT - 1) else 128
        DQT.append((_mat, _kt, _rows,
                    NSL if _mat < 2 else D,
                    NG if _mat < 2 else NGD,
                    D if _mat < 2 else NSL))
NDQ = len(DQT)
OHC = [(D + 511) // 512, (D + 511) // 512, (NSL + 511) // 512]
OHOFF = [0, 8, 16]
DQOFF = [0, 32, 64]
DQU = []
UOFF = []
for _ktn, (_mat, _kt, _rows, _nwid, _ngr, _klen) in enumerate(DQT):
    UOFF.append(len(DQU))
    for _chk in range(-(-_nwid // 512)):
        _wid = min(512, _nwid - _chk * 512)
        DQU.append((_ktn, _mat, _kt, _rows, _nwid, _ngr, _klen, _chk, _wid))
NDQU = len(DQU)
UOFF.append(NDQU)
UMAT = [0, UOFF[32], UOFF[64]]


def _ceil(a, b):
    return -(-a // b)


def build_nc(upto=9, dbg=()):
    nc = bass.Bass("TRN2", target_bir_lowering=False, num_devices=W)

    di = lambda n, s, d: nc.dram_tensor(n, s, d, kind="ExternalInput").ap()
    xh = di("xh", [128, 32, S], F32)
    x_own = di("x_own", [128, 32, TOK], F32)
    wqT = di("wqT", [128, 32, EOWN], F32)
    wkT = di("wkT", [128, 32, EOWN], F32)
    wvT = di("wvT", [128, 32, EOWN], F32)
    woT = di("woT", [128, 32, D], F32)
    cos2 = di("cos2", [128, S], F32)
    sin2 = di("sin2", [128, S], F32)
    ident = di("ident", [128, 128], F32)
    ones_row = di("ones_row", [1, 128], F32)
    ones_col_f = di("ones_col_f", [128, 1], F32)
    ones_col_b = di("ones_col_b", [128, 1], BF16)
    iota32 = di("iota32", [32, 1], F32)
    iota86 = di("iota86", [86, 1], F32)
    shifts = di("shifts", [128, 1], I32)
    cvec = di("cvec", [128, NCONST], F32)
    g_qw = di("g_qw", [D, NSL], I32)
    u_qw = di("u_qw", [D, NSL], I32)
    d_qw = di("d_qw", [NSL, D], I32)
    g_sc = di("g_sc", [NG, NSL], F32)
    u_sc = di("u_sc", [NG, NSL], F32)
    d_sc = di("d_sc", [NGD, D], F32)
    g_qz = di("g_qz", [NG, NSLP], I32)
    u_qz = di("u_qz", [NG, NSLP], I32)
    d_qz = di("d_qz", [NGD, D // 8], I32)
    g_gi = di("g_gi", [1, D], I32)
    u_gi = di("u_gi", [1, D], I32)
    d_gi = di("d_gi", [1, NSL], I32)
    y_out = nc.dram_tensor("y", [TOK, D], F32, kind="ExternalOutput").ap()

    dt = lambda n, s, d: nc.dram_tensor(
        n, s, d, kind=("ExternalOutput" if n in dbg else "Internal")).ap()
    q_d = dt("q_d", [HPC, 128, S], BF16)
    k_d = dt("k_d", [HPC, 128, S], BF16)
    v_d = dt("v_d", [HPC, 128, S // 128, HD], BF16)
    a2a_in = [dt(f"a2a_in{h}", [W, 128, TOK], BF16) for h in range(HPC)]
    a2a_out = [dt(f"a2a_out{h}", [W, 128, TOK], BF16) for h in range(HPC)]
    s1_d = dt("s1_d", [32, 128, TOK], F32)
    agx_in = dt("agx_in", [32, 128, TOK], BF16)
    agx_out = [nc.dram_tensor(f"agx_out{i}", [W, 8, 128, TOK], BF16,
                              kind="Internal", addr_space="Shared").ap()
               for i in range(4)]
    wg_d = dt("wg_d", [32, 128, NSL], BF16)
    wu_d = dt("wu_d", [32, 128, NSL], BF16)
    wd_d = dt("wd_d", [NKT, 128, D], BF16)
    hg_d = dt("hg_d", [NKT, 128, S], BF16)
    h_d = dt("h_d", [NKT, 128, S], BF16)
    rs_in = [dt(f"rs_in{j}", [W, 1024, TOK], BF16) for j in range(4)]
    rs_out = [dt(f"rs_out{j}", [1024, TOK], BF16) for j in range(4)]
    if "rs_dbg" in dbg:
        rsi_dbg = nc.dram_tensor("rsi_dbg", [W, 1024, TOK], BF16, kind="ExternalOutput").ap()
        rso_dbg = nc.dram_tensor("rso_dbg", [1024, TOK], BF16, kind="ExternalOutput").ap()
    WMAT = [wg_d, wu_d, wd_d]

    es = contextlib.ExitStack()
    mk = lambda n: es.enter_context(nc.semaphore(n))
    mk2 = lambda n: [mk(n + "0"), mk(n + "1")]
    (dPre, tQ, tK, vRope, tV, sEv, tS, sExp, vZ, tPv,
     vZf, tZ, vRec, tZb, sZb, vO, cA2A, dCtxIn, tY, vS1,
     tSt, vSt, sRstd, tBc, vBc, vXf, cAGX, dDq, dSc, tOh, vOh,
     vOh2, tG, vDq, dWres, tH, sSilu, tH2, vH2,
     tY3, vE3, cRS, vF, tTr, sTe, vLn, sLn, vLn2) = [mk(f"s{i}") for i in range(48)]
    dX = mk2("dX")
    dXv = mk2("dXv")
    dCS = mk2("dCS")
    dSpQ = mk2("dSpQ")
    dSpK = mk2("dSpK")
    dSpV = mk2("dSpV")
    dKV = mk2("dKV")
    dCtx = mk2("dCtx")
    dWo = mk2("dWo")
    dXo = mk2("dXo")
    dS1 = mk2("dS1")
    dS1r = mk2("dS1r")
    dAgx = mk2("dAgx")
    dQw = mk2("dQw")
    dWbf = mk2("dWbf")
    dXf = mk2("dXf")
    dHg = mk2("dHg")
    dHgr = mk2("dHgr")
    dH2 = mk2("dH2")
    dH3 = mk2("dH3")
    dRs8 = [[mk(f"dRs{j}_{p}") for p in range(2)] for j in range(4)]
    dF1 = dCS
    dYs = dKV
    dOut = dWo
    vCh = mk("vCh")
    sCh = mk("sCh")

    AL = nc.alloc_sbuf_tensor
    ident_sb = AL("ident_sb", [128, 128], F32).ap()
    ones_row_sb = AL("ones_row_sb", [1, 128], F32).ap()
    ones_col_f_sb = AL("ones_col_f_sb", [128, 1], F32).ap()
    ones_col_b_sb = AL("ones_col_b_sb", [128, 1], BF16).ap()
    iota32_sb = AL("iota32_sb", [32, 1], F32).ap()
    iota86_sb = AL("iota86_sb", [86, 1], F32).ap()
    shifts_sb = AL("shifts_sb", [128, 1], I32).ap()
    cv = AL("cv_sb", [128, NCONST], F32).ap()
    mub_sb = AL("mub_sb", [128, TOK], F32).ap()
    rstdb_sb = AL("rstdb_sb", [128, TOK], F32).ap()
    red_sb = AL("red_sb", [128, 6], F32).ap()

    ABASE = nc.sbuf_base
    ATOP = nc.sbuf_top

    class Ph:
        def __init__(self):
            self.off = ABASE

        def t(self, name, shape, dtype):
            per = int(np.prod(shape[1:])) * np.dtype(mybir.dt.np(dtype)).itemsize
            off = (self.off + 31) & ~31
            h = nc.alloc_sbuf_tensor_at(name, shape, dtype, offset=off)
            self.off = off + per
            assert self.off <= ATOP, f"{name}: arena overflow {self.off} > {ATOP}"
            return h.ap()

    ph = Ph()
    wqT_sb = ph.t("wqT_sb", [128, 32, EOWN], BF16)
    wkT_sb = ph.t("wkT_sb", [128, 32, EOWN], BF16)
    xt_sb = [ph.t(f"xt{i}", [128, 32, 512], BF16) for i in range(2)]
    cos_sb = [ph.t(f"cos{i}", [128, 512], F32) for i in range(2)]
    sin_sb = [ph.t(f"sin{i}", [128, 512], F32) for i in range(2)]
    uv_sb = ph.t("uv_sb", [128, 1024], F32)
    qst_sb = [ph.t(f"qst{i}", [128, HPC, 512], BF16) for i in range(2)]
    kst_sb = [ph.t(f"kst{i}", [128, HPC, 512], BF16) for i in range(2)]
    wvT_sb = ph.t("wvT_sb", [128, 32, EOWN], BF16)
    vst_sb = [ph.t(f"vst{i}", [128, 4 * EOWN], BF16) for i in range(2)]

    pa = Ph()
    k_sb = [pa.t(f"k_sb{i}", [128, S], BF16) for i in range(2)]
    v_sb = [pa.t(f"v_sb{i}", [128, S // 128, HD], BF16) for i in range(2)]
    q_sb = [pa.t(f"q_sb{i}", [128, S], BF16) for i in range(2)]
    pring = pa.t("pring", [128, 8, 1024], BF16)
    accb = pa.t("accb", [128, 1024], BF16)
    acc32 = pa.t("acc32", [128, 1024], F32)
    accf = pa.t("accf", [128, 512], F32)
    recip_sb = pa.t("recip_sb", [1, 512], F32)
    zb_sb = pa.t("zb_sb", [128, 512], F32)
    ctxst = [pa.t(f"ctxst{i}", [128, 512], BF16) for i in range(2)]

    po = Ph()
    s1r_sb = [po.t(f"s1r{i}", [128, TOK], F32) for i in range(2)]
    xfb_st = [po.t(f"xfb{i}", [128, TOK], BF16) for i in range(2)]
    norm_top = po.off
    ctx_sb = po.t("ctx_sb", [128, 32, TOK], BF16)
    woT_sb = [po.t(f"woT{i}", [128, 32, 128], BF16) for i in range(2)]
    xo_sb = [po.t(f"xo{i}", [128, TOK], F32) for i in range(2)]
    s1_st = [po.t(f"s1st{i}", [128, TOK], F32) for i in range(2)]
    s1b_sb = po.t("s1b_sb", [128, TOK], BF16)
    sqb_sb = po.t("sqb_sb", [128, TOK], BF16)
    mu_sb = po.t("mu_sb", [1, TOK], F32)
    msq_sb = po.t("msq_sb", [1, TOK], F32)
    var_sb = po.t("var_sb", [1, TOK], F32)
    var2_sb = po.t("var2_sb", [1, TOK], F32)
    rstd_sb = po.t("rstd_sb", [1, TOK], F32)

    pd = Ph()
    pd.off = norm_top
    gi_i = pd.t("gi_i", [1, D], I32)
    gi_f = pd.t("gi_f", [1, D], F32)
    oh_sb = pd.t("oh_sb", [86, D], BF16)
    z_sb = pd.t("z_sb", [86, D], I32)
    zp1_sb = pd.t("zp1_sb", [86, D], BF16)
    sc_sb = pd.t("sc_sb", [86, D], BF16)
    qz_sb = pd.t("qz_sb", [86, 512], I32)
    qw_sb = [pd.t(f"qw{i}", [128, D], I32) for i in range(2)]
    w4f_sb = pd.t("w4f_sb", [128, D], F32)
    tdq_sb = pd.t("tdq_sb", [128, 512], F32)
    wst_sb = [pd.t(f"wst{i}", [128, D], BF16) for i in range(2)]

    pf = Ph()
    wres_raw = pf.t("wres_sb", [128, NKT * D], BF16)
    wres_gu = wres_raw[:, 0:32 * NSL].rearrange("p (k n) -> p k n", n=NSL)
    wres_d = wres_raw.rearrange("p (k n) -> p k n", n=D)
    xf_off = pf.off
    xf_sb = [pf.t(f"xf{i}", [128, 32, 512], BF16) for i in range(2)]
    hst_sb = [pf.t(f"hst{i}", [128, NKT, 512], BF16) for i in range(2)]
    hgr_sb = [pf.t(f"hgr{i}", [128, NKT, 512], BF16) for i in range(2)]
    yst_sb = [nc.alloc_sbuf_tensor_at(
        f"yst{i}", [128, 8, 512], BF16,
        offset=((xf_off + 31) & ~31) + i * 8 * 512 * 2).ap() for i in range(2)]

    pz = Ph()
    s1f_sb = [pz.t(f"s1f{i}", [128, 32, 128], F32) for i in range(2)]
    ysl_sb = [pz.t(f"ysl{i}", [128, 32, 128], F32) for i in range(2)]
    s2_sb = pz.t("s2_sb", [128, 32, 128], F32)
    tmaj_sb = [pz.t(f"tmaj{i}", [128, D], F32) for i in range(2)]
    t3_sb = pz.t("t3_sb", [128, D], F32)
    out_st = [pz.t(f"outst{i}", [128, D], F32) for i in range(2)]

    ps_a = nc.alloc_psum_tensor("ps_a", [128, 2048], F32).ap()
    ps_b = nc.alloc_psum_tensor("ps_b", [128, 2048], F32).ap()
    ps_s = [ps_a[:, 0:1024], ps_a[:, 1024:2048], ps_b[:, 0:1024]]
    ps_o = ps_b[:, 1024:1536]
    ps_zzb = ps_b[:, 1536:2048]
    ps_q = ps_a
    ps_k = ps_b
    ps_y = [ps_a[:, 0:1024], ps_a[:, 1024:2048]]
    ps_mu = ps_b[0:1, 0:1024]
    ps_sq = ps_b[0:1, 1024:2048]
    ps_bc = ps_a
    ps_dq = [[ps_a[:, 0:512], ps_a[:, 512:1024]],
             [ps_a[:, 1024:1536], ps_a[:, 1536:2048]]]
    ps_ohb = ps_b[0:86, 0:512]
    ps_h = [ps_a[:, 0:512], ps_a[:, 512:1024], ps_a[:, 1024:1536], ps_a[:, 1536:2048]]
    ps_y3 = [ps_b[:, 0:512], ps_b[:, 512:1024]]
    ps_tr = [ps_b[:, 0:128], ps_b[:, 512:640]]

    NPRE = 10
    vch = [0]
    sch = [0]

    with es:
        with nc.Block() as block:

            @block.sync
            def _(sync):
                sync.dma_start(out=ident_sb, in_=ident).then_inc(dPre, 16)
                sync.dma_start(out=ones_row_sb, in_=ones_row).then_inc(dPre, 16)
                sync.dma_start(out=ones_col_f_sb, in_=ones_col_f).then_inc(dPre, 16)
                sync.dma_start(out=ones_col_b_sb, in_=ones_col_b).then_inc(dPre, 16)
                sync.dma_start(out=iota32_sb, in_=iota32).then_inc(dPre, 16)
                sync.dma_start(out=iota86_sb, in_=iota86).then_inc(dPre, 16)
                sync.dma_start(out=shifts_sb, in_=shifts).then_inc(dPre, 16)
                sync.dma_start(out=cv, in_=cvec).then_inc(dPre, 16)

                for c in range(SC if upto >= 1 else 0):
                    sl = slice(c * 512, (c + 1) * 512)
                    if c >= 2:
                        sync.wait_ge(vRope, 2 * (c - 1))
                    sync.dma_start(out=cos_sb[c % 2], in_=cos2[:, sl]).then_inc(dCS[c % 2], 16)
                    sync.dma_start(out=sin_sb[c % 2], in_=sin2[:, sl]).then_inc(dCS[c % 2], 16)
                    sync.wait_ge(vRope, 2 * c + 1)
                    sync.dma_start(out=q_d[:, :, sl].rearrange("h p s -> p h s"),
                                   in_=qst_sb[c % 2]).then_inc(dSpQ[c % 2], 16)
                    sync.wait_ge(vRope, 2 * c + 2)
                    sync.dma_start(out=k_d[:, :, sl].rearrange("h p s -> p h s"),
                                   in_=kst_sb[c % 2]).then_inc(dSpK[c % 2], 16)
                for c in range(SC if upto >= 2 else 0):
                    sl = slice(c * 512, (c + 1) * 512)
                    sync.wait_ge(sEv, c + 1)
                    vv = vst_sb[c % 2].rearrange("p (t h d) -> p t h d", t=4, h=HPC)
                    for hh in range(HPC):
                        sync.dma_start(
                            out=v_d[hh, :, 4 * c:4 * c + 4, :].rearrange("p t d -> p t d"),
                            in_=vv[:, :, hh, :]).then_inc(dSpV[c % 2], 16)

                for h in range(HPC if upto >= 3 else 0):
                    if h == 0:
                        for sm_ in (dSpQ[0], dSpQ[1], dSpK[0], dSpK[1]):
                            sync.wait_ge(sm_, 16 * 8)
                        sync.wait_ge(dSpV[0], 64 * 8)
                        sync.wait_ge(dSpV[1], 64 * 8)
                    elif h >= 2:
                        sync.wait_ge(tPv, 32 * QC * (h - 1))
                    sync.dma_start(out=k_sb[h % 2], in_=k_d[h]).then_inc(dKV[h % 2], 16)
                    sync.dma_start(out=v_sb[h % 2], in_=v_d[h]).then_inc(dKV[h % 2], 16)
                    sync.dma_start(out=q_sb[h % 2], in_=q_d[h]).then_inc(dKV[h % 2], 16)

                if upto < 4:
                    return
                sync.wait_ge(cA2A, HPC)
                for h in range(HPC):
                    sync.dma_start(out=ctx_sb[:, h::HPC, :],
                                   in_=a2a_out[h].rearrange("r p t -> p r t")
                                   ).then_inc(dCtxIn, 16)
                for ec in range(32):
                    if ec >= 2:
                        sync.wait_ge(vS1, ec - 1)
                    sync.dma_start(out=xo_sb[ec % 2], in_=x_own[:, ec, :]).then_inc(dXo[ec % 2], 16)
                    sync.wait_ge(vS1, ec + 1)
                    sync.dma_start(out=s1_d[ec], in_=s1_st[ec % 2]).then_inc(dS1[ec % 2], 16)
                def sy_norm(ec):
                    if ec >= 2:
                        sync.wait_ge(vXf, ec - 1)
                    sync.dma_start(out=s1r_sb[ec % 2], in_=s1_d[ec]).then_inc(dS1r[ec % 2], 16)
                    sync.wait_ge(vXf, ec + 1)
                    sync.dma_start(out=agx_in[ec], in_=xfb_st[ec % 2]).then_inc(dAgx[ec % 2], 16)

                if upto < 6:
                    for ec in range(32):
                        sy_norm(ec)
                    return
                sync.wait_ge(tY, 32)
                sync.dma_start(out=qz_sb[0:NG, 0:NSLP], in_=g_qz).then_inc(dDq, 16)
                sync.dma_start(out=gi_i, in_=g_gi).then_inc(dDq, 16)
                for ktn, (mat, kt, rows, nwid, ngr, klen) in enumerate(DQT):
                    if ktn == 0:
                        sync.wait_ge(vS1, 32)
                        sync.wait_ge(dS1[0], 256)
                        sync.wait_ge(dS1[1], 256)
                        sync.wait_ge(tBc, 1)
                    if ktn >= 2:
                        sync.wait_ge(vDq, UOFF[ktn - 1])
                    sync.dma_start(
                        out=qw_sb[ktn % 2][0:rows, 0:nwid],
                        in_=(g_qw, u_qw, d_qw)[mat][kt * 128: kt * 128 + rows, :]
                    ).then_inc(dQw[ktn % 2], 16)
                    if ktn < 32:
                        sy_norm(ktn)
                    if ktn >= 1:
                        pm, pk, prow, pnw, _, _ = DQT[ktn - 1]
                        sync.wait_ge(vDq, UOFF[ktn])
                        sync.dma_start(
                            out=WMAT[pm][pk][0:prow, :],
                            in_=wst_sb[(ktn - 1) % 2][0:prow, 0:pnw]).then_inc(dWbf[(ktn - 1) % 2], 16)
                    if mat < 2 and kt == 31:
                        sync.wait_ge(vOh, mat + 1)
                        sync.wait_ge(tOh, OHOFF[mat + 1])
                        nxt_qz, nxt_gi = (u_qz, u_gi) if mat == 0 else (d_qz, d_gi)
                        sync.dma_start(out=qz_sb[0:(NG if mat == 0 else NGD),
                                                 0:(NSLP if mat == 0 else D // 8)],
                                       in_=nxt_qz).then_inc(dDq, 16)
                        sync.dma_start(out=gi_i[:, 0:(D if mat == 0 else NSL)],
                                       in_=nxt_gi).then_inc(dDq, 16)
                pm, pk, prow, pnw, _, _ = DQT[NDQ - 1]
                sync.wait_ge(vDq, NDQU)
                sync.dma_start(out=WMAT[pm][pk][0:prow, :],
                               in_=wst_sb[(NDQ - 1) % 2][0:prow, 0:pnw]).then_inc(dWbf[(NDQ - 1) % 2], 16)

                if upto < 7:
                    return
                sync.wait_ge(dWbf[0], 16 * 38)
                sync.wait_ge(dWbf[1], 16 * 37)
                sync.wait_ge(dAgx[0], 16 * 16)
                sync.wait_ge(dAgx[1], 16 * 16)
                sync.dma_start(out=wres_gu,
                               in_=wg_d.rearrange("k p n -> p k n")).then_inc(dWres, 16)
                for tc in range(16):
                    r, hf = tc // 2, tc % 2
                    if tc >= 2:
                        sync.wait_ge(tH, NKT * (tc - 1))
                    for i in range(4):
                        if tc == 0:
                            sync.wait_ge(cAGX, i + 1)
                        sync.dma_start(
                            out=xf_sb[tc % 2][:, 8 * i:8 * (i + 1), :],
                            in_=agx_out[i][r, :, :, hf * 512:(hf + 1) * 512].rearrange(
                                "m p t -> p m t")).then_inc(dXf[tc % 2], 16)
                    sync.wait_ge(sSilu, NKT * (tc + 1))
                    sync.dma_start(
                        out=hg_d[:, :, tc * 512:(tc + 1) * 512].rearrange("n p t -> p n t"),
                        in_=hst_sb[tc % 2]).then_inc(dHg[tc % 2], 16)
                sync.wait_ge(tH, NKT * 16)
                sync.dma_start(out=wres_gu,
                               in_=wu_d.rearrange("k p n -> p k n")).then_inc(dWres, 16)
                for tc in range(16):
                    if tc >= 2:
                        sync.wait_ge(vH2, NKT * (tc - 1))
                        sync.wait_ge(dH2[tc % 2], 16 * (tc // 2))
                    sync.dma_start(
                        out=hgr_sb[tc % 2],
                        in_=hg_d[:, :, tc * 512:(tc + 1) * 512].rearrange(
                            "n p t -> p n t")).then_inc(dHgr[tc % 2], 16)
                    sync.wait_ge(vH2, NKT * (tc + 1))
                    sync.dma_start(
                        out=h_d[:, :, tc * 512:(tc + 1) * 512].rearrange("n p t -> p n t"),
                        in_=hgr_sb[tc % 2]).then_inc(dH2[tc % 2], 16)
                if upto < 8:
                    return
                sync.wait_ge(tH2, NKT * 16)
                sync.wait_ge(dH2[0], 16 * 8)
                sync.wait_ge(dH2[1], 16 * 8)
                sync.dma_start(out=wres_d,
                               in_=wd_d.rearrange("k p n -> p k n")).then_inc(dWres, 16)
                i3 = 0
                for j in range(4):
                    for tc in range(16):
                        r, hf = tc // 2, tc % 2
                        if i3 >= 2:
                            sync.wait_ge(tY3, 8 * (i3 - 1))
                        sync.dma_start(
                            out=hgr_sb[i3 % 2],
                            in_=h_d[:, :, tc * 512:(tc + 1) * 512].rearrange(
                                "n p t -> p n t")).then_inc(dH3[i3 % 2], 16)
                        i3 += 1

                if "rs_dbg" in dbg:
                    sync.wait_ge(dRs4[0], 16 * 16)
                    sync.dma_start(out=rsi_dbg, in_=rs_in[0]).then_inc(dPre, 16)
                    sync.wait_ge(cRS, 1)
                    sync.dma_start(out=rso_dbg, in_=rs_out[0]).then_inc(dPre, 16)
                if upto < 9:
                    return
                for tt in range(NTT):
                    tsl = slice(tt * 128, (tt + 1) * 128)
                    if tt == 0:
                        sync.wait_ge(tY3, 8 * 64)
                    if tt >= 2:
                        sync.wait_ge(vF, tt - 1)
                    sync.dma_start(
                        out=s1f_sb[tt % 2],
                        in_=s1_d[:, :, tsl].rearrange("e p t -> p e t")).then_inc(dF1[tt % 2], 16)

            @block.gpsimd
            def _(gpsimd):
                gpsimd.dma_start(out=wqT_sb, in_=wqT).then_inc(dPre, 16)
                gpsimd.dma_start(out=wkT_sb, in_=wkT).then_inc(dPre, 16)
                gpsimd.dma_start(out=wvT_sb, in_=wvT).then_inc(dPre, 16)
                for c in range(SC if upto >= 1 else 0):
                    if c >= 2:
                        gpsimd.wait_ge(tK, c - 1)
                    gpsimd.dma_start(out=xt_sb[c % 2],
                                     in_=xh[:, :, c * 512:(c + 1) * 512]).then_inc(dX[c % 2], 16)
                if upto < 2:
                    return
                gpsimd.wait_ge(tK, SC)
                for c in range(SC):
                    if c >= 2:
                        gpsimd.wait_ge(tV, c - 1)
                    gpsimd.dma_start(out=xt_sb[c % 2],
                                     in_=xh[:, :, c * 512:(c + 1) * 512]).then_inc(dXv[c % 2], 16)
                if upto < 3:
                    return
                for ci in range(NCH):
                    h, qc = divmod(ci, QC)
                    gpsimd.wait_ge(vO, ci + 1)
                    gpsimd.dma_start(
                        out=a2a_in[h][qc // 2, :, (qc % 2) * 512:((qc % 2) + 1) * 512],
                        in_=ctxst[ci % 2]).then_inc(dCtx[ci % 2], 16)
                    if qc == QC - 1 and upto >= 4:
                        gpsimd.wait_ge(dCtx[0], 16 * 8 * (h + 1))
                        gpsimd.wait_ge(dCtx[1], 16 * 8 * (h + 1))
                        gpsimd.collective_compute(
                            "AllToAll", ALU.bypass, replica_groups=RG,
                            ins=[a2a_in[h].opt()],
                            outs=[a2a_out[h].opt()]).then_inc(cA2A, 1)
                if upto < 4:
                    return
                for ec in range(32):
                    if ec >= 2:
                        gpsimd.wait_ge(tY, ec - 1)
                    gpsimd.dma_start(out=woT_sb[ec % 2],
                                     in_=woT[:, :, ec * 128:(ec + 1) * 128]).then_inc(dWo[ec % 2], 16)
                if upto >= 6:
                    gpsimd.wait_ge(tY, 32)
                    gpsimd.dma_start(out=sc_sb[0:NG, 0:NSL], in_=g_sc).then_inc(dSc, 16)
                if upto < 5:
                    return
                for i in range(4):
                    gpsimd.wait_ge(dAgx[0], 16 * 4 * (i + 1))
                    gpsimd.wait_ge(dAgx[1], 16 * 4 * (i + 1))
                    gpsimd.collective_compute(
                        "AllGather", ALU.bypass, replica_groups=RG,
                        ins=[agx_in[8 * i:8 * (i + 1)].opt()],
                        outs=[agx_out[i].opt()]).then_inc(cAGX, 1)
                if upto < 6:
                    return
                gpsimd.wait_ge(tG, UMAT[1])
                gpsimd.dma_start(out=sc_sb[0:NG, 0:NSL], in_=u_sc).then_inc(dSc, 16)
                gpsimd.wait_ge(tG, UMAT[2])
                gpsimd.dma_start(out=sc_sb[0:NGD, 0:D], in_=d_sc).then_inc(dSc, 16)
                if upto >= 7:
                    gpsimd.wait_ge(tH, NKT * 16)
                    for tc in range(16):
                        r, hf = tc // 2, tc % 2
                        if tc >= 2:
                            gpsimd.wait_ge(tH2, NKT * (tc - 1))
                        for i in range(4):
                            gpsimd.dma_start(
                                out=xf_sb[tc % 2][:, 8 * i:8 * (i + 1), :],
                                in_=agx_out[i][r, :, :, hf * 512:(hf + 1) * 512].rearrange(
                                    "m p t -> p m t")).then_inc(dSpQ[tc % 2], 16)
                if upto < 8:
                    return
                for j in range(4):
                    gpsimd.wait_ge(dRs8[j][0], 16 * 8)
                    gpsimd.wait_ge(dRs8[j][1], 16 * 8)
                    gpsimd.collective_compute(
                        "ReduceScatter", ALU.add, replica_groups=RG,
                        ins=[rs_in[j].opt()], outs=[rs_out[j].opt()]).then_inc(cRS, 1)
                if upto < 9:
                    return
                for tt in range(NTT):
                    tsl = slice(tt * 128, (tt + 1) * 128)
                    if tt == 0:
                        gpsimd.wait_ge(cRS, 4)
                    if tt >= 2:
                        gpsimd.wait_ge(vF, tt - 1)
                    for j in range(4):
                        gpsimd.dma_start(
                            out=ysl_sb[tt % 2][:, j * 8:(j + 1) * 8, :],
                            in_=rs_out[j][:, tsl].rearrange(
                                "(ee p) t -> p ee t", p=128)).then_inc(dYs[tt % 2], 16)

            @block.tensor
            def _(tensor):
                tensor.wait_ge(dPre, 16 * NPRE)
                for c in range(SC if upto >= 1 else 0):
                    tensor.wait_ge(dX[c % 2], 16 * (c // 2 + 1))
                    if c >= 1:
                        tensor.wait_ge(vRope, 2 * c - 1)
                    for et in range(HPC):
                        for m in range(32):
                            last = tensor.matmul(ps_q[:, et * 512:(et + 1) * 512],
                                          wqT_sb[:, m, et * 128:(et + 1) * 128],
                                          xt_sb[c % 2][:, m, :],
                                          start=(m == 0), stop=(m == 31))
                    last.then_inc(tQ, 1)
                    if c >= 1:
                        tensor.wait_ge(vRope, 2 * c)
                    for et in range(HPC):
                        for m in range(32):
                            last = tensor.matmul(ps_k[:, et * 512:(et + 1) * 512],
                                          wkT_sb[:, m, et * 128:(et + 1) * 128],
                                          xt_sb[c % 2][:, m, :],
                                          start=(m == 0), stop=(m == 31))
                    last.then_inc(tK, 1)
                if upto < 2:
                    return
                tensor.wait_ge(vRope, 2 * SC)
                for c in range(SC):
                    tensor.wait_ge(dXv[c % 2], 16 * (c // 2 + 1))
                    tensor.wait_ge(dPre, 16 * (NPRE + 1))
                    if c >= 1:
                        tensor.wait_ge(sEv, c)
                    for st in range(4):
                        for m in range(32):
                            last = tensor.matmul(ps_q[:, st * 512:(st + 1) * 512],
                                          xt_sb[c % 2][:, m, st * 128:(st + 1) * 128],
                                          wvT_sb[:, m, :],
                                          start=(m == 0), stop=(m == 31))
                    last.then_inc(tV, 1)

                if upto < 3:
                    return
                def z_mm(ci_):
                    tensor.wait_ge(vZf, ci_ + 1)
                    tensor.wait_ge(sZb, ci_)
                    tensor.matmul(ps_zzb[0:1, :], ones_col_f_sb, accf,
                                  start=True, stop=True).then_inc(tZ, 1)

                def zb_mm(ci_):
                    tensor.wait_ge(vRec, ci_ + 1)
                    tensor.matmul(ps_zzb, ones_row_sb, recip_sb,
                                  start=True, stop=True).then_inc(tZb, 1)

                for h in range(HPC):
                    tensor.wait_ge(dKV[h % 2], 16 * 3 * (h // 2 + 1))
                    for qc in range(QC):
                        ci = h * QC + qc
                        qsl = slice(qc * 512, (qc + 1) * 512)
                        for g in range(32):
                            gg = 32 * ci + g
                            if ci >= 1 and g == 0:
                                z_mm(ci - 1)
                            if ci >= 1 and g == 2:
                                zb_mm(ci - 1)
                            tensor.wait_ge(sExp, max(0, gg - 2))
                            if ci == 0 and g == 0:
                                tensor.wait_ge(sEv, SC)
                            for jj in range(2):
                                kt = 2 * g + jj
                                last = tensor.matmul(ps_s[g % 3][:, jj * 512:(jj + 1) * 512],
                                              k_sb[h % 2][:, kt * 128:(kt + 1) * 128],
                                              q_sb[h % 2][:, qsl],
                                              start=True, stop=True)
                            last.then_inc(tS, 1)
                            if g >= 4:
                                tensor.wait_ge(sExp, gg - 3)
                                if g == 4:
                                    tensor.wait_ge(vO, ci)
                                for jj in range(2):
                                    kt = 2 * (g - 4) + jj
                                    last = tensor.matmul(ps_o, v_sb[h % 2][:, kt, :],
                                                  pring[:, (g - 4) % 8,
                                                        jj * 512:(jj + 1) * 512],
                                                  start=(kt == 0), stop=False)
                                last.then_inc(tPv, 1)
                        for gl in (28, 29, 30, 31):
                            gg = 32 * ci + gl
                            tensor.wait_ge(sExp, gg + 1)
                            for jj in range(2):
                                kt = 2 * gl + jj
                                last = tensor.matmul(ps_o, v_sb[h % 2][:, kt, :],
                                              pring[:, gl % 8, jj * 512:(jj + 1) * 512],
                                              start=False, stop=(kt == 63))
                            last.then_inc(tPv, 1)
                z_mm(NCH - 1)
                zb_mm(NCH - 1)

                if upto < 4:
                    return
                tensor.wait_ge(dCtxIn, 16 * HPC)
                for ec in range(32):
                    tensor.wait_ge(dWo[ec % 2], 16 * (ec // 2 + 1))
                    if ec < 2:
                        tensor.wait_ge(sZb, NCH)
                        tensor.wait_ge(vO, NCH)
                    else:
                        tensor.wait_ge(vS1, ec - 1)
                    for ts in range(2):
                        for m in range(32):
                            last = tensor.matmul(ps_y[ec % 2][:, ts * 512:(ts + 1) * 512],
                                          woT_sb[ec % 2][:, m, :],
                                          ctx_sb[:, m, ts * 512:(ts + 1) * 512],
                                          start=(m == 0), stop=(m == 31))
                    last.then_inc(tY, 1)
                    tensor.wait_ge(vS1, ec + 1)
                    for ts in range(2):
                        tensor.matmul(ps_mu[:, ts * 512:(ts + 1) * 512], ones_col_b_sb,
                                      s1b_sb[:, ts * 512:(ts + 1) * 512],
                                      start=(ec == 0), stop=(ec == 31))
                        last = tensor.matmul(ps_sq[:, ts * 512:(ts + 1) * 512], ones_col_b_sb,
                                      sqb_sb[:, ts * 512:(ts + 1) * 512],
                                      start=(ec == 0), stop=(ec == 31))
                    if ec == 31:
                        last.then_inc(tSt, 1)
                tensor.wait_ge(sRstd, 1)
                for ts in range(2):
                    tensor.matmul(ps_bc[:, ts * 512:(ts + 1) * 512], ones_row_sb,
                                  mu_sb[:, ts * 512:(ts + 1) * 512], start=True, stop=True)
                    last = tensor.matmul(ps_bc[:, 1024 + ts * 512:1024 + (ts + 1) * 512],
                                  ones_row_sb, rstd_sb[:, ts * 512:(ts + 1) * 512],
                                  start=True, stop=True)
                last.then_inc(tBc, 1)

                if upto < 6:
                    return
                tensor.wait_ge(vBc, 1)
                for (ktn, mat, kt, rows, nwid, ngr, klen, chk, wid) in DQU:
                    if kt == 0 and chk == 0:
                        tensor.wait_ge(vOh, mat + 1)
                        for ochk in range(OHC[mat]):
                            owid = min(512, klen - ochk * 512)
                            if OHOFF[mat] + ochk >= 1:
                                tensor.wait_ge(vOh2, OHOFF[mat] + ochk)
                            tensor.matmul(ps_ohb[0:ngr, 0:owid], ones_row_sb[:, 0:ngr],
                                          gi_f[:, ochk * 512:ochk * 512 + owid],
                                          start=True, stop=True).then_inc(tOh, 1)
                        tensor.wait_ge(vOh2, OHOFF[mat] + OHC[mat])
                        tensor.wait_ge(dSc, 16 * (mat + 1))
                    u = UOFF[ktn] + chk
                    if u >= 2:
                        tensor.wait_ge(vDq, u - 1)
                    pss = ps_dq[u % 2]
                    tensor.matmul(pss[0][0:rows, 0:wid],
                                  oh_sb[0:ngr, kt * 128:kt * 128 + rows],
                                  sc_sb[0:ngr, chk * 512:chk * 512 + wid],
                                  start=True, stop=True)
                    tensor.matmul(pss[1][0:rows, 0:wid],
                                  oh_sb[0:ngr, kt * 128:kt * 128 + rows],
                                  zp1_sb[0:ngr, chk * 512:chk * 512 + wid],
                                  start=True, stop=True).then_inc(tG, 1)

                if upto < 7:
                    return
                tensor.wait_ge(vDq, NDQU)
                tensor.wait_ge(dWres, 16)
                for tc in range(16):
                    for nt in range(NKT):
                        rows = 128 if nt < 10 else 96
                        it = NKT * tc + nt
                        if it >= 4:
                            tensor.wait_ge(sSilu, it - 3)
                        for kt in range(32):
                            if nt == 0 and kt % 8 == 0:
                                tensor.wait_ge(dXf[tc % 2],
                                               64 * (tc // 2) + 16 * (kt // 8 + 1))
                            last = tensor.matmul(ps_h[it % 4][0:rows, :],
                                          wres_gu[:, kt, nt * 128:nt * 128 + rows],
                                          xf_sb[tc % 2][:, kt, :],
                                          start=(kt == 0), stop=(kt == 31))
                        last.then_inc(tH, 1)
                tensor.wait_ge(dWres, 32)
                tensor.wait_ge(sSilu, NKT * 16)
                for tc in range(16):
                    tensor.wait_ge(dSpQ[tc % 2], 128 + 64 * (tc // 2 + 1))
                    for nt in range(NKT):
                        rows = 128 if nt < 10 else 96
                        it = NKT * tc + nt
                        if it >= 4:
                            tensor.wait_ge(vH2, it - 3)
                        for kt in range(32):
                            last = tensor.matmul(ps_h[it % 4][0:rows, :],
                                          wres_gu[:, kt, nt * 128:nt * 128 + rows],
                                          xf_sb[tc % 2][:, kt, :],
                                          start=(kt == 0), stop=(kt == 31))
                        last.then_inc(tH2, 1)
                if upto < 8:
                    return
                tensor.wait_ge(dWres, 48)
                tensor.wait_ge(vH2, NKT * 16)
                i3 = 0
                for j in range(4):
                    for tc in range(16):
                        tensor.wait_ge(dH3[i3 % 2], 16 * (i3 // 2 + 1))
                        for ecc in range(8):
                            ie = 8 * i3 + ecc
                            if ie >= 2:
                                tensor.wait_ge(vE3, ie - 1)
                            esl = slice((j * 8 + ecc) * 128, (j * 8 + ecc + 1) * 128)
                            for nt in range(NKT):
                                rows = 128 if nt < 10 else 96
                                last = tensor.matmul(ps_y3[ecc % 2],
                                              wres_d[0:rows, nt, esl],
                                              hgr_sb[i3 % 2][0:rows, nt, :],
                                              start=(nt == 0), stop=(nt == NKT - 1))
                            last.then_inc(tY3, 1)
                        i3 += 1

                if upto < 9:
                    return
                for tt in range(NTT):
                    tensor.wait_ge(vF, tt + 1)
                    for ec in range(32):
                        ie = 32 * tt + ec
                        if ie >= 2:
                            tensor.wait_ge(sTe, ie - 1)
                        tensor.transpose(ps_tr[ec % 2], s2_sb[:, ec, :],
                                         ident_sb).then_inc(tTr, 1)

            @block.vector
            def _(vector):
                vector.wait_ge(dPre, 16 * NPRE)
                for c in range(SC if upto >= 1 else 0):
                    vector.wait_ge(dCS[c % 2], 32 * (c // 2 + 1))
                    if c >= 2:
                        vector.wait_ge(dSpQ[c % 2], 16 * (c // 2))
                        vector.wait_ge(dSpK[c % 2], 16 * (c // 2))
                    vector.wait_ge(tQ, c + 1)
                    for et in range(HPC):
                        psl = ps_q[:, et * 512:(et + 1) * 512]
                        vector.tensor_mul(uv_sb[:, 0:512], psl, cos_sb[c % 2])
                        vector.tensor_mul(psl, psl, sin_sb[c % 2])
                        vector.tensor_sub(qst_sb[c % 2][0:64, et, :],
                                          uv_sb[0:64, 0:512], psl[64:128, :])
                        last = vector.tensor_add(qst_sb[c % 2][64:128, et, :],
                                          uv_sb[64:128, 0:512], psl[0:64, :])
                    last.then_inc(vRope, 1)
                    vector.wait_ge(tK, c + 1)
                    for et in range(HPC):
                        psl = ps_k[:, et * 512:(et + 1) * 512]
                        vector.tensor_mul(uv_sb[:, 0:512], psl, cos_sb[c % 2])
                        vector.tensor_mul(psl, psl, sin_sb[c % 2])
                        vector.tensor_sub(kst_sb[c % 2][0:64, et, :],
                                          uv_sb[0:64, 0:512], psl[64:128, :])
                        last = vector.tensor_add(kst_sb[c % 2][64:128, et, :],
                                          uv_sb[64:128, 0:512], psl[0:64, :])
                    last.then_inc(vRope, 1)

                if upto < 3:
                    return
                def v_recip(ci_):
                    vector.wait_ge(tZ, ci_ + 1)
                    vector.reciprocal(recip_sb, ps_zzb[0:1, :]).then_inc(vRec, 1)

                def v_zbcopy(ci_):
                    vector.wait_ge(tZb, ci_ + 1)
                    vector.tensor_copy(zb_sb, ps_zzb).then_inc(sZb, 1)

                def v_oevac(ci_):
                    vector.wait_ge(tPv, 32 * (ci_ + 1))
                    if ci_ >= 2:
                        vector.wait_ge(dCtx[ci_ % 2], 16 * (ci_ // 2))
                    vector.tensor_mul(ctxst[ci_ % 2], ps_o, zb_sb).then_inc(vO, 1)

                for ci in range(NCH):
                    for g in range(32):
                        gg = 32 * ci + g
                        if ci >= 1 and g == 2:
                            v_recip(ci - 1)
                        if ci >= 1 and g == 3:
                            v_zbcopy(ci - 1)
                        if ci >= 1 and g == 4:
                            v_oevac(ci - 1)
                        vector.wait_ge(sExp, gg + 1)
                        pg = pring[:, g % 8, :]
                        if g % 8 == 0:
                            last = vector.tensor_copy(accb, pg)
                        else:
                            last = vector.tensor_add(accb, accb, pg)
                        if g % 8 == 7:
                            if g == 7:
                                last = vector.tensor_copy(acc32, accb)
                            else:
                                last = vector.tensor_add(acc32, acc32, accb)
                        last.then_inc(vZ, 1)
                    vector.wait_ge(tZ, ci)
                    vector.tensor_add(accf, acc32[:, 0:512], acc32[:, 512:1024]).then_inc(vZf, 1)
                v_recip(NCH - 1)
                v_zbcopy(NCH - 1)
                v_oevac(NCH - 1)

                if upto < 4:
                    return
                for ec in range(32):
                    vector.wait_ge(tY, ec + 1)
                    vector.wait_ge(dXo[ec % 2], 16 * (ec // 2 + 1))
                    if ec >= 2:
                        vector.wait_ge(dS1[ec % 2], 16 * (ec // 2))
                    vector.tensor_add(s1_st[ec % 2], ps_y[ec % 2], xo_sb[ec % 2])
                    vector.tensor_copy(s1b_sb, s1_st[ec % 2])
                    vector.tensor_mul(sqb_sb, s1b_sb, s1b_sb).then_inc(vS1, 1)
                vector.wait_ge(tSt, 1)
                vector.tensor_scalar(mu_sb, ps_mu, 1.0 / D, None, ALU.mult)
                vector.tensor_scalar(msq_sb, ps_sq, 1.0 / D,
                                     None, ALU.mult).then_inc(vCh, 1)
                vector.wait_ge(vCh, vch[0] + 1)
                vch[0] += 1
                vector.tensor_mul(var_sb, mu_sb, mu_sb).then_inc(vCh, 1)
                vector.wait_ge(vCh, vch[0] + 1)
                vch[0] += 1
                vector.tensor_sub(var2_sb, msq_sb, var_sb).then_inc(vSt, 1)
                vector.wait_ge(tBc, 1)
                vector.tensor_copy(mub_sb, ps_bc[:, 0:1024])
                vector.tensor_copy(rstdb_sb, ps_bc[:, 1024:2048]).then_inc(vBc, 1)
                def v_norm(ec):
                    vector.wait_ge(dS1r[ec % 2], 16 * (ec // 2 + 1))
                    if ec >= 2:
                        vector.wait_ge(dAgx[ec % 2], 16 * (ec // 2))
                    vector.tensor_sub(s1r_sb[ec % 2], s1r_sb[ec % 2], mub_sb)
                    vector.tensor_mul(xfb_st[ec % 2], s1r_sb[ec % 2], rstdb_sb).then_inc(vXf, 1)

                if upto < 6:
                    for ec in range(32):
                        v_norm(ec)
                    return
                nrm = [0]
                for (ktn, mat, kt, rows, nwid, ngr, klen, chk, wid) in DQU:
                    u = UOFF[ktn] + chk
                    if u % 3 == 0 and nrm[0] < 32:
                        v_norm(nrm[0])
                        nrm[0] += 1
                    pss = ps_dq[u % 2]
                    if kt == 0 and chk == 0:
                        npk = NSLP if mat < 2 else D // 8
                        vector.wait_ge(dDq, 16 * 2 * (mat + 1))
                        vector.wait_ge(dSc, 16 * (mat + 1))
                        if mat >= 1:
                            vector.wait_ge(tOh, OHOFF[mat])
                            vector.wait_ge(tG, UMAT[mat])
                        vector.tensor_copy(gi_f[:, 0:klen], gi_i[:, 0:klen])
                        for jjj in range(8):
                            vector.tensor_scalar(z_sb[0:ngr, jjj:nwid:8],
                                                 qz_sb[0:ngr, 0:npk],
                                                 4 * jjj, 15,
                                                 ALU.logical_shift_right, ALU.bitwise_and)
                        vector.tensor_copy(w4f_sb[0:ngr, 0:nwid], z_sb[0:ngr, 0:nwid])
                        vector.tensor_scalar(zp1_sb[0:ngr, 0:nwid], w4f_sb[0:ngr, 0:nwid],
                                             1.0, None, ALU.add).then_inc(vOh, 1)
                        for ochk in range(OHC[mat]):
                            owid = min(512, klen - ochk * 512)
                            vector.wait_ge(tOh, OHOFF[mat] + ochk + 1)
                            vector.tensor_scalar(
                                oh_sb[0:ngr, ochk * 512:ochk * 512 + owid],
                                ps_ohb[0:ngr, 0:owid],
                                iota32_sb[0:ngr, 0:1] if mat < 2 else iota86_sb[0:ngr, 0:1],
                                None, ALU.is_equal).then_inc(vOh2, 1)
                    if chk == 0:
                        vector.wait_ge(dQw[ktn % 2], 16 * (ktn // 2 + 1))
                        if ktn >= 2:
                            vector.wait_ge(dWbf[ktn % 2], 16 * (ktn // 2))
                        vector.tensor_scalar(qw_sb[ktn % 2][0:rows, 0:nwid],
                                             qw_sb[ktn % 2][0:rows, 0:nwid],
                                             shifts_sb[0:rows, 0:1], 15,
                                             ALU.logical_shift_right, ALU.bitwise_and)
                        vector.tensor_copy(w4f_sb[0:rows, 0:nwid],
                                           qw_sb[ktn % 2][0:rows, 0:nwid])
                    vector.wait_ge(tG, u + 1)
                    csl = slice(chk * 512, chk * 512 + wid)
                    vector.tensor_sub(tdq_sb[0:rows, 0:wid],
                                      w4f_sb[0:rows, csl], pss[1][0:rows, 0:wid])
                    vector.tensor_mul(wst_sb[ktn % 2][0:rows, csl],
                                      tdq_sb[0:rows, 0:wid],
                                      pss[0][0:rows, 0:wid]).then_inc(vDq, 1)

                while nrm[0] < 32:
                    v_norm(nrm[0])
                    nrm[0] += 1

                if upto < 7:
                    return
                for tc in range(16):
                    vector.wait_ge(dHgr[tc % 2], 16 * (tc // 2 + 1))
                    for nt in range(NKT):
                        rows = 128 if nt < 10 else 96
                        it = NKT * tc + nt
                        vector.wait_ge(tH2, it + 1)
                        vector.tensor_mul(hgr_sb[tc % 2][0:rows, nt, :],
                                          ps_h[it % 4][0:rows, :],
                                          hgr_sb[tc % 2][0:rows, nt, :]).then_inc(vH2, 1)
                if upto < 8:
                    return
                i3 = 0
                for j in range(4):
                    for tc in range(16):
                        if i3 >= 2:
                            pj, pt = divmod(i3 - 2, 16)
                            vector.wait_ge(dRs8[pj][pt % 2], 16 * (pt // 2 + 1))
                        for ecc in range(8):
                            ie = 8 * i3 + ecc
                            vector.wait_ge(tY3, ie + 1)
                            vector.tensor_copy(yst_sb[i3 % 2][:, ecc, :],
                                               ps_y3[ecc % 2]).then_inc(vE3, 1)
                        i3 += 1

                if upto < 9:
                    return
                for tt in range(NTT):
                    vector.wait_ge(dF1[tt % 2], 256 + 16 * (tt // 2 + 1))
                    vector.wait_ge(dYs[tt % 2], 96 + 64 * (tt // 2 + 1))
                    tsl = slice(tt * 128, (tt + 1) * 128)
                    bm = mub_sb[:, tsl].unsqueeze(1).broadcast_to([128, 32, 128])
                    br = rstdb_sb[:, tsl].unsqueeze(1).broadcast_to([128, 32, 128])
                    vector.tensor_sub(s2_sb, s1f_sb[tt % 2], bm).then_inc(vCh, 1)
                    vector.wait_ge(vCh, vch[0] + 1)
                    vch[0] += 1
                    vector.tensor_mul(s2_sb, s2_sb, br).then_inc(vCh, 1)
                    vector.wait_ge(vCh, vch[0] + 1)
                    vch[0] += 1
                    vector.tensor_add(s2_sb, s2_sb, ysl_sb[tt % 2]).then_inc(vF, 1)
                    vector.wait_ge(sTe, 32 * (tt + 1))
                    vector.tensor_mul(t3_sb, tmaj_sb[tt % 2], tmaj_sb[tt % 2])
                    vector.reduce_sum(red_sb[:, 0:1], tmaj_sb[tt % 2],
                                      mybir.AxisListType.X)
                    vector.reduce_sum(red_sb[:, 1:2], t3_sb,
                                      mybir.AxisListType.X).then_inc(vCh, 1)
                    vector.wait_ge(vCh, vch[0] + 1)
                    vch[0] += 1
                    vector.tensor_scalar(red_sb[:, 2:3], red_sb[:, 0:1],
                                         -1.0 / D, None, ALU.mult)
                    vector.tensor_scalar(red_sb[:, 4:5], red_sb[:, 1:2],
                                         1.0 / D, None, ALU.mult).then_inc(vCh, 1)
                    vector.wait_ge(vCh, vch[0] + 1)
                    vch[0] += 1
                    vector.tensor_mul(red_sb[:, 3:4], red_sb[:, 2:3],
                                      red_sb[:, 2:3]).then_inc(vCh, 1)
                    vector.wait_ge(vCh, vch[0] + 1)
                    vch[0] += 1
                    vector.tensor_sub(red_sb[:, 5:6], red_sb[:, 4:5],
                                      red_sb[:, 3:4]).then_inc(vLn, 1)
                    vector.wait_ge(sLn, tt + 1)
                    if tt >= 2:
                        vector.wait_ge(dOut[tt % 2], 256 + 16 * (tt // 2))
                    vector.tensor_scalar(out_st[tt % 2], tmaj_sb[tt % 2],
                                         red_sb[:, 2:3], red_sb[:, 3:4],
                                         ALU.add, ALU.mult).then_inc(vLn2, 1)

            @block.scalar
            def _(scalar):
                scalar.wait_ge(dPre, 16 * NPRE)
                for c in range(SC if upto >= 2 else 0):
                    scalar.wait_ge(tV, c + 1)
                    if c >= 2:
                        scalar.wait_ge(dSpV[c % 2], 64 * (c // 2))
                    scalar.copy(vst_sb[c % 2], ps_q).then_inc(sEv, 1)

                if upto < 3:
                    return
                for ci in range(NCH):
                    for g in range(32):
                        gg = 32 * ci + g
                        scalar.wait_ge(tS, gg + 1)
                        if gg >= 8:
                            scalar.wait_ge(tPv, gg - 7)
                            scalar.wait_ge(vZ, gg - 7)
                        scalar.activation(pring[:, g % 8, :], ps_s[g % 3], AF.Exp,
                                          bias=cv[:, C_EXPB:C_EXPB + 1],
                                          scale=cv[:, C_QKSC:C_QKSC + 1]).then_inc(sExp, 1)

                if upto < 4:
                    return
                scalar.wait_ge(vSt, 1)
                scalar.activation(var_sb, var2_sb, AF.Ln,
                                  bias=cv[0:1, C_EPS:C_EPS + 1]).then_inc(sCh, 1)
                scalar.wait_ge(sCh, sch[0] + 1)
                sch[0] += 1
                scalar.activation(rstd_sb, var_sb, AF.Exp,
                                  scale=cv[0:1, C_NHALF:C_NHALF + 1]).then_inc(sRstd, 1)

                if upto < 7:
                    return
                for tc in range(16):
                    for nt in range(NKT):
                        rows = 128 if nt < 10 else 96
                        it = NKT * tc + nt
                        scalar.wait_ge(tH, it + 1)
                        if tc >= 2 and nt == 0:
                            scalar.wait_ge(dHg[tc % 2], 16 * (tc // 2))
                        scalar.activation(hst_sb[tc % 2][0:rows, nt, :],
                                          ps_h[it % 4][0:rows, :],
                                          AF.Silu).then_inc(sSilu, 1)

                if upto < 8:
                    return
                i3s = 0
                for j in range(4):
                    for tc in range(16):
                        r, hf = tc // 2, tc % 2
                        scalar.wait_ge(vE3, 8 * (i3s + 1))
                        scalar.dma_start(
                            out=rs_in[j][r, :, hf * 512:(hf + 1) * 512].rearrange(
                                "(ee p) t -> p ee t", p=128),
                            in_=yst_sb[i3s % 2]).then_inc(dRs8[j][tc % 2], 16)
                        i3s += 1

                if upto < 9:
                    return
                for tt in range(NTT):
                    for ec in range(32):
                        scalar.wait_ge(tTr, 32 * tt + ec + 1)
                        scalar.copy(tmaj_sb[tt % 2][:, ec * 128:(ec + 1) * 128],
                                    ps_tr[ec % 2]).then_inc(sTe, 1)
                    scalar.wait_ge(vLn, tt + 1)
                    scalar.activation(red_sb[:, 4:5], red_sb[:, 5:6], AF.Ln,
                                      bias=cv[:, C_EPS:C_EPS + 1]).then_inc(sCh, 1)
                    scalar.wait_ge(sCh, sch[0] + 1)
                    sch[0] += 1
                    scalar.activation(red_sb[:, 3:4], red_sb[:, 4:5], AF.Exp,
                                      scale=cv[:, C_NHALF:C_NHALF + 1]).then_inc(sLn, 1)
                    scalar.wait_ge(vLn2, tt + 1)
                    scalar.dma_start(out=y_out[tt * 128:(tt + 1) * 128, :],
                                     in_=out_st[tt % 2]).then_inc(dOut[tt % 2], 16)
                scalar.wait_ge(dOut[0], 256 + 16 * NTT // 2)
                scalar.wait_ge(dOut[1], 256 + 16 * NTT // 2)

    return nc


def _prep_inputs(inputs):
    bf = ml_dtypes.bfloat16
    x = np.asarray(inputs["L0_attn_x"])
    Wq = np.asarray(inputs["L0_attn_Wq"])
    Wk = np.asarray(inputs["L0_attn_Wk"])
    Wv = np.asarray(inputs["L0_attn_Wv"])
    Wo = np.asarray(inputs["L0_attn_Wo"])

    x0 = np.ascontiguousarray(x.reshape(S, D))
    xh = np.ascontiguousarray(x0.T.reshape(32, 128, S).transpose(1, 0, 2))

    def tiled_T(Wm):
        return np.ascontiguousarray(Wm.T.reshape(32, 128, D).transpose(1, 0, 2))

    wqT, wkT, wvT, woT = map(tiled_T, (Wq, Wk, Wv, Wo))

    inv = 1.0 / (10000.0 ** (np.arange(0, HD, 2, dtype=np.float64) / HD))
    fr = inv[:, None] * np.arange(S, dtype=np.float64)[None, :]
    cos1 = np.cos(fr).astype(np.float32)
    sin1 = np.sin(fr).astype(np.float32)
    cos2 = np.ascontiguousarray(np.concatenate([cos1, cos1], axis=0))
    sin2 = np.ascontiguousarray(np.concatenate([sin1, sin1], axis=0))

    cvec = np.zeros((128, NCONST), dtype=np.float32)
    cvec[:, C_EXPB] = EXP_BIAS
    cvec[:, C_QKSC] = QK_SCALE
    cvec[:, C_EPS] = EPS
    cvec[:, C_NHALF] = -0.5

    shared = {
        "xh": xh, "cos2": cos2, "sin2": sin2,
        "ident": np.eye(128, dtype=np.float32),
        "ones_row": np.ones((1, 128), dtype=np.float32),
        "ones_col_f": np.ones((128, 1), dtype=np.float32),
        "ones_col_b": np.ones((128, 1), dtype=bf),
        "iota32": np.arange(32, dtype=np.float32).reshape(32, 1),
        "iota86": np.arange(86, dtype=np.float32).reshape(86, 1),
        "shifts": (4 * (np.arange(128) % 8)).astype(np.int32).reshape(128, 1),
        "cvec": cvec, "woT": woT,
        "g_gi": np.ascontiguousarray(np.asarray(inputs["L0_ffn_gate_g_idx"]).reshape(1, D)),
        "u_gi": np.ascontiguousarray(np.asarray(inputs["L0_ffn_up_g_idx"]).reshape(1, D)),
        "d_sc": np.ascontiguousarray(np.asarray(inputs["L0_ffn_down_scales"])),
        "d_qz": np.ascontiguousarray(np.asarray(inputs["L0_ffn_down_qzeros"])),
    }
    g_qw_full = np.asarray(inputs["L0_ffn_gate_qweight"])
    u_qw_full = np.asarray(inputs["L0_ffn_up_qweight"])
    d_qw_full = np.asarray(inputs["L0_ffn_down_qweight"])
    g_sc_full = np.asarray(inputs["L0_ffn_gate_scales"])
    u_sc_full = np.asarray(inputs["L0_ffn_up_scales"])
    g_qz_full = np.asarray(inputs["L0_ffn_gate_qzeros"])
    u_qz_full = np.asarray(inputs["L0_ffn_up_qzeros"])
    d_gi_full = np.asarray(inputs["L0_ffn_down_g_idx"])

    in_maps = []
    for c in range(W):
        esl = slice(EOWN * c, EOWN * (c + 1))
        nsl = slice(NSL * c, NSL * (c + 1))
        nslp = slice(NSLP * c, NSLP * (c + 1))
        m = dict(shared)
        m["x_own"] = np.ascontiguousarray(xh[:, :, TOK * c:TOK * (c + 1)])
        m["wqT"] = np.ascontiguousarray(wqT[:, :, esl])
        m["wkT"] = np.ascontiguousarray(wkT[:, :, esl])
        m["wvT"] = np.ascontiguousarray(wvT[:, :, esl])
        m["g_qw"] = np.ascontiguousarray(np.repeat(g_qw_full[:, nsl], 8, axis=0))
        m["u_qw"] = np.ascontiguousarray(np.repeat(u_qw_full[:, nsl], 8, axis=0))
        m["d_qw"] = np.ascontiguousarray(
            np.repeat(d_qw_full[NSLP * c:NSLP * (c + 1), :], 8, axis=0))
        m["g_sc"] = np.ascontiguousarray(g_sc_full[:, nsl])
        m["u_sc"] = np.ascontiguousarray(u_sc_full[:, nsl])
        m["g_qz"] = np.ascontiguousarray(g_qz_full[:, nslp])
        m["u_qz"] = np.ascontiguousarray(u_qz_full[:, nslp])
        m["d_gi"] = np.ascontiguousarray(d_gi_full[nsl].reshape(1, NSL))
        in_maps.append(m)
    return in_maps


def kernel(**inputs):
    in_maps = _prep_inputs(inputs)
    nc = build_nc()
    res = run_bass_kernel_spmd(nc, in_maps, core_ids=list(range(W)))
    out = np.concatenate([res.results[c]["y"] for c in range(W)], axis=0)
    return out.reshape(1, S, D).astype(np.float32)


# version 09 (final)
# speedup vs baseline: 1.2977x; 1.0078x over previous
import os
import contextlib
import numpy as np
import ml_dtypes

os.environ.setdefault("TRN_ALLOW_ICI_ALL_TO_ALL", "1")

import concourse.bass as bass
import concourse.mybir as mybir
from concourse.bass_utils import run_bass_kernel_spmd

F32 = mybir.dt.float32
BF16 = mybir.dt.bfloat16
I32 = mybir.dt.int32
AF = mybir.ActivationFunctionType
ALU = mybir.AluOpType

W = 8
S = 8192
D = 4096
HD = 128
HPC = 4
EOWN = HPC * HD
TOK = S // W
INTER = 11008
NSL = INTER // W
NSLP = NSL // 8
NG = 32
NGD = 86
NKT = 11
SC = 16
QC = 16
NCH = HPC * QC
NTT = TOK // 128

EXP_BIAS = -12.0
QK_SCALE = 1.0 / float(np.sqrt(HD))
EPS = 1e-5

C_EXPB, C_QKSC, C_EPS, C_NHALF = range(4)
NCONST = 4
RG = [list(range(W))]

DQT = []
for _mat in range(3):
    _nkt = 32 if _mat < 2 else NKT
    for _kt in range(_nkt):
        _rows = 96 if (_mat == 2 and _kt == NKT - 1) else 128
        DQT.append((_mat, _kt, _rows,
                    NSL if _mat < 2 else D,
                    NG if _mat < 2 else NGD,
                    D if _mat < 2 else NSL))
NDQ = len(DQT)
OHC = [(D + 511) // 512, (D + 511) // 512, (NSL + 511) // 512]
OHOFF = [0, 8, 16]
DQOFF = [0, 32, 64]
DQU = []
UOFF = []
for _ktn, (_mat, _kt, _rows, _nwid, _ngr, _klen) in enumerate(DQT):
    UOFF.append(len(DQU))
    for _chk in range(-(-_nwid // 512)):
        _wid = min(512, _nwid - _chk * 512)
        DQU.append((_ktn, _mat, _kt, _rows, _nwid, _ngr, _klen, _chk, _wid))
NDQU = len(DQU)
UOFF.append(NDQU)
UMAT = [0, UOFF[32], UOFF[64]]


def _ceil(a, b):
    return -(-a // b)


def build_nc(upto=9, dbg=()):
    nc = bass.Bass("TRN2", target_bir_lowering=False, num_devices=W)

    di = lambda n, s, d: nc.dram_tensor(n, s, d, kind="ExternalInput").ap()
    xh = di("xh", [128, 32, S], F32)
    x_own = di("x_own", [128, 32, TOK], F32)
    wqT = di("wqT", [128, 32, EOWN], F32)
    wkT = di("wkT", [128, 32, EOWN], F32)
    wvT = di("wvT", [128, 32, EOWN], F32)
    woT = di("woT", [128, 32, D], F32)
    cos2 = di("cos2", [128, S], F32)
    sin2 = di("sin2", [128, S], F32)
    ident = di("ident", [128, 128], F32)
    ones_row = di("ones_row", [1, 128], F32)
    ones_col_f = di("ones_col_f", [128, 1], F32)
    ones_col_b = di("ones_col_b", [128, 1], BF16)
    iota32 = di("iota32", [32, 1], F32)
    iota86 = di("iota86", [86, 1], F32)
    shifts = di("shifts", [128, 1], I32)
    cvec = di("cvec", [128, NCONST], F32)
    g_qw = di("g_qw", [D, NSL], I32)
    u_qw = di("u_qw", [D, NSL], I32)
    d_qw = di("d_qw", [NSL, D], I32)
    g_sc = di("g_sc", [NG, NSL], F32)
    u_sc = di("u_sc", [NG, NSL], F32)
    d_sc = di("d_sc", [NGD, D], F32)
    g_qz = di("g_qz", [NG, NSLP], I32)
    u_qz = di("u_qz", [NG, NSLP], I32)
    d_qz = di("d_qz", [NGD, D // 8], I32)
    g_gi = di("g_gi", [1, D], I32)
    u_gi = di("u_gi", [1, D], I32)
    d_gi = di("d_gi", [1, NSL], I32)
    y_out = nc.dram_tensor("y", [TOK, D], F32, kind="ExternalOutput").ap()

    dt = lambda n, s, d: nc.dram_tensor(
        n, s, d, kind=("ExternalOutput" if n in dbg else "Internal")).ap()
    q_d = dt("q_d", [HPC, 128, S], BF16)
    k_d = dt("k_d", [HPC, 128, S], BF16)
    v_d = dt("v_d", [HPC, 128, S // 128, HD], BF16)
    a2a_in = [dt(f"a2a_in{h}", [W, 128, TOK], BF16) for h in range(HPC)]
    a2a_out = [dt(f"a2a_out{h}", [W, 128, TOK], BF16) for h in range(HPC)]
    s1_d = dt("s1_d", [32, 128, TOK], F32)
    agx_in = dt("agx_in", [32, 128, TOK], BF16)
    agx_out = [nc.dram_tensor(f"agx_out{i}", [W, 8, 128, TOK], BF16,
                              kind="Internal", addr_space="Shared").ap()
               for i in range(4)]
    wg_d = dt("wg_d", [32, 128, NSL], BF16)
    wu_d = dt("wu_d", [32, 128, NSL], BF16)
    wd_d = dt("wd_d", [NKT, 128, D], BF16)
    hg_d = dt("hg_d", [NKT, 128, S], BF16)
    h_d = dt("h_d", [NKT, 128, S], BF16)
    rs_in = [dt(f"rs_in{j}", [W, 1024, TOK], BF16) for j in range(4)]
    rs_out = [dt(f"rs_out{j}", [1024, TOK], BF16) for j in range(4)]
    if "rs_dbg" in dbg:
        rsi_dbg = nc.dram_tensor("rsi_dbg", [W, 1024, TOK], BF16, kind="ExternalOutput").ap()
        rso_dbg = nc.dram_tensor("rso_dbg", [1024, TOK], BF16, kind="ExternalOutput").ap()
    WMAT = [wg_d, wu_d, wd_d]

    es = contextlib.ExitStack()
    mk = lambda n: es.enter_context(nc.semaphore(n))
    mk2 = lambda n: [mk(n + "0"), mk(n + "1")]
    (dPre, tQ, tK, vRope, tV, sEv, tS, sExp, vZ, tPv,
     vZf, tZ, vRec, tZb, sZb, vO, cA2A, dCtxIn, tY, vS1,
     tSt, vSt, sRstd, tBc, vBc, vXf, cAGX, dDq, dSc, tOh, vOh,
     vOh2, tG, vDq, dWres, tH, sSilu, tH2, vH2,
     tY3, vE3, cRS, vF, tTr, sTe, vLn, sLn, vLn2) = [mk(f"s{i}") for i in range(48)]
    dX = mk2("dX")
    dXv = mk2("dXv")
    dCS = mk2("dCS")
    dSpQ = mk2("dSpQ")
    dSpK = mk2("dSpK")
    dSpV = mk2("dSpV")
    dKV = mk2("dKV")
    dCtx = mk2("dCtx")
    dWo = mk2("dWo")
    dXo = mk2("dXo")
    dS1 = mk2("dS1")
    dS1r = mk2("dS1r")
    dAgx = mk2("dAgx")
    dQw = mk2("dQw")
    dWbf = mk2("dWbf")
    dXf = mk2("dXf")
    dHg = mk2("dHg")
    dHgr = mk2("dHgr")
    dH2 = mk2("dH2")
    dH3 = mk2("dH3")
    dRs8 = [[mk(f"dRs{j}_{p}") for p in range(2)] for j in range(4)]
    sF1 = [(dCS[0], 256), (dCS[1], 256), (dHg[0], 128), (dHg[1], 128)]
    sYs = [(dKV[0], 96), (dKV[1], 96), (dHgr[0], 128), (dHgr[1], 128)]
    dOut = dWo
    vCh = mk("vCh")
    sCh = mk("sCh")

    AL = nc.alloc_sbuf_tensor
    ident_sb = AL("ident_sb", [128, 128], F32).ap()
    ones_row_sb = AL("ones_row_sb", [1, 128], F32).ap()
    ones_col_f_sb = AL("ones_col_f_sb", [128, 1], F32).ap()
    ones_col_b_sb = AL("ones_col_b_sb", [128, 1], BF16).ap()
    iota32_sb = AL("iota32_sb", [32, 1], F32).ap()
    iota86_sb = AL("iota86_sb", [86, 1], F32).ap()
    shifts_sb = AL("shifts_sb", [128, 1], I32).ap()
    cv = AL("cv_sb", [128, NCONST], F32).ap()
    mub_sb = AL("mub_sb", [128, TOK], F32).ap()
    rstdb_sb = AL("rstdb_sb", [128, TOK], F32).ap()
    red_sb = AL("red_sb", [128, 16], F32).ap()

    ABASE = nc.sbuf_base
    ATOP = nc.sbuf_top

    class Ph:
        def __init__(self):
            self.off = ABASE

        def t(self, name, shape, dtype):
            per = int(np.prod(shape[1:])) * np.dtype(mybir.dt.np(dtype)).itemsize
            off = (self.off + 31) & ~31
            h = nc.alloc_sbuf_tensor_at(name, shape, dtype, offset=off)
            self.off = off + per
            assert self.off <= ATOP, f"{name}: arena overflow {self.off} > {ATOP}"
            return h.ap()

    ph = Ph()
    wqT_sb = ph.t("wqT_sb", [128, 32, EOWN], BF16)
    wkT_sb = ph.t("wkT_sb", [128, 32, EOWN], BF16)
    xt_sb = [ph.t(f"xt{i}", [128, 32, 512], BF16) for i in range(2)]
    cos_sb = [ph.t(f"cos{i}", [128, 512], F32) for i in range(2)]
    sin_sb = [ph.t(f"sin{i}", [128, 512], F32) for i in range(2)]
    uv_sb = ph.t("uv_sb", [128, 1024], F32)
    qst_sb = [ph.t(f"qst{i}", [128, HPC, 512], BF16) for i in range(2)]
    kst_sb = [ph.t(f"kst{i}", [128, HPC, 512], BF16) for i in range(2)]
    wvT_sb = ph.t("wvT_sb", [128, 32, EOWN], BF16)
    vst_sb = [ph.t(f"vst{i}", [128, 4 * EOWN], BF16) for i in range(2)]

    pa = Ph()
    k_sb = [pa.t(f"k_sb{i}", [128, S], BF16) for i in range(2)]
    v_sb = [pa.t(f"v_sb{i}", [128, S // 128, HD], BF16) for i in range(2)]
    q_sb = [pa.t(f"q_sb{i}", [128, S], BF16) for i in range(2)]
    pring = pa.t("pring", [128, 8, 1024], BF16)
    accb = pa.t("accb", [128, 1024], BF16)
    acc32 = pa.t("acc32", [128, 1024], F32)
    accf = pa.t("accf", [128, 512], F32)
    recip_sb = pa.t("recip_sb", [1, 512], F32)
    zb_sb = pa.t("zb_sb", [128, 512], F32)
    ctxst = [pa.t(f"ctxst{i}", [128, 512], BF16) for i in range(2)]

    po = Ph()
    s1r_sb = [po.t(f"s1r{i}", [128, TOK], F32) for i in range(2)]
    xfb_st = [po.t(f"xfb{i}", [128, TOK], BF16) for i in range(2)]
    norm_top = po.off
    ctx_sb = po.t("ctx_sb", [128, 32, TOK], BF16)
    woT_sb = [po.t(f"woT{i}", [128, 32, 128], BF16) for i in range(2)]
    xo_sb = [po.t(f"xo{i}", [128, TOK], F32) for i in range(2)]
    s1_st = [po.t(f"s1st{i}", [128, TOK], F32) for i in range(2)]
    s1b_sb = po.t("s1b_sb", [128, TOK], BF16)
    sqb_sb = po.t("sqb_sb", [128, TOK], BF16)
    mu_sb = po.t("mu_sb", [1, TOK], F32)
    msq_sb = po.t("msq_sb", [1, TOK], F32)
    var_sb = po.t("var_sb", [1, TOK], F32)
    var2_sb = po.t("var2_sb", [1, TOK], F32)
    rstd_sb = po.t("rstd_sb", [1, TOK], F32)

    pd = Ph()
    pd.off = norm_top
    gi_i = pd.t("gi_i", [1, D], I32)
    gi_f = pd.t("gi_f", [1, D], F32)
    oh_sb = pd.t("oh_sb", [86, D], BF16)
    z_sb = pd.t("z_sb", [86, D], I32)
    zp1_sb = pd.t("zp1_sb", [86, D], BF16)
    sc_sb = pd.t("sc_sb", [86, D], BF16)
    qz_sb = pd.t("qz_sb", [86, 512], I32)
    qw_sb = [pd.t(f"qw{i}", [128, D], I32) for i in range(2)]
    w4f_sb = pd.t("w4f_sb", [128, D], F32)
    tdq_sb = pd.t("tdq_sb", [128, 512], F32)
    wst_sb = [pd.t(f"wst{i}", [128, D], BF16) for i in range(2)]

    pf = Ph()
    wres_raw = pf.t("wres_sb", [128, NKT * D], BF16)
    wres_gu = wres_raw[:, 0:32 * NSL].rearrange("p (k n) -> p k n", n=NSL)
    wres_d = wres_raw.rearrange("p (k n) -> p k n", n=D)
    xf_off = pf.off
    xf_sb = [pf.t(f"xf{i}", [128, 32, 512], BF16) for i in range(2)]
    hst_sb = [pf.t(f"hst{i}", [128, NKT, 512], BF16) for i in range(2)]
    hgr_sb = [pf.t(f"hgr{i}", [128, NKT, 512], BF16) for i in range(2)]
    yst_sb = [nc.alloc_sbuf_tensor_at(
        f"yst{i}", [128, 8, 512], BF16,
        offset=((xf_off + 31) & ~31) + i * 8 * 512 * 2).ap() for i in range(2)]

    pz = Ph()
    s1f_sb = [pz.t(f"s1f{i}", [128, 32, 128], F32) for i in range(4)]
    ysl_sb = [pz.t(f"ysl{i}", [128, 32, 128], BF16) for i in range(4)]
    s2_sb = [pz.t(f"s2_{i}", [128, 32, 128], F32) for i in range(2)]
    tmaj_sb = [pz.t(f"tmaj{i}", [128, D], F32) for i in range(2)]
    t3c_sb = pz.t("t3c_sb", [128, 1024], F32)
    out_st = [pz.t(f"outst{i}", [128, D], F32) for i in range(2)]

    ps_a = nc.alloc_psum_tensor("ps_a", [128, 2048], F32).ap()
    ps_b = nc.alloc_psum_tensor("ps_b", [128, 2048], F32).ap()
    ps_s = [ps_a[:, 0:1024], ps_a[:, 1024:2048], ps_b[:, 0:1024]]
    ps_o = ps_b[:, 1024:1536]
    ps_zzb = ps_b[:, 1536:2048]
    ps_q = ps_a
    ps_k = ps_b
    ps_y = [ps_a[:, 0:1024], ps_a[:, 1024:2048]]
    ps_mu = ps_b[0:1, 0:1024]
    ps_sq = ps_b[0:1, 1024:2048]
    ps_bc = ps_a
    ps_dq = [[ps_a[:, 0:512], ps_a[:, 512:1024]],
             [ps_a[:, 1024:1536], ps_a[:, 1536:2048]]]
    ps_ohb = ps_b[0:86, 0:512]
    ps_h = [ps_a[:, 0:512], ps_a[:, 512:1024], ps_a[:, 1024:1536], ps_a[:, 1536:2048]]
    ps_y3 = [ps_b[:, 0:512], ps_b[:, 512:1024]]
    ps_tr = [ps_b[:, 0:128], ps_b[:, 512:640]]

    NPRE = 10
    vch = [0]
    sch = [0]

    with es:
        with nc.Block() as block:

            @block.sync
            def _(sync):
                sync.dma_start(out=ident_sb, in_=ident).then_inc(dPre, 16)
                sync.dma_start(out=ones_row_sb, in_=ones_row).then_inc(dPre, 16)
                sync.dma_start(out=ones_col_f_sb, in_=ones_col_f).then_inc(dPre, 16)
                sync.dma_start(out=ones_col_b_sb, in_=ones_col_b).then_inc(dPre, 16)
                sync.dma_start(out=iota32_sb, in_=iota32).then_inc(dPre, 16)
                sync.dma_start(out=iota86_sb, in_=iota86).then_inc(dPre, 16)
                sync.dma_start(out=shifts_sb, in_=shifts).then_inc(dPre, 16)
                sync.dma_start(out=cv, in_=cvec).then_inc(dPre, 16)

                for c in range(SC if upto >= 1 else 0):
                    sl = slice(c * 512, (c + 1) * 512)
                    if c >= 2:
                        sync.wait_ge(vRope, 2 * (c - 1))
                    sync.dma_start(out=cos_sb[c % 2], in_=cos2[:, sl]).then_inc(dCS[c % 2], 16)
                    sync.dma_start(out=sin_sb[c % 2], in_=sin2[:, sl]).then_inc(dCS[c % 2], 16)
                    sync.wait_ge(vRope, 2 * c + 1)
                    sync.dma_start(out=q_d[:, :, sl].rearrange("h p s -> p h s"),
                                   in_=qst_sb[c % 2]).then_inc(dSpQ[c % 2], 16)
                    sync.wait_ge(vRope, 2 * c + 2)
                    sync.dma_start(out=k_d[:, :, sl].rearrange("h p s -> p h s"),
                                   in_=kst_sb[c % 2]).then_inc(dSpK[c % 2], 16)
                for c in range(SC if upto >= 2 else 0):
                    sl = slice(c * 512, (c + 1) * 512)
                    sync.wait_ge(sEv, c + 1)
                    vv = vst_sb[c % 2].rearrange("p (t h d) -> p t h d", t=4, h=HPC)
                    for hh in range(HPC):
                        sync.dma_start(
                            out=v_d[hh, :, 4 * c:4 * c + 4, :].rearrange("p t d -> p t d"),
                            in_=vv[:, :, hh, :]).then_inc(dSpV[c % 2], 16)

                for h in range(HPC if upto >= 3 else 0):
                    if h == 0:
                        for sm_ in (dSpQ[0], dSpQ[1], dSpK[0], dSpK[1]):
                            sync.wait_ge(sm_, 16 * 8)
                        sync.wait_ge(dSpV[0], 64 * 8)
                        sync.wait_ge(dSpV[1], 64 * 8)
                    elif h >= 2:
                        sync.wait_ge(tPv, 32 * QC * (h - 1))
                    sync.dma_start(out=k_sb[h % 2], in_=k_d[h]).then_inc(dKV[h % 2], 16)
                    sync.dma_start(out=v_sb[h % 2], in_=v_d[h]).then_inc(dKV[h % 2], 16)
                    sync.dma_start(out=q_sb[h % 2], in_=q_d[h]).then_inc(dKV[h % 2], 16)

                if upto < 4:
                    return
                sync.wait_ge(cA2A, HPC)
                for h in range(HPC):
                    sync.dma_start(out=ctx_sb[:, h::HPC, :],
                                   in_=a2a_out[h].rearrange("r p t -> p r t")
                                   ).then_inc(dCtxIn, 16)
                for ec in range(32):
                    if ec >= 2:
                        sync.wait_ge(vS1, ec - 1)
                    sync.dma_start(out=xo_sb[ec % 2], in_=x_own[:, ec, :]).then_inc(dXo[ec % 2], 16)
                    sync.wait_ge(vS1, ec + 1)
                    sync.dma_start(out=s1_d[ec], in_=s1_st[ec % 2]).then_inc(dS1[ec % 2], 16)
                def sy_norm(ec):
                    if ec >= 2:
                        sync.wait_ge(vXf, ec - 1)
                    sync.dma_start(out=s1r_sb[ec % 2], in_=s1_d[ec]).then_inc(dS1r[ec % 2], 16)
                    sync.wait_ge(vXf, ec + 1)
                    sync.dma_start(out=agx_in[ec], in_=xfb_st[ec % 2]).then_inc(dAgx[ec % 2], 16)

                if upto < 6:
                    for ec in range(32):
                        sy_norm(ec)
                    return
                sync.wait_ge(tY, 32)
                sync.dma_start(out=qz_sb[0:NG, 0:NSLP], in_=g_qz).then_inc(dDq, 16)
                sync.dma_start(out=gi_i, in_=g_gi).then_inc(dDq, 16)
                for ktn, (mat, kt, rows, nwid, ngr, klen) in enumerate(DQT):
                    if ktn == 0:
                        sync.wait_ge(vS1, 32)
                        sync.wait_ge(dS1[0], 256)
                        sync.wait_ge(dS1[1], 256)
                        sync.wait_ge(tBc, 1)
                    if ktn >= 2:
                        sync.wait_ge(vDq, UOFF[ktn - 1])
                    sync.dma_start(
                        out=qw_sb[ktn % 2][0:rows, 0:nwid],
                        in_=(g_qw, u_qw, d_qw)[mat][kt * 128: kt * 128 + rows, :]
                    ).then_inc(dQw[ktn % 2], 16)
                    if ktn < 32:
                        sy_norm(ktn)
                    if ktn >= 1:
                        pm, pk, prow, pnw, _, _ = DQT[ktn - 1]
                        sync.wait_ge(vDq, UOFF[ktn])
                        sync.dma_start(
                            out=WMAT[pm][pk][0:prow, :],
                            in_=wst_sb[(ktn - 1) % 2][0:prow, 0:pnw]).then_inc(dWbf[(ktn - 1) % 2], 16)
                    if mat < 2 and kt == 31:
                        sync.wait_ge(vOh, mat + 1)
                        sync.wait_ge(tOh, OHOFF[mat + 1])
                        nxt_qz, nxt_gi = (u_qz, u_gi) if mat == 0 else (d_qz, d_gi)
                        sync.dma_start(out=qz_sb[0:(NG if mat == 0 else NGD),
                                                 0:(NSLP if mat == 0 else D // 8)],
                                       in_=nxt_qz).then_inc(dDq, 16)
                        sync.dma_start(out=gi_i[:, 0:(D if mat == 0 else NSL)],
                                       in_=nxt_gi).then_inc(dDq, 16)
                pm, pk, prow, pnw, _, _ = DQT[NDQ - 1]
                sync.wait_ge(vDq, NDQU)
                sync.dma_start(out=WMAT[pm][pk][0:prow, :],
                               in_=wst_sb[(NDQ - 1) % 2][0:prow, 0:pnw]).then_inc(dWbf[(NDQ - 1) % 2], 16)

                if upto < 7:
                    return
                sync.wait_ge(dWbf[0], 16 * 38)
                sync.wait_ge(dWbf[1], 16 * 37)
                sync.wait_ge(dAgx[0], 16 * 16)
                sync.wait_ge(dAgx[1], 16 * 16)
                sync.dma_start(out=wres_gu,
                               in_=wg_d.rearrange("k p n -> p k n")).then_inc(dWres, 16)
                for tc in range(16):
                    r, hf = tc // 2, tc % 2
                    if tc >= 2:
                        sync.wait_ge(tH, NKT * (tc - 1))
                    for i in range(4):
                        if tc == 0:
                            sync.wait_ge(cAGX, i + 1)
                        sync.dma_start(
                            out=xf_sb[tc % 2][:, 8 * i:8 * (i + 1), :],
                            in_=agx_out[i][r, :, :, hf * 512:(hf + 1) * 512].rearrange(
                                "m p t -> p m t")).then_inc(dXf[tc % 2], 16)
                    sync.wait_ge(sSilu, NKT * (tc + 1))
                    sync.dma_start(
                        out=hg_d[:, :, tc * 512:(tc + 1) * 512].rearrange("n p t -> p n t"),
                        in_=hst_sb[tc % 2]).then_inc(dHg[tc % 2], 16)
                sync.wait_ge(tH, NKT * 16)
                sync.dma_start(out=wres_gu,
                               in_=wu_d.rearrange("k p n -> p k n")).then_inc(dWres, 16)
                for tc in range(16):
                    if tc >= 2:
                        sync.wait_ge(vH2, NKT * (tc - 1))
                        sync.wait_ge(dH2[tc % 2], 16 * (tc // 2))
                    sync.dma_start(
                        out=hgr_sb[tc % 2],
                        in_=hg_d[:, :, tc * 512:(tc + 1) * 512].rearrange(
                            "n p t -> p n t")).then_inc(dHgr[tc % 2], 16)
                    sync.wait_ge(vH2, NKT * (tc + 1))
                    sync.dma_start(
                        out=h_d[:, :, tc * 512:(tc + 1) * 512].rearrange("n p t -> p n t"),
                        in_=hgr_sb[tc % 2]).then_inc(dH2[tc % 2], 16)
                if upto < 8:
                    return
                sync.wait_ge(tH2, NKT * 16)
                sync.wait_ge(dH2[0], 16 * 8)
                sync.wait_ge(dH2[1], 16 * 8)
                sync.dma_start(out=wres_d,
                               in_=wd_d.rearrange("k p n -> p k n")).then_inc(dWres, 16)
                i3 = 0
                for j in range(4):
                    for tc in range(16):
                        r, hf = tc // 2, tc % 2
                        if i3 >= 2:
                            sync.wait_ge(tY3, 8 * (i3 - 1))
                        sync.dma_start(
                            out=hgr_sb[i3 % 2],
                            in_=h_d[:, :, tc * 512:(tc + 1) * 512].rearrange(
                                "n p t -> p n t")).then_inc(dH3[i3 % 2], 16)
                        i3 += 1

                if "rs_dbg" in dbg:
                    sync.wait_ge(dRs4[0], 16 * 16)
                    sync.dma_start(out=rsi_dbg, in_=rs_in[0]).then_inc(dPre, 16)
                    sync.wait_ge(cRS, 1)
                    sync.dma_start(out=rso_dbg, in_=rs_out[0]).then_inc(dPre, 16)
                if upto < 9:
                    return
                for tt in range(NTT):
                    tsl = slice(tt * 128, (tt + 1) * 128)
                    if tt == 0:
                        sync.wait_ge(tY3, 8 * 64)
                    if tt >= 4:
                        sync.wait_ge(tQ, 16 + 4 * (tt - 3))
                    sync.dma_start(
                        out=s1f_sb[tt % 4],
                        in_=s1_d[:, :, tsl].rearrange(
                            "e p t -> p e t")).then_inc(sF1[tt % 4][0], 16)

            @block.gpsimd
            def _(gpsimd):
                gpsimd.dma_start(out=wqT_sb, in_=wqT).then_inc(dPre, 16)
                gpsimd.dma_start(out=wkT_sb, in_=wkT).then_inc(dPre, 16)
                gpsimd.dma_start(out=wvT_sb, in_=wvT).then_inc(dPre, 16)
                for c in range(SC if upto >= 1 else 0):
                    if c >= 2:
                        gpsimd.wait_ge(tK, c - 1)
                    gpsimd.dma_start(out=xt_sb[c % 2],
                                     in_=xh[:, :, c * 512:(c + 1) * 512]).then_inc(dX[c % 2], 16)
                if upto < 2:
                    return
                gpsimd.wait_ge(tK, SC)
                for c in range(SC):
                    if c >= 2:
                        gpsimd.wait_ge(tV, c - 1)
                    gpsimd.dma_start(out=xt_sb[c % 2],
                                     in_=xh[:, :, c * 512:(c + 1) * 512]).then_inc(dXv[c % 2], 16)
                if upto < 3:
                    return
                for ci in range(NCH):
                    h, qc = divmod(ci, QC)
                    gpsimd.wait_ge(vO, ci + 1)
                    gpsimd.dma_start(
                        out=a2a_in[h][qc // 2, :, (qc % 2) * 512:((qc % 2) + 1) * 512],
                        in_=ctxst[ci % 2]).then_inc(dCtx[ci % 2], 16)
                    if qc == QC - 1 and upto >= 4:
                        gpsimd.wait_ge(dCtx[0], 16 * 8 * (h + 1))
                        gpsimd.wait_ge(dCtx[1], 16 * 8 * (h + 1))
                        gpsimd.collective_compute(
                            "AllToAll", ALU.bypass, replica_groups=RG,
                            ins=[a2a_in[h].opt()],
                            outs=[a2a_out[h].opt()]).then_inc(cA2A, 1)
                if upto < 4:
                    return
                for ec in range(32):
                    if ec >= 2:
                        gpsimd.wait_ge(tY, ec - 1)
                    gpsimd.dma_start(out=woT_sb[ec % 2],
                                     in_=woT[:, :, ec * 128:(ec + 1) * 128]).then_inc(dWo[ec % 2], 16)
                if upto >= 6:
                    gpsimd.wait_ge(tY, 32)
                    gpsimd.dma_start(out=sc_sb[0:NG, 0:NSL], in_=g_sc).then_inc(dSc, 16)
                if upto < 5:
                    return
                for i in range(4):
                    gpsimd.wait_ge(dAgx[0], 16 * 4 * (i + 1))
                    gpsimd.wait_ge(dAgx[1], 16 * 4 * (i + 1))
                    gpsimd.collective_compute(
                        "AllGather", ALU.bypass, replica_groups=RG,
                        ins=[agx_in[8 * i:8 * (i + 1)].opt()],
                        outs=[agx_out[i].opt()]).then_inc(cAGX, 1)
                if upto < 6:
                    return
                gpsimd.wait_ge(tG, UMAT[1])
                gpsimd.dma_start(out=sc_sb[0:NG, 0:NSL], in_=u_sc).then_inc(dSc, 16)
                gpsimd.wait_ge(tG, UMAT[2])
                gpsimd.dma_start(out=sc_sb[0:NGD, 0:D], in_=d_sc).then_inc(dSc, 16)
                if upto >= 7:
                    gpsimd.wait_ge(tH, NKT * 16)
                    for tc in range(16):
                        r, hf = tc // 2, tc % 2
                        if tc >= 2:
                            gpsimd.wait_ge(tH2, NKT * (tc - 1))
                        for i in range(4):
                            gpsimd.dma_start(
                                out=xf_sb[tc % 2][:, 8 * i:8 * (i + 1), :],
                                in_=agx_out[i][r, :, :, hf * 512:(hf + 1) * 512].rearrange(
                                    "m p t -> p m t")).then_inc(dSpQ[tc % 2], 16)
                if upto < 8:
                    return
                for j in range(4):
                    gpsimd.wait_ge(dRs8[j][0], 16 * 8)
                    gpsimd.wait_ge(dRs8[j][1], 16 * 8)
                    gpsimd.collective_compute(
                        "ReduceScatter", ALU.add, replica_groups=RG,
                        ins=[rs_in[j].opt()], outs=[rs_out[j].opt()]).then_inc(cRS, 1)
                if upto < 9:
                    return
                for tt in range(NTT):
                    tsl = slice(tt * 128, (tt + 1) * 128)
                    if tt == 0:
                        gpsimd.wait_ge(cRS, 4)
                    if tt >= 4:
                        gpsimd.wait_ge(tQ, 16 + 4 * (tt - 3))
                    for j in range(4):
                        gpsimd.dma_start(
                            out=ysl_sb[tt % 4][:, j * 8:(j + 1) * 8, :],
                            in_=rs_out[j][:, tsl].rearrange(
                                "(ee p) t -> p ee t", p=128)).then_inc(sYs[tt % 4][0], 16)

            @block.tensor
            def _(tensor):
                tensor.wait_ge(dPre, 16 * NPRE)
                for c in range(SC if upto >= 1 else 0):
                    tensor.wait_ge(dX[c % 2], 16 * (c // 2 + 1))
                    if c >= 1:
                        tensor.wait_ge(vRope, 2 * c - 1)
                    for et in range(HPC):
                        for m in range(32):
                            last = tensor.matmul(ps_q[:, et * 512:(et + 1) * 512],
                                          wqT_sb[:, m, et * 128:(et + 1) * 128],
                                          xt_sb[c % 2][:, m, :],
                                          start=(m == 0), stop=(m == 31))
                    last.then_inc(tQ, 1)
                    if c >= 1:
                        tensor.wait_ge(vRope, 2 * c)
                    for et in range(HPC):
                        for m in range(32):
                            last = tensor.matmul(ps_k[:, et * 512:(et + 1) * 512],
                                          wkT_sb[:, m, et * 128:(et + 1) * 128],
                                          xt_sb[c % 2][:, m, :],
                                          start=(m == 0), stop=(m == 31))
                    last.then_inc(tK, 1)
                if upto < 2:
                    return
                tensor.wait_ge(vRope, 2 * SC)
                for c in range(SC):
                    tensor.wait_ge(dXv[c % 2], 16 * (c // 2 + 1))
                    tensor.wait_ge(dPre, 16 * (NPRE + 1))
                    if c >= 1:
                        tensor.wait_ge(sEv, c)
                    for st in range(4):
                        for m in range(32):
                            last = tensor.matmul(ps_q[:, st * 512:(st + 1) * 512],
                                          xt_sb[c % 2][:, m, st * 128:(st + 1) * 128],
                                          wvT_sb[:, m, :],
                                          start=(m == 0), stop=(m == 31))
                    last.then_inc(tV, 1)

                if upto < 3:
                    return
                def z_mm(ci_):
                    tensor.wait_ge(vZf, ci_ + 1)
                    tensor.wait_ge(sZb, ci_)
                    tensor.matmul(ps_zzb[0:1, :], ones_col_f_sb, accf,
                                  start=True, stop=True).then_inc(tZ, 1)

                def zb_mm(ci_):
                    tensor.wait_ge(vRec, ci_ + 1)
                    tensor.matmul(ps_zzb, ones_row_sb, recip_sb,
                                  start=True, stop=True).then_inc(tZb, 1)

                for h in range(HPC):
                    tensor.wait_ge(dKV[h % 2], 16 * 3 * (h // 2 + 1))
                    for qc in range(QC):
                        ci = h * QC + qc
                        qsl = slice(qc * 512, (qc + 1) * 512)
                        for g in range(32):
                            gg = 32 * ci + g
                            if ci >= 1 and g == 0:
                                z_mm(ci - 1)
                            if ci >= 1 and g == 2:
                                zb_mm(ci - 1)
                            tensor.wait_ge(sExp, max(0, gg - 2))
                            if ci == 0 and g == 0:
                                tensor.wait_ge(sEv, SC)
                            for jj in range(2):
                                kt = 2 * g + jj
                                last = tensor.matmul(ps_s[g % 3][:, jj * 512:(jj + 1) * 512],
                                              k_sb[h % 2][:, kt * 128:(kt + 1) * 128],
                                              q_sb[h % 2][:, qsl],
                                              start=True, stop=True)
                            last.then_inc(tS, 1)
                            if g >= 4:
                                tensor.wait_ge(sExp, gg - 3)
                                if g == 4:
                                    tensor.wait_ge(vO, ci)
                                for jj in range(2):
                                    kt = 2 * (g - 4) + jj
                                    last = tensor.matmul(ps_o, v_sb[h % 2][:, kt, :],
                                                  pring[:, (g - 4) % 8,
                                                        jj * 512:(jj + 1) * 512],
                                                  start=(kt == 0), stop=False)
                                last.then_inc(tPv, 1)
                        for gl in (28, 29, 30, 31):
                            gg = 32 * ci + gl
                            tensor.wait_ge(sExp, gg + 1)
                            for jj in range(2):
                                kt = 2 * gl + jj
                                last = tensor.matmul(ps_o, v_sb[h % 2][:, kt, :],
                                              pring[:, gl % 8, jj * 512:(jj + 1) * 512],
                                              start=False, stop=(kt == 63))
                            last.then_inc(tPv, 1)
                z_mm(NCH - 1)
                zb_mm(NCH - 1)

                if upto < 4:
                    return
                tensor.wait_ge(dCtxIn, 16 * HPC)
                for ec in range(32):
                    tensor.wait_ge(dWo[ec % 2], 16 * (ec // 2 + 1))
                    if ec < 2:
                        tensor.wait_ge(sZb, NCH)
                        tensor.wait_ge(vO, NCH)
                    else:
                        tensor.wait_ge(vS1, ec - 1)
                    for ts in range(2):
                        for m in range(32):
                            last = tensor.matmul(ps_y[ec % 2][:, ts * 512:(ts + 1) * 512],
                                          woT_sb[ec % 2][:, m, :],
                                          ctx_sb[:, m, ts * 512:(ts + 1) * 512],
                                          start=(m == 0), stop=(m == 31))
                    last.then_inc(tY, 1)
                    tensor.wait_ge(vS1, ec + 1)
                    for ts in range(2):
                        tensor.matmul(ps_mu[:, ts * 512:(ts + 1) * 512], ones_col_b_sb,
                                      s1b_sb[:, ts * 512:(ts + 1) * 512],
                                      start=(ec == 0), stop=(ec == 31))
                        last = tensor.matmul(ps_sq[:, ts * 512:(ts + 1) * 512], ones_col_b_sb,
                                      sqb_sb[:, ts * 512:(ts + 1) * 512],
                                      start=(ec == 0), stop=(ec == 31))
                    if ec == 31:
                        last.then_inc(tSt, 1)
                tensor.wait_ge(sRstd, 1)
                for ts in range(2):
                    tensor.matmul(ps_bc[:, ts * 512:(ts + 1) * 512], ones_row_sb,
                                  mu_sb[:, ts * 512:(ts + 1) * 512], start=True, stop=True)
                    last = tensor.matmul(ps_bc[:, 1024 + ts * 512:1024 + (ts + 1) * 512],
                                  ones_row_sb, rstd_sb[:, ts * 512:(ts + 1) * 512],
                                  start=True, stop=True)
                last.then_inc(tBc, 1)

                if upto < 6:
                    return
                tensor.wait_ge(vBc, 1)
                for (ktn, mat, kt, rows, nwid, ngr, klen, chk, wid) in DQU:
                    if kt == 0 and chk == 0:
                        tensor.wait_ge(vOh, mat + 1)
                        for ochk in range(OHC[mat]):
                            owid = min(512, klen - ochk * 512)
                            if OHOFF[mat] + ochk >= 1:
                                tensor.wait_ge(vOh2, OHOFF[mat] + ochk)
                            tensor.matmul(ps_ohb[0:ngr, 0:owid], ones_row_sb[:, 0:ngr],
                                          gi_f[:, ochk * 512:ochk * 512 + owid],
                                          start=True, stop=True).then_inc(tOh, 1)
                        tensor.wait_ge(vOh2, OHOFF[mat] + OHC[mat])
                        tensor.wait_ge(dSc, 16 * (mat + 1))
                    u = UOFF[ktn] + chk
                    if u >= 2:
                        tensor.wait_ge(vDq, u - 1)
                    pss = ps_dq[u % 2]
                    tensor.matmul(pss[0][0:rows, 0:wid],
                                  oh_sb[0:ngr, kt * 128:kt * 128 + rows],
                                  sc_sb[0:ngr, chk * 512:chk * 512 + wid],
                                  start=True, stop=True)
                    tensor.matmul(pss[1][0:rows, 0:wid],
                                  oh_sb[0:ngr, kt * 128:kt * 128 + rows],
                                  zp1_sb[0:ngr, chk * 512:chk * 512 + wid],
                                  start=True, stop=True).then_inc(tG, 1)

                if upto < 7:
                    return
                tensor.wait_ge(vDq, NDQU)
                tensor.wait_ge(dWres, 16)
                for tc in range(16):
                    for nt in range(NKT):
                        rows = 128 if nt < 10 else 96
                        it = NKT * tc + nt
                        if it >= 4:
                            tensor.wait_ge(sSilu, it - 3)
                        for kt in range(32):
                            if nt == 0 and kt % 8 == 0:
                                tensor.wait_ge(dXf[tc % 2],
                                               64 * (tc // 2) + 16 * (kt // 8 + 1))
                            last = tensor.matmul(ps_h[it % 4][0:rows, :],
                                          wres_gu[:, kt, nt * 128:nt * 128 + rows],
                                          xf_sb[tc % 2][:, kt, :],
                                          start=(kt == 0), stop=(kt == 31))
                        last.then_inc(tH, 1)
                tensor.wait_ge(dWres, 32)
                tensor.wait_ge(sSilu, NKT * 16)
                for tc in range(16):
                    tensor.wait_ge(dSpQ[tc % 2], 128 + 64 * (tc // 2 + 1))
                    for nt in range(NKT):
                        rows = 128 if nt < 10 else 96
                        it = NKT * tc + nt
                        if it >= 4:
                            tensor.wait_ge(vH2, it - 3)
                        for kt in range(32):
                            last = tensor.matmul(ps_h[it % 4][0:rows, :],
                                          wres_gu[:, kt, nt * 128:nt * 128 + rows],
                                          xf_sb[tc % 2][:, kt, :],
                                          start=(kt == 0), stop=(kt == 31))
                        last.then_inc(tH2, 1)
                if upto < 8:
                    return
                tensor.wait_ge(dWres, 48)
                tensor.wait_ge(vH2, NKT * 16)
                i3 = 0
                for j in range(4):
                    for tc in range(16):
                        tensor.wait_ge(dH3[i3 % 2], 16 * (i3 // 2 + 1))
                        for ecc in range(8):
                            ie = 8 * i3 + ecc
                            if ie >= 2:
                                tensor.wait_ge(vE3, ie - 1)
                            esl = slice((j * 8 + ecc) * 128, (j * 8 + ecc + 1) * 128)
                            for nt in range(NKT):
                                rows = 128 if nt < 10 else 96
                                last = tensor.matmul(ps_y3[ecc % 2],
                                              wres_d[0:rows, nt, esl],
                                              hgr_sb[i3 % 2][0:rows, nt, :],
                                              start=(nt == 0), stop=(nt == NKT - 1))
                            last.then_inc(tY3, 1)
                        i3 += 1

                if upto < 9:
                    return
                for tt in range(NTT):
                    for cc in range(4):
                        tensor.wait_ge(tQ, 16 + 4 * tt + cc + 1)
                        for k in range(8):
                            ec = cc * 8 + k
                            ie = 32 * tt + ec
                            if ie >= 2:
                                tensor.wait_ge(sTe, ie - 1)
                            tensor.transpose(ps_tr[ec % 2], s2_sb[tt % 2][:, ec, :],
                                             ident_sb).then_inc(tTr, 1)

            @block.vector
            def _(vector):
                vector.wait_ge(dPre, 16 * NPRE)
                for c in range(SC if upto >= 1 else 0):
                    vector.wait_ge(dCS[c % 2], 32 * (c // 2 + 1))
                    if c >= 2:
                        vector.wait_ge(dSpQ[c % 2], 16 * (c // 2))
                        vector.wait_ge(dSpK[c % 2], 16 * (c // 2))
                    vector.wait_ge(tQ, c + 1)
                    for et in range(HPC):
                        psl = ps_q[:, et * 512:(et + 1) * 512]
                        vector.tensor_mul(uv_sb[:, 0:512], psl, cos_sb[c % 2])
                        vector.tensor_mul(psl, psl, sin_sb[c % 2])
                        vector.tensor_sub(qst_sb[c % 2][0:64, et, :],
                                          uv_sb[0:64, 0:512], psl[64:128, :])
                        last = vector.tensor_add(qst_sb[c % 2][64:128, et, :],
                                          uv_sb[64:128, 0:512], psl[0:64, :])
                    last.then_inc(vRope, 1)
                    vector.wait_ge(tK, c + 1)
                    for et in range(HPC):
                        psl = ps_k[:, et * 512:(et + 1) * 512]
                        vector.tensor_mul(uv_sb[:, 0:512], psl, cos_sb[c % 2])
                        vector.tensor_mul(psl, psl, sin_sb[c % 2])
                        vector.tensor_sub(kst_sb[c % 2][0:64, et, :],
                                          uv_sb[0:64, 0:512], psl[64:128, :])
                        last = vector.tensor_add(kst_sb[c % 2][64:128, et, :],
                                          uv_sb[64:128, 0:512], psl[0:64, :])
                    last.then_inc(vRope, 1)

                if upto < 3:
                    return
                def v_recip(ci_):
                    vector.wait_ge(tZ, ci_ + 1)
                    vector.reciprocal(recip_sb, ps_zzb[0:1, :]).then_inc(vRec, 1)

                def v_zbcopy(ci_):
                    vector.wait_ge(tZb, ci_ + 1)
                    vector.tensor_copy(zb_sb, ps_zzb).then_inc(sZb, 1)

                def v_oevac(ci_):
                    vector.wait_ge(tPv, 32 * (ci_ + 1))
                    if ci_ >= 2:
                        vector.wait_ge(dCtx[ci_ % 2], 16 * (ci_ // 2))
                    vector.tensor_mul(ctxst[ci_ % 2], ps_o, zb_sb).then_inc(vO, 1)

                for ci in range(NCH):
                    for g in range(32):
                        gg = 32 * ci + g
                        if ci >= 1 and g == 2:
                            v_recip(ci - 1)
                        if ci >= 1 and g == 3:
                            v_zbcopy(ci - 1)
                        if ci >= 1 and g == 4:
                            v_oevac(ci - 1)
                        vector.wait_ge(sExp, gg + 1)
                        pg = pring[:, g % 8, :]
                        if g % 8 == 0:
                            last = vector.tensor_copy(accb, pg)
                        else:
                            last = vector.tensor_add(accb, accb, pg)
                        if g % 8 == 7:
                            if g == 7:
                                last = vector.tensor_copy(acc32, accb)
                            else:
                                last = vector.tensor_add(acc32, acc32, accb)
                        last.then_inc(vZ, 1)
                    vector.wait_ge(tZ, ci)
                    vector.tensor_add(accf, acc32[:, 0:512], acc32[:, 512:1024]).then_inc(vZf, 1)
                v_recip(NCH - 1)
                v_zbcopy(NCH - 1)
                v_oevac(NCH - 1)

                if upto < 4:
                    return
                for ec in range(32):
                    vector.wait_ge(tY, ec + 1)
                    vector.wait_ge(dXo[ec % 2], 16 * (ec // 2 + 1))
                    if ec >= 2:
                        vector.wait_ge(dS1[ec % 2], 16 * (ec // 2))
                    vector.tensor_add(s1_st[ec % 2], ps_y[ec % 2], xo_sb[ec % 2])
                    vector.tensor_copy(s1b_sb, s1_st[ec % 2])
                    vector.tensor_mul(sqb_sb, s1b_sb, s1b_sb).then_inc(vS1, 1)
                vector.wait_ge(tSt, 1)
                vector.tensor_scalar(mu_sb, ps_mu, 1.0 / D, None, ALU.mult)
                vector.tensor_scalar(msq_sb, ps_sq, 1.0 / D,
                                     None, ALU.mult).then_inc(vCh, 1)
                vector.wait_ge(vCh, vch[0] + 1)
                vch[0] += 1
                vector.tensor_mul(var_sb, mu_sb, mu_sb).then_inc(vCh, 1)
                vector.wait_ge(vCh, vch[0] + 1)
                vch[0] += 1
                vector.tensor_sub(var2_sb, msq_sb, var_sb).then_inc(vSt, 1)
                vector.wait_ge(tBc, 1)
                vector.tensor_copy(mub_sb, ps_bc[:, 0:1024])
                vector.tensor_copy(rstdb_sb, ps_bc[:, 1024:2048]).then_inc(vBc, 1)
                def v_norm(ec):
                    vector.wait_ge(dS1r[ec % 2], 16 * (ec // 2 + 1))
                    if ec >= 2:
                        vector.wait_ge(dAgx[ec % 2], 16 * (ec // 2))
                    vector.tensor_sub(s1r_sb[ec % 2], s1r_sb[ec % 2], mub_sb)
                    vector.tensor_mul(xfb_st[ec % 2], s1r_sb[ec % 2], rstdb_sb).then_inc(vXf, 1)

                if upto < 6:
                    for ec in range(32):
                        v_norm(ec)
                    return
                nrm = [0]
                for (ktn, mat, kt, rows, nwid, ngr, klen, chk, wid) in DQU:
                    u = UOFF[ktn] + chk
                    if u % 3 == 0 and nrm[0] < 32:
                        v_norm(nrm[0])
                        nrm[0] += 1
                    pss = ps_dq[u % 2]
                    if kt == 0 and chk == 0:
                        npk = NSLP if mat < 2 else D // 8
                        vector.wait_ge(dDq, 16 * 2 * (mat + 1))
                        vector.wait_ge(dSc, 16 * (mat + 1))
                        if mat >= 1:
                            vector.wait_ge(tOh, OHOFF[mat])
                            vector.wait_ge(tG, UMAT[mat])
                        vector.tensor_copy(gi_f[:, 0:klen], gi_i[:, 0:klen])
                        for jjj in range(8):
                            vector.tensor_scalar(z_sb[0:ngr, jjj:nwid:8],
                                                 qz_sb[0:ngr, 0:npk],
                                                 4 * jjj, 15,
                                                 ALU.logical_shift_right, ALU.bitwise_and)
                        vector.tensor_copy(w4f_sb[0:ngr, 0:nwid], z_sb[0:ngr, 0:nwid])
                        vector.tensor_scalar(zp1_sb[0:ngr, 0:nwid], w4f_sb[0:ngr, 0:nwid],
                                             1.0, None, ALU.add).then_inc(vOh, 1)
                        for ochk in range(OHC[mat]):
                            owid = min(512, klen - ochk * 512)
                            vector.wait_ge(tOh, OHOFF[mat] + ochk + 1)
                            vector.tensor_scalar(
                                oh_sb[0:ngr, ochk * 512:ochk * 512 + owid],
                                ps_ohb[0:ngr, 0:owid],
                                iota32_sb[0:ngr, 0:1] if mat < 2 else iota86_sb[0:ngr, 0:1],
                                None, ALU.is_equal).then_inc(vOh2, 1)
                    if chk == 0:
                        vector.wait_ge(dQw[ktn % 2], 16 * (ktn // 2 + 1))
                        if ktn >= 2:
                            vector.wait_ge(dWbf[ktn % 2], 16 * (ktn // 2))
                        vector.tensor_scalar(qw_sb[ktn % 2][0:rows, 0:nwid],
                                             qw_sb[ktn % 2][0:rows, 0:nwid],
                                             shifts_sb[0:rows, 0:1], 15,
                                             ALU.logical_shift_right, ALU.bitwise_and)
                        vector.tensor_copy(w4f_sb[0:rows, 0:nwid],
                                           qw_sb[ktn % 2][0:rows, 0:nwid])
                    vector.wait_ge(tG, u + 1)
                    csl = slice(chk * 512, chk * 512 + wid)
                    vector.tensor_sub(tdq_sb[0:rows, 0:wid],
                                      w4f_sb[0:rows, csl], pss[1][0:rows, 0:wid])
                    vector.tensor_mul(wst_sb[ktn % 2][0:rows, csl],
                                      tdq_sb[0:rows, 0:wid],
                                      pss[0][0:rows, 0:wid]).then_inc(vDq, 1)

                while nrm[0] < 32:
                    v_norm(nrm[0])
                    nrm[0] += 1

                if upto < 7:
                    return
                for tc in range(16):
                    vector.wait_ge(dHgr[tc % 2], 16 * (tc // 2 + 1))
                    for nt in range(NKT):
                        rows = 128 if nt < 10 else 96
                        it = NKT * tc + nt
                        vector.wait_ge(tH2, it + 1)
                        vector.tensor_mul(hgr_sb[tc % 2][0:rows, nt, :],
                                          ps_h[it % 4][0:rows, :],
                                          hgr_sb[tc % 2][0:rows, nt, :]).then_inc(vH2, 1)
                if upto < 8:
                    return
                i3 = 0
                for j in range(4):
                    for tc in range(16):
                        if i3 >= 2:
                            pj, pt = divmod(i3 - 2, 16)
                            vector.wait_ge(dRs8[pj][pt % 2], 16 * (pt // 2 + 1))
                        for ecc in range(8):
                            ie = 8 * i3 + ecc
                            vector.wait_ge(tY3, ie + 1)
                            vector.tensor_copy(yst_sb[i3 % 2][:, ecc, :],
                                               ps_y3[ecc % 2]).then_inc(vE3, 1)
                        i3 += 1

                if upto < 9:
                    return
                for tt in range(NTT):
                    vector.wait_ge(sF1[tt % 4][0], sF1[tt % 4][1] + 16 * (tt // 4 + 1))
                    vector.wait_ge(sYs[tt % 4][0], sYs[tt % 4][1] + 64 * (tt // 4 + 1))
                    tsl = slice(tt * 128, (tt + 1) * 128)
                    if tt >= 2:
                        vector.wait_ge(tTr, 32 * (tt - 1))
                    bm = mub_sb[:, tsl].unsqueeze(1).broadcast_to([128, 8, 128])
                    br = rstdb_sb[:, tsl].unsqueeze(1).broadcast_to([128, 8, 128])
                    for cc in range(4):
                        csl = slice(cc * 8, (cc + 1) * 8)
                        vector.tensor_sub(s2_sb[tt % 2][:, csl, :],
                                          s1f_sb[tt % 4][:, csl, :], bm).then_inc(vCh, 1)
                        vector.wait_ge(vCh, vch[0] + 1)
                        vch[0] += 1
                        vector.tensor_mul(s2_sb[tt % 2][:, csl, :],
                                          s2_sb[tt % 2][:, csl, :], br).then_inc(vCh, 1)
                        vector.wait_ge(vCh, vch[0] + 1)
                        vch[0] += 1
                        vector.tensor_add(s2_sb[tt % 2][:, csl, :],
                                          s2_sb[tt % 2][:, csl, :],
                                          ysl_sb[tt % 4][:, csl, :]).then_inc(tQ, 1)
                    for cc in range(4):
                        vector.wait_ge(sTe, 32 * tt + 8 * (cc + 1))
                        ch = tmaj_sb[tt % 2][:, cc * 1024:(cc + 1) * 1024]
                        vector.tensor_mul(t3c_sb, ch, ch)
                        vector.reduce_sum(red_sb[:, 6 + cc:7 + cc], ch,
                                          mybir.AxisListType.X)
                        vector.reduce_sum(red_sb[:, 10 + cc:11 + cc], t3c_sb,
                                          mybir.AxisListType.X)
                    vector.reduce_sum(red_sb[:, 0:1], red_sb[:, 6:10],
                                      mybir.AxisListType.X)
                    vector.reduce_sum(red_sb[:, 1:2], red_sb[:, 10:14],
                                      mybir.AxisListType.X).then_inc(vCh, 1)
                    vector.wait_ge(vCh, vch[0] + 1)
                    vch[0] += 1
                    vector.tensor_scalar(red_sb[:, 2:3], red_sb[:, 0:1],
                                         -1.0 / D, None, ALU.mult)
                    vector.tensor_scalar(red_sb[:, 4:5], red_sb[:, 1:2],
                                         1.0 / D, None, ALU.mult).then_inc(vCh, 1)
                    vector.wait_ge(vCh, vch[0] + 1)
                    vch[0] += 1
                    vector.tensor_mul(red_sb[:, 3:4], red_sb[:, 2:3],
                                      red_sb[:, 2:3]).then_inc(vCh, 1)
                    vector.wait_ge(vCh, vch[0] + 1)
                    vch[0] += 1
                    vector.tensor_sub(red_sb[:, 5:6], red_sb[:, 4:5],
                                      red_sb[:, 3:4]).then_inc(vLn, 1)
                    vector.wait_ge(sLn, tt + 1)
                    if tt >= 2:
                        vector.wait_ge(dOut[tt % 2], 256 + 16 * (tt // 2))
                    vector.tensor_scalar(out_st[tt % 2], tmaj_sb[tt % 2],
                                         red_sb[:, 2:3], red_sb[:, 3:4],
                                         ALU.add, ALU.mult).then_inc(vLn2, 1)

            @block.scalar
            def _(scalar):
                scalar.wait_ge(dPre, 16 * NPRE)
                for c in range(SC if upto >= 2 else 0):
                    scalar.wait_ge(tV, c + 1)
                    if c >= 2:
                        scalar.wait_ge(dSpV[c % 2], 64 * (c // 2))
                    scalar.copy(vst_sb[c % 2], ps_q).then_inc(sEv, 1)

                if upto < 3:
                    return
                for ci in range(NCH):
                    for g in range(32):
                        gg = 32 * ci + g
                        scalar.wait_ge(tS, gg + 1)
                        if gg >= 8:
                            scalar.wait_ge(tPv, gg - 7)
                            scalar.wait_ge(vZ, gg - 7)
                        scalar.activation(pring[:, g % 8, :], ps_s[g % 3], AF.Exp,
                                          bias=cv[:, C_EXPB:C_EXPB + 1],
                                          scale=cv[:, C_QKSC:C_QKSC + 1]).then_inc(sExp, 1)

                if upto < 4:
                    return
                scalar.wait_ge(vSt, 1)
                scalar.activation(var_sb, var2_sb, AF.Ln,
                                  bias=cv[0:1, C_EPS:C_EPS + 1]).then_inc(sCh, 1)
                scalar.wait_ge(sCh, sch[0] + 1)
                sch[0] += 1
                scalar.activation(rstd_sb, var_sb, AF.Exp,
                                  scale=cv[0:1, C_NHALF:C_NHALF + 1]).then_inc(sRstd, 1)

                if upto < 7:
                    return
                for tc in range(16):
                    for nt in range(NKT):
                        rows = 128 if nt < 10 else 96
                        it = NKT * tc + nt
                        scalar.wait_ge(tH, it + 1)
                        if tc >= 2 and nt == 0:
                            scalar.wait_ge(dHg[tc % 2], 16 * (tc // 2))
                        scalar.activation(hst_sb[tc % 2][0:rows, nt, :],
                                          ps_h[it % 4][0:rows, :],
                                          AF.Silu).then_inc(sSilu, 1)

                if upto < 8:
                    return
                i3s = 0
                for j in range(4):
                    for tc in range(16):
                        r, hf = tc // 2, tc % 2
                        scalar.wait_ge(vE3, 8 * (i3s + 1))
                        scalar.dma_start(
                            out=rs_in[j][r, :, hf * 512:(hf + 1) * 512].rearrange(
                                "(ee p) t -> p ee t", p=128),
                            in_=yst_sb[i3s % 2]).then_inc(dRs8[j][tc % 2], 16)
                        i3s += 1

                if upto < 9:
                    return
                for tt in range(NTT):
                    for ec in range(32):
                        scalar.wait_ge(tTr, 32 * tt + ec + 1)
                        scalar.copy(tmaj_sb[tt % 2][:, ec * 128:(ec + 1) * 128],
                                    ps_tr[ec % 2]).then_inc(sTe, 1)
                    scalar.wait_ge(vLn, tt + 1)
                    scalar.activation(red_sb[:, 4:5], red_sb[:, 5:6], AF.Ln,
                                      bias=cv[:, C_EPS:C_EPS + 1]).then_inc(sCh, 1)
                    scalar.wait_ge(sCh, sch[0] + 1)
                    sch[0] += 1
                    scalar.activation(red_sb[:, 3:4], red_sb[:, 4:5], AF.Exp,
                                      scale=cv[:, C_NHALF:C_NHALF + 1]).then_inc(sLn, 1)
                    scalar.wait_ge(vLn2, tt + 1)
                    scalar.dma_start(out=y_out[tt * 128:(tt + 1) * 128, :],
                                     in_=out_st[tt % 2]).then_inc(dOut[tt % 2], 16)
                scalar.wait_ge(dOut[0], 256 + 16 * NTT // 2)
                scalar.wait_ge(dOut[1], 256 + 16 * NTT // 2)

    return nc


def _prep_inputs(inputs):
    bf = ml_dtypes.bfloat16
    x = np.asarray(inputs["L0_attn_x"])
    Wq = np.asarray(inputs["L0_attn_Wq"])
    Wk = np.asarray(inputs["L0_attn_Wk"])
    Wv = np.asarray(inputs["L0_attn_Wv"])
    Wo = np.asarray(inputs["L0_attn_Wo"])

    x0 = np.ascontiguousarray(x.reshape(S, D))
    xh = np.ascontiguousarray(x0.T.reshape(32, 128, S).transpose(1, 0, 2))

    def tiled_T(Wm):
        return np.ascontiguousarray(Wm.T.reshape(32, 128, D).transpose(1, 0, 2))

    wqT, wkT, wvT, woT = map(tiled_T, (Wq, Wk, Wv, Wo))

    inv = 1.0 / (10000.0 ** (np.arange(0, HD, 2, dtype=np.float64) / HD))
    fr = inv[:, None] * np.arange(S, dtype=np.float64)[None, :]
    cos1 = np.cos(fr).astype(np.float32)
    sin1 = np.sin(fr).astype(np.float32)
    cos2 = np.ascontiguousarray(np.concatenate([cos1, cos1], axis=0))
    sin2 = np.ascontiguousarray(np.concatenate([sin1, sin1], axis=0))

    cvec = np.zeros((128, NCONST), dtype=np.float32)
    cvec[:, C_EXPB] = EXP_BIAS
    cvec[:, C_QKSC] = QK_SCALE
    cvec[:, C_EPS] = EPS
    cvec[:, C_NHALF] = -0.5

    shared = {
        "xh": xh, "cos2": cos2, "sin2": sin2,
        "ident": np.eye(128, dtype=np.float32),
        "ones_row": np.ones((1, 128), dtype=np.float32),
        "ones_col_f": np.ones((128, 1), dtype=np.float32),
        "ones_col_b": np.ones((128, 1), dtype=bf),
        "iota32": np.arange(32, dtype=np.float32).reshape(32, 1),
        "iota86": np.arange(86, dtype=np.float32).reshape(86, 1),
        "shifts": (4 * (np.arange(128) % 8)).astype(np.int32).reshape(128, 1),
        "cvec": cvec, "woT": woT,
        "g_gi": np.ascontiguousarray(np.asarray(inputs["L0_ffn_gate_g_idx"]).reshape(1, D)),
        "u_gi": np.ascontiguousarray(np.asarray(inputs["L0_ffn_up_g_idx"]).reshape(1, D)),
        "d_sc": np.ascontiguousarray(np.asarray(inputs["L0_ffn_down_scales"])),
        "d_qz": np.ascontiguousarray(np.asarray(inputs["L0_ffn_down_qzeros"])),
    }
    g_qw_full = np.asarray(inputs["L0_ffn_gate_qweight"])
    u_qw_full = np.asarray(inputs["L0_ffn_up_qweight"])
    d_qw_full = np.asarray(inputs["L0_ffn_down_qweight"])
    g_sc_full = np.asarray(inputs["L0_ffn_gate_scales"])
    u_sc_full = np.asarray(inputs["L0_ffn_up_scales"])
    g_qz_full = np.asarray(inputs["L0_ffn_gate_qzeros"])
    u_qz_full = np.asarray(inputs["L0_ffn_up_qzeros"])
    d_gi_full = np.asarray(inputs["L0_ffn_down_g_idx"])

    in_maps = []
    for c in range(W):
        esl = slice(EOWN * c, EOWN * (c + 1))
        nsl = slice(NSL * c, NSL * (c + 1))
        nslp = slice(NSLP * c, NSLP * (c + 1))
        m = dict(shared)
        m["x_own"] = np.ascontiguousarray(xh[:, :, TOK * c:TOK * (c + 1)])
        m["wqT"] = np.ascontiguousarray(wqT[:, :, esl])
        m["wkT"] = np.ascontiguousarray(wkT[:, :, esl])
        m["wvT"] = np.ascontiguousarray(wvT[:, :, esl])
        m["g_qw"] = np.ascontiguousarray(np.repeat(g_qw_full[:, nsl], 8, axis=0))
        m["u_qw"] = np.ascontiguousarray(np.repeat(u_qw_full[:, nsl], 8, axis=0))
        m["d_qw"] = np.ascontiguousarray(
            np.repeat(d_qw_full[NSLP * c:NSLP * (c + 1), :], 8, axis=0))
        m["g_sc"] = np.ascontiguousarray(g_sc_full[:, nsl])
        m["u_sc"] = np.ascontiguousarray(u_sc_full[:, nsl])
        m["g_qz"] = np.ascontiguousarray(g_qz_full[:, nslp])
        m["u_qz"] = np.ascontiguousarray(u_qz_full[:, nslp])
        m["d_gi"] = np.ascontiguousarray(d_gi_full[nsl].reshape(1, NSL))
        in_maps.append(m)
    return in_maps


def kernel(**inputs):
    in_maps = _prep_inputs(inputs)
    nc = build_nc()
    res = run_bass_kernel_spmd(nc, in_maps, core_ids=list(range(W)))
    out = np.concatenate([res.results[c]["y"] for c in range(W)], axis=0)
    return out.reshape(1, S, D).astype(np.float32)
